# version 14
# baseline (speedup 1.0000x reference)
import numpy as np
import ml_dtypes
import concourse.bass as bass
import concourse.mybir as mybir
from concourse.bass_utils import run_bass_kernel_spmd
from contextlib import ExitStack

F32 = mybir.dt.float32
BF16 = mybir.dt.bfloat16
AF = mybir.ActivationFunctionType
ALU = mybir.AluOpType
AX = mybir.AxisListType
bts = bass.ts

NCORE = 8
B, T, C = 2, 8192, 1024
TL = (B * T) // NCORE
NT = TL // 128
NCH = C // 128
SLAB = 512
NSLAB = TL // SLAB
NTS = SLAB // 128
EPS_LN = 1e-5
EPS_WKV = 1e-8
C23 = float(3 * 2**22)
C52 = float(3 * 2**51)
RC = 1.0 / float(C)
BF = ml_dtypes.bfloat16


class P:

    def __init__(self):
        self.nc = bass.Bass("TRN2", target_bir_lowering=False)
        self.stack = ExitStack()
        self.cur = self.stack
        self.sems = {}
        self.cnt = {}

    def push(self):
        self.cur = ExitStack()
        return self.cur

    def pop(self):
        self.cur.close()
        self.cur = self.stack

    def sem(self, name):
        if name not in self.sems:
            self.sems[name] = self.stack.enter_context(self.nc.semaphore(name))
            self.cnt[name] = 0
        return self.sems[name]

    def inc(self, inst, name, k=1):
        inst.then_inc(self.sem(name), k)
        self.cnt[name] += k
        return self.cnt[name]

    def wait(self, eng, name, v=None):
        eng.wait_ge(self.sem(name), self.cnt[name] if v is None else v)

    def sb(self, name, shape, dt=F32):
        return self.cur.enter_context(
            self.nc.sbuf_tensor(getattr(self, "prefix", "") + name, shape, dt))

    def ps(self, name, shape, dt=F32):
        return self.cur.enter_context(
            self.nc.psum_tensor(getattr(self, "prefix", "") + name, shape, dt))

    def din(self, name, shape, dt=F32):
        return self.nc.dram_tensor(name, shape, dt, kind="ExternalInput").ap()

    def dout(self, name, shape, dt=F32):
        return self.nc.dram_tensor(name, shape, dt, kind="ExternalOutput").ap()


def emit_barrier(p, blk):
    sems = [s for s in ("pe", "ve", "se", "po", "dl", "dg", "dp") if s in p.cnt]

    @blk.tensor
    def _(t):
        for s in sems:
            p.wait(t, s)

    @blk.vector
    def _(v):
        for s in sems:
            p.wait(v, s)

    @blk.scalar
    def _(sc):
        for s in sems:
            p.wait(sc, s)

    @blk.gpsimd
    def _(g):
        for s in sems:
            p.wait(g, s)

    @blk.sync
    def _(sy):
        for s in sems:
            p.wait(sy, s)


def emit_identity(p, blk, ident):
    @blk.gpsimd
    def _(g):
        p.inc(g.memset(ident[:], 0.0), "po")
        p.wait(g, "po")
        p.inc(g.affine_select(out=ident[:], in_=ident[:], compare_op=ALU.not_equal,
                              fill=1.0, base=0, pattern=[[-1, 128]],
                              channel_multiplier=1), "po")


def emit_ln_batch(p, blk, xs, outs, st, st2, junk, wv=()):
    n = len(xs)

    @blk.vector
    def _(v):
        for nm, val in wv:
            p.wait(v, nm, val)
        inst = None
        for i in range(n):
            inst = v.reduce_sum(st[:xs[i].shape[0], i, 0:1], xs[i], AX.X)
        p.inc(inst, "ve")

    @blk.scalar
    def _(s):
        for nm, val in wv:
            p.wait(s, nm, val)
        for i in range(n):
            p.inc(s.activation(junk[:xs[i].shape[0], 0:xs[i].shape[1]], xs[i], AF.Square,
                               accum_out=st[:xs[i].shape[0], i, 1:2]), "se")
            p.wait(s, "se")

    pr = max(x.shape[0] for x in xs)

    @blk.vector
    def _(v):
        p.wait(v, "ve")
        p.wait(v, "se")
        p.inc(v.tensor_scalar(st[:pr, 0:n, 2:3], st[:pr, 0:n, 0:1], -RC, None,
                              ALU.mult), "ve")
        p.wait(v, "ve")
        p.inc(v.tensor_tensor(st2[:pr, 0:n, 0:1], st[:pr, 0:n, 2:3],
                              st[:pr, 0:n, 2:3], ALU.mult), "ve")
        p.wait(v, "ve")
        p.inc(v.scalar_tensor_tensor(st2[:pr, 0:n, 1:2], st[:pr, 0:n, 1:2], RC,
                                     st2[:pr, 0:n, 0:1], ALU.mult, ALU.subtract),
              "ve")

    @blk.scalar
    def _(s):
        p.wait(s, "ve")
        p.inc(s.activation(st2[:pr, 0:n, 2:3], st2[:pr, 0:n, 1:2], AF.Ln,
                           bias=np.float32(EPS_LN)), "se")
        p.wait(s, "se")
        p.inc(s.activation(st[:pr, 0:n, 3:4], st2[:pr, 0:n, 2:3], AF.Exp,
                           scale=-0.5), "se")

    @blk.vector
    def _(v):
        p.wait(v, "se")
        inst = None
        for i in range(n):
            inst = v.tensor_scalar(outs[i], xs[i], st[:xs[i].shape[0], i, 2:3],
                                   st[:xs[i].shape[0], i, 3:4], ALU.add, ALU.mult)
        p.inc(inst, "ve")


def build_launch1():
    p = P()
    xp = p.din("xp", [TL + 1, C])
    wts = {n: p.din("w" + n, [C, C]) for n in "kvr"}
    prow = p.din("prow", [5, C])
    sflag = p.din("sflag", [1, 1])
    o_k = p.dout("o_k", [128, NCH * TL])
    o_v = p.dout("o_v", [128, NCH * TL])
    o_r = p.dout("o_r", [128, NCH * TL], BF16)
    o_ta = p.dout("o_ta", [128, NCH * 2])

    identb = p.sb("identb", [128, 128], BF16)
    identf = p.sb("identf", [128, 128], F32)
    wstage = p.sb("wstage", [128, C])
    wbf = p.sb("wbf", [128, C], BF16)
    wT = {n: p.sb("wT_" + n, [128, NCH, C], BF16) for n in "kvr"}
    cks = p.sb("cks", [128, NCH, 3])
    pcol = p.sb("pcol", [128, NCH, 8])
    pstage = p.sb("pstage", [128, NCH * 128])
    b1cb = p.sb("b1cb", [128, NCH], BF16)
    halo2 = p.sb("halo2", [1, C])
    zn_h = p.sb("zn_h", [1, C])
    gr = p.sb("gr", [1, C])
    nbg = p.sb("nbg", [1, C])
    g1row = p.sb("g1row", [1, C])
    b1row = p.sb("b1row", [1, C])
    sfc = p.sb("sfc", [128, 2])
    ones1 = p.sb("ones1", [1, 128])
    znf = p.sb("znf", [128, NCH, 1 + SLAB], BF16)
    colt = p.sb("colt", [128, NCH])
    x_tm = p.sb("x_tm", [128, NTS, C])
    zn_tm = p.sb("zn_tm", [128, NTS, C], BF16)
    st = p.sb("st", [128, NTS, 4])
    st2 = p.sb("st2", [128, NTS, 4])
    junk = p.sb("junk", [128, C])
    dts = p.sb("dts", [128, NCH, SLAB], BF16)
    tsv = {n: p.sb("tsv_" + n, [128, NCH, SLAB], BF16) for n in "kvr"}
    kout = p.sb("kout", [128, NCH, SLAB])
    vout = p.sb("vout", [128, NCH, SLAB])
    rout = p.sb("rout", [128, NCH, SLAB], BF16)
    sksum = p.sb("sksum", [128, NCH])
    carry = p.sb("carry", [128, NCH, 1])
    ta = p.sb("ta", [128, NCH, 2])

    pbank = [p.ps(f"pb{i}", [128, 512]) for i in range(4)]
    ptb = p.ps("ptb", [128, 1024], BF16)
    ptf = p.ps("ptf", [128, 512], F32)

    blk = p.stack.enter_context(p.nc.Block())
    emit_identity(p, blk, identb)
    emit_identity(p, blk, identf)

    @blk.sync
    def _(sy):
        p.inc(sy.dma_start(out=zn_h[:], in_=xp[0:1, :]), "dl", 16)
        p.inc(sy.dma_start(out=sfc[0:1, 0:1], in_=sflag), "dl", 16)
        p.inc(sy.dma_start(out=g1row[:], in_=prow[0:1, :]), "dl", 16)
        p.inc(sy.dma_start(out=b1row[:], in_=prow[1:2, :]), "dl", 16)
        p.wait(sy, "dl")

    @blk.gpsimd
    def _(g):
        p.inc(g.memset(pstage[:, :], 0.0), "po")
        p.inc(g.memset(ones1[:], 1.0), "po")

    @blk.sync
    def _(sy):
        p.wait(sy, "po")
        p.inc(sy.dma_start(out=pstage[0:5, :], in_=prow[:, :]), "dl", 16)
        p.wait(sy, "dl")
    ld_par = p.cnt["dl"]

    @blk.tensor
    def _(t):
        p.wait(t, "dl", ld_par)
        p.wait(t, "po")
        p.inc(t.matmul(ptf[:, 0:1], ones1[:], sfc[0:1, 0:1], start=True, stop=True),
              "pe")

    @blk.vector
    def _(v):
        p.wait(v, "pe")
        p.inc(v.tensor_copy(sfc[:, 0:1], ptf[:, 0:1]), "ve")
        p.wait(v, "ve")
        p.inc(v.tensor_scalar(sfc[:, 1:2], sfc[:, 0:1], -1.0, 1.0, ALU.mult, ALU.add),
              "ve")

    @blk.vector
    def _(v):
        p.wait(v, "dl", ld_par)
        p.inc(v.reciprocal(gr[:], g1row[:]), "ve")
        p.wait(v, "ve")
        p.inc(v.scalar_tensor_tensor(nbg[:], b1row[:], -1.0, gr[:],
                                     ALU.mult, ALU.mult), "ve")
        p.wait(v, "ve")
        p.wait(v, "po")
        p.inc(v.tensor_copy(pstage[32:33, :], nbg[:]), "ve")

    emit_ln_batch(p, blk, [zn_h[:, :]], [halo2[:, :]], st, st2, junk,
                  wv=[("dl", ld_par)])
    emit_ln_batch(p, blk, [halo2[:, :]], [zn_h[:, :]], st, st2, junk,
                  wv=[("ve", p.cnt["ve"])])

    @blk.vector
    def _(v):
        p.wait(v, "ve")
        p.wait(v, "po")
        p.inc(v.tensor_copy(pstage[64:65, :], zn_h[:, :]), "ve")
    pstage_ready = p.cnt["ve"]

    ev_ptf = [p.cnt["ve"], p.cnt["ve"]]
    for dc in range(NCH):
        @blk.tensor
        def _(t, dc=dc, evf=None):
            p.wait(t, "ve", pstage_ready)
            p.wait(t, "ve", ev_ptf[0])
            p.inc(t.transpose(ptf[:, 0:128], pstage[:, bts(dc, 128)],
                              identf[:]), "pe")
        tpd = p.cnt["pe"]

        @blk.vector
        def _(v, dc=dc, tpd=tpd):
            p.wait(v, "pe", tpd)
            v.tensor_copy(pcol[:, dc, 0:5], ptf[:, 0:5])
            v.tensor_copy(pcol[:, dc, 5:6], ptf[:, 32:33])
            ev_ptf[0] = p.inc(
                v.tensor_copy(pcol[:, dc, 6:7], ptf[:, 64:65]), "ve")
    ev_ptf[1] = ev_ptf[0]

    @blk.vector
    def _(v):
        p.wait(v, "ve")
        inst = None
        for dc in range(NCH):
            inst = v.tensor_tensor(colt[:, dc:dc + 1], pcol[:, dc, 6:7],
                                   pcol[:, dc, 5:6], ALU.subtract)
        p.inc(inst, "ve")
        p.wait(v, "ve")
        for dc in range(NCH):
            inst = v.scalar_tensor_tensor(znf[:, dc, 0:1], colt[:, dc:dc + 1],
                                          sfc[:, 1:2], pcol[:, dc, 5:6],
                                          ALU.mult, ALU.add)
        p.inc(inst, "ve")
        p.wait(v, "ve")
        p.inc(v.tensor_copy(b1cb[:, :], pcol[:, :, 1]), "ve")
    b1_ready = p.cnt["ve"]

    cast_done = 0
    tp_done = 0
    ev_ptb = p.cnt["ve"]
    for wi, n in enumerate("kvr"):
        for oc in range(NCH):
            @blk.sync
            def _(sy, n=n, oc=oc, cd=cast_done):
                p.wait(sy, "ve", cd)
                p.inc(sy.dma_start(out=wstage[:, :], in_=wts[n][bts(oc, 128), :]),
                      "dl", 16)
                p.wait(sy, "dl")
            ldw = p.cnt["dl"]

            @blk.vector
            def _(v, ldw=ldw, tpd=tp_done):
                p.wait(v, "dl", ldw)
                p.wait(v, "pe", tpd)
                p.inc(v.tensor_copy(wbf[:, :], wstage[:, :]), "ve")
            cast_done = p.cnt["ve"]

            @blk.tensor
            def _(t, cd=cast_done, evp=ev_ptb):
                p.wait(t, "ve", cd)
                p.wait(t, "ve", evp)
                inst = None
                for dc in range(NCH):
                    inst = t.transpose(ptb[:, bts(dc, 128)], wbf[:, bts(dc, 128)],
                                       identb[:])
                p.inc(inst, "pe")
            tp_done = p.cnt["pe"]

            @blk.vector
            def _(v, n=n, oc=oc, tpd=tp_done):
                p.wait(v, "pe", tpd)
                p.inc(v.tensor_copy(wT[n][:, :, bts(oc, 128)],
                                    ptb[:, :].rearrange("p (c j) -> p c j", c=NCH)),
                      "ve")
            ev_ptb = p.cnt["ve"]
    wT_ready = p.cnt["ve"]

    @blk.tensor
    def _(t):
        p.wait(t, "ve", max(wT_ready, b1_ready))
        p.wait(t, "ve", max(ev_ptf))
        inst = None
        for wi, n in enumerate("kvr"):
            for oc in range(NCH):
                for dc in range(NCH):
                    inst = t.matmul(ptf[:, oc * 3 + wi: oc * 3 + wi + 1],
                                    wT[n][:, dc, bts(oc, 128)], b1cb[:, dc:dc + 1],
                                    start=(dc == 0), stop=(dc == NCH - 1))
        p.inc(inst, "pe")
    cvec_done = p.cnt["pe"]

    @blk.vector
    def _(v):
        p.wait(v, "pe", cvec_done)
        p.inc(v.tensor_copy(cks[:, :, :],
                            ptf[:, 0:24].rearrange("p (o j) -> p o j", j=3)), "ve")
        p.wait(v, "ve")
        inst = None
        for n in "kvr":
            for dc in range(NCH):
                inst = v.tensor_scalar(wT[n][:, dc, :], wT[n][:, dc, :],
                                       pcol[:, dc, 0:1], None, ALU.mult)
        p.inc(inst, "ve")
        inst = v.memset(carry[:, :, :], 0.0)
        p.inc(inst, "ve")
    prep_done = p.cnt["ve"]
    cks_ready = prep_done
    carry_ok = prep_done

    bank_ev = [p.cnt["se"]] * 4
    slab_ln_ve = [0] * NSLAB
    slab_ln_se = [0] * NSLAB
    slab_tp = [0] * NSLAB
    slab_mm = [0] * NSLAB
    slab_ev = [0] * NSLAB
    slab_dma = [0] * NSLAB
    slab_scan = [0] * NSLAB
    outbuf = {"k": kout, "v": vout, "r": rout}
    func = {"k": AF.Exp, "v": AF.Identity, "r": AF.Sigmoid}
    ckj = {"k": 0, "v": 1, "r": 2}

    for s in range(NSLAB):
        @blk.sync
        def _(sy, s=s):
            if s > 0:
                p.wait(sy, "ve", slab_ln_ve[s - 1])
                p.wait(sy, "se", slab_ln_se[s - 1])
            src = xp[1 + s * SLAB: 1 + (s + 1) * SLAB, :].rearrange(
                "(a p) c -> p a c", p=128)
            p.inc(sy.dma_start(out=x_tm[:, :, :], in_=src), "dl", 16)
            p.wait(sy, "dl")
        ld_x = p.cnt["dl"]

        @blk.vector
        def _(v, s=s):
            p.wait(v, "dl", ld_x)
            if s > 0:
                p.wait(v, "pe", slab_tp[s - 1])

        @blk.scalar
        def _(sc, s=s):
            p.wait(sc, "dl", ld_x)

        xs = [x_tm[:, i, :] for i in range(NTS)]
        zns = [zn_tm[:, i, :] for i in range(NTS)]
        emit_ln_batch(p, blk, xs, xs, st, st2, junk, wv=[("dl", ld_x)])
        emit_ln_batch(p, blk, xs, zns, st, st2, junk,
                      wv=[("ve", p.cnt["ve"]), ("se", p.cnt["se"])])
        slab_ln_ve[s] = p.cnt["ve"]
        slab_ln_se[s] = p.cnt["se"]
        zn_ready = p.cnt["ve"]

        if s > 0:
            @blk.vector
            def _(v, s=s):
                p.inc(v.tensor_copy(znf[:, :, 0:1], znf[:, :, SLAB:SLAB + 1]), "ve")
                p.wait(v, "ve")

        for i in range(NTS):
            @blk.tensor
            def _(t, s=s, i=i, znr=zn_ready, evp=ev_ptb):
                p.wait(t, "ve", znr)
                p.wait(t, "ve", evp)
                inst = None
                for dc in range(NCH):
                    inst = t.transpose(ptb[:, bts(dc, 128)],
                                       zn_tm[:, i, bts(dc, 128)], identb[:])
                p.inc(inst, "pe")
            tpd = p.cnt["pe"]

            @blk.vector
            def _(v, s=s, i=i, tpd=tpd):
                p.wait(v, "pe", tpd)
                dst = znf[:, :, 1 + i * 128: 1 + (i + 1) * 128]
                p.inc(v.tensor_copy(dst,
                                    ptb[:, :].rearrange("p (c j) -> p c j", c=NCH)),
                      "ve")
            ev_ptb = p.cnt["ve"]
        slab_tp[s] = p.cnt["pe"]
        znf_ready = p.cnt["ve"]

        @blk.vector
        def _(v, s=s):
            p.wait(v, "ve", znf_ready)
            if s > 0:
                p.wait(v, "pe", slab_mm[s - 1])
            cur = slice(1, 1 + SLAB)
            prv = slice(0, SLAB)
            inst = None
            for dc in range(NCH):
                inst = v.tensor_tensor(dts[:, dc, :], znf[:, dc, cur],
                                       znf[:, dc, prv], ALU.subtract)
            p.inc(inst, "ve")
            p.wait(v, "ve")
            for wi, n in enumerate("kvr"):
                for dc in range(NCH):
                    inst = v.scalar_tensor_tensor(tsv[n][:, dc, :], dts[:, dc, :],
                                                  pcol[:, dc, 2 + wi: 3 + wi],
                                                  znf[:, dc, prv], ALU.mult, ALU.add)
            p.inc(inst, "ve")
        ts_ready = p.cnt["ve"]

        gi = 0
        for wi, n in enumerate("kvr"):
            for oc in range(NCH):
                bk = gi % 4

                @blk.tensor
                def _(t, n=n, oc=oc, bk=bk, s=s, gi=gi, tsr=ts_ready):
                    p.wait(t, "ve", tsr)
                    p.wait(t, "se", bank_ev[bk])
                    if gi == 0 and s > 0:
                        p.wait(t, "dg", slab_dma[s - 1])
                        p.wait(t, "ve", slab_scan[s - 1])
                    inst = None
                    for dc in range(NCH):
                        inst = t.matmul(pbank[bk][:, :], wT[n][:, dc, bts(oc, 128)],
                                        tsv[n][:, dc, :], start=(dc == 0),
                                        stop=(dc == NCH - 1))
                    p.inc(inst, "pe")
                mmd = p.cnt["pe"]
                gi += 1

                @blk.scalar
                def _(sc, n=n, oc=oc, bk=bk, mmd=mmd):
                    p.wait(sc, "pe", mmd)
                    p.wait(sc, "ve", cks_ready)
                    j = ckj[n]
                    bank_ev[bk] = p.inc(
                        sc.activation(outbuf[n][:, oc, :], pbank[bk][:, :], func[n],
                                      bias=cks[:, oc, j:j + 1]), "se")
        slab_mm[s] = p.cnt["pe"]
        slab_ev[s] = p.cnt["se"]

        @blk.vector
        def _(v, s=s):
            p.wait(v, "se", slab_ev[s])
            p.wait(v, "ve", carry_ok)
            inst = None
            for dc in range(NCH):
                inst = v.reduce_sum(sksum[:, dc:dc + 1], kout[:, dc, :], AX.X)
            p.inc(inst, "ve")
            p.wait(v, "ve")
            p.inc(v.tensor_tensor(carry[:, :, 0], carry[:, :, 0], sksum[:, :],
                                  ALU.add), "ve")
            p.wait(v, "ve")
            if s == NSLAB - 1:
                v.tensor_copy(ta[:, :, 1], carry[:, :, 0])
                p.inc(v.tensor_tensor(ta[:, :, 0], carry[:, :, 0],
                                      kout[:, :, SLAB - 1], ALU.subtract), "ve")
                p.wait(v, "ve")
        carry_ok = p.cnt["ve"]
        slab_scan[s] = p.cnt["ve"]

        @blk.gpsimd
        def _(g, s=s):
            p.wait(g, "se", slab_ev[s])
            cur = slice(s * SLAB, (s + 1) * SLAB)
            p.inc(g.dma_start(out=o_k.rearrange("p (c t) -> p c t", c=NCH)[:, :, cur],
                              in_=kout[:, :, :]), "dg", 16)
            p.inc(g.dma_start(out=o_v.rearrange("p (c t) -> p c t", c=NCH)[:, :, cur],
                              in_=vout[:, :, :]), "dg", 16)
            p.inc(g.dma_start(out=o_r.rearrange("p (c t) -> p c t", c=NCH)[:, :, cur],
                              in_=rout[:, :, :]), "dg", 16)
        slab_dma[s] = p.cnt["dg"]

    @blk.gpsimd
    def _(g):
        p.wait(g, "ve", slab_scan[NSLAB - 1])
        p.inc(g.dma_start(out=o_ta, in_=ta[:, :, :].rearrange("p c t -> p (c t)")),
              "dg", 16)
        p.wait(g, "dg")

    return p


def _seq_start(c):
    return (c % (NCORE // B)) == 0


def _shard_rows(x_rows, halo_rows=1):
    out = []
    for c in range(NCORE):
        s = c * TL
        lead = []
        for j in range(halo_rows, 0, -1):
            i = s if _seq_start(c) else s - j
            lead.append(x_rows[i:i + 1])
        out.append(np.ascontiguousarray(
            np.concatenate(lead + [x_rows[s:s + TL]], axis=0)))
    return out


def run_l1(inputs, x_rows):
    p = build_launch1()
    shards = _shard_rows(x_rows, 1)
    prow = np.ascontiguousarray(np.stack(
        [inputs["L0_attn_ln1_g"], inputs["L0_attn_ln1_b"], inputs["L0_attn_mix_k"],
         inputs["L0_attn_mix_v"], inputs["L0_attn_mix_r"]]).astype(np.float32))
    maps = []
    for c in range(NCORE):
        sf = np.array([[1.0 if _seq_start(c) else 0.0]], np.float32)
        maps.append(dict(
            xp=shards[c], wk=np.ascontiguousarray(inputs["L0_attn_Wk_t"]),
            wv=np.ascontiguousarray(inputs["L0_attn_Wv_t"]),
            wr=np.ascontiguousarray(inputs["L0_attn_Wr_t"]), prow=prow, sflag=sf))
    res = run_bass_kernel_spmd(p.nc, maps, core_ids=list(range(NCORE)))
    return res.results




def build_launch2():
    p = P()
    kc = p.din("kc", [128, NCH * TL])
    ta_all = p.din("ta_all", [NCORE * 128, NCH * 2])
    pmask = p.din("pmask", [1, NCORE])
    wgs = p.din("wgs", [4 * 128, C])
    wus = p.din("wus", [4 * 128, C])
    wds = p.din("wds", [128, 4 * C])
    o_tb = p.dout("o_tb", [128, NCH * 2])
    o_wsums = p.dout("o_wsums", [1, 4])

    tstack = p.sb("tstack", [128, NCORE, NCH, 2])
    pmr = p.sb("pmr", [1, NCORE])
    pmc = p.sb("pmc", [128, NCORE])
    ones1 = p.sb("ones1", [1, 128])
    macc = p.sb("macc", [128, NCH, 2])
    kslab = p.sb("kslab", [128, NCH, SLAB])
    m0 = p.sb("m0", [128, NCH, SLAB + 1])
    em = p.sb("em", [128, NCH, SLAB])
    esc = p.sb("esc", [128, NCH, SLAB + 1])
    mcarry = p.sb("mcarry", [128, NCH])
    ecarry = p.sb("ecarry", [128, NCH])
    tb = p.sb("tb", [128, NCH, 2])
    wqs = [p.sb("wqs0", [128, 4 * C]), p.sb("wqs1", [128, 4 * C])]
    wjunk = p.sb("wjunk", [128, 4 * C])
    wcs = p.sb("wcs", [128, 12])
    w3 = p.sb("w3", [128, 4])
    onescol = p.sb("onescol", [128, 1])
    wsrow = p.sb("wsrow", [1, 4])
    ptf = p.ps("l2ptf", [128, 512])
    ptw = p.ps("l2ptw", [128, 512])

    blk = p.stack.enter_context(p.nc.Block())

    @blk.sync
    def _(sy):
        p.inc(sy.dma_start(out=tstack[:, :, :, :],
                           in_=ta_all.rearrange("(j p) c -> p j c", p=128)
                           .rearrange("p j (c t) -> p j c t", t=2)), "dl", 16)
        p.inc(sy.dma_start(out=pmr[:], in_=pmask), "dl", 16)
        p.wait(sy, "dl")
    ld0 = p.cnt["dl"]

    @blk.gpsimd
    def _(g):
        p.inc(g.memset(ones1[:], 1.0), "po")
        p.inc(g.memset(onescol[:], 1.0), "po")

    wchunks = ([(wgs[bts(j, 128), :], C) for j in range(4)]
               + [(wus[bts(j, 128), :], C) for j in range(4)]
               + [(wds, 4 * C)])
    buf_free = [0, 0]
    for k, (src, wid) in enumerate(wchunks):
        buf = wqs[k % 2]

        @blk.gpsimd
        def _(g, src=src, wid=wid, buf=buf, bf=buf_free[k % 2]):
            p.wait(g, "se", bf)
            p.inc(g.dma_start(out=buf[:, 0:wid], in_=src), "dg", 16)
            p.wait(g, "dg")
        ldw = p.cnt["dg"]

        @blk.scalar
        def _(sc, k=k, wid=wid, buf=buf, ldw=ldw, pse=p.cnt.get("se", 0)):
            p.wait(sc, "dg", ldw)
            p.wait(sc, "se", pse)
            p.inc(sc.activation(wjunk[:, 0:wid], buf[:, 0:wid], AF.Abs,
                                accum_out=wcs[:, k:k + 1]), "se")
        buf_free[k % 2] = p.cnt["se"]
    wabs_done = p.cnt["se"]

    @blk.vector
    def _(v, ad=wabs_done):
        p.wait(v, "se", ad)
        p.inc(v.memset(w3[:, :], 0.0), "ve")
        p.wait(v, "ve")
        p.inc(v.reduce_sum(w3[:, 0:1], wcs[:, 0:4], AX.X), "ve")
        p.inc(v.reduce_sum(w3[:, 1:2], wcs[:, 4:8], AX.X), "ve")
        p.inc(v.tensor_copy(w3[:, 2:3], wcs[:, 8:9]), "ve")
        p.wait(v, "ve")
    w3_done = p.cnt["ve"]

    @blk.tensor
    def _(t, wd=w3_done):
        p.wait(t, "ve", wd)
        p.wait(t, "po")
        p.wait(t, "pe")
        p.inc(t.matmul(ptw[0:1, 8:12], onescol[:, 0:1], w3[:, :],
                       start=True, stop=True), "pe")
    wsum_pe = p.cnt["pe"]

    @blk.vector
    def _(v, wp=wsum_pe):
        p.wait(v, "pe", wp)
        p.inc(v.tensor_copy(wsrow[0:1, :], ptw[0:1, 8:12]), "ve")
        p.wait(v, "ve")
    wsrow_done = p.cnt["ve"]

    @blk.gpsimd
    def _(g, wd=wsrow_done):
        p.wait(g, "ve", wd)
        p.inc(g.dma_start(out=o_wsums, in_=wsrow[0:1, :]), "dg", 16)
        p.wait(g, "dg")

    @blk.tensor
    def _(t):
        p.wait(t, "dl", ld0)
        p.wait(t, "po")
        p.inc(t.matmul(ptf[:, 0:NCORE], ones1[:], pmr[:], start=True, stop=True), "pe")

    @blk.vector
    def _(v):
        p.wait(v, "pe")
        p.inc(v.tensor_copy(pmc[:, :], ptf[:, 0:NCORE]), "ve")
        p.wait(v, "ve")
        p.inc(v.memset(macc[:, :, 0], 0.0), "ve")
        p.wait(v, "ve")
        for j in range(NCORE):
            a, b = j % 2, 1 - j % 2
            p.inc(v.scalar_tensor_tensor(macc[:, :, b], tstack[:, j, :, 1],
                                         pmc[:, j:j + 1], macc[:, :, a],
                                         ALU.mult, ALU.add), "ve")
            p.wait(v, "ve")
        p.inc(v.tensor_copy(mcarry[:, :], macc[:, :, NCORE % 2]), "ve")
        inst = v.memset(ecarry[:, :], 0.0)
        p.inc(inst, "ve")
        p.wait(v, "ve")

    for s in range(NSLAB):
        @blk.sync
        def _(sy, s=s):
            if s > 0:
                p.wait(sy, "ve")
            p.inc(sy.dma_start(
                out=kslab[:, :, :],
                in_=kc.rearrange("p (c t) -> p c t", c=NCH)[:, :, bts(s, SLAB)]),
                "dl", 16)
            p.wait(sy, "dl")
        ldk = p.cnt["dl"]

        @blk.vector
        def _(v, s=s, ldk=ldk):
            p.wait(v, "dl", ldk)
            p.inc(v.tensor_copy(m0[:, :, 0:1], mcarry[:, :].rearrange("p (c o) -> p c o", o=1)),
                  "ve")
            p.wait(v, "ve")
            inst = None
            for dc in range(NCH):
                inst = v.tensor_tensor_scan(m0[:, dc, 1:SLAB + 1], kslab[:, dc, :],
                                            kslab[:, dc, :], mcarry[:, dc:dc + 1],
                                            ALU.add, ALU.bypass)
            p.inc(inst, "ve")
            p.wait(v, "ve")
            p.inc(v.tensor_copy(mcarry[:, :], m0[:, :, SLAB]), "ve")
            inst = None
            for dc in range(NCH):
                inst = v.tensor_tensor(em[:, dc, :], m0[:, dc, 0:SLAB],
                                       kslab[:, dc, :], ALU.max)
            p.inc(inst, "ve")
            p.wait(v, "ve")
            inst = None
            for dc in range(NCH):
                inst = v.tensor_tensor_scan(esc[:, dc, 1:SLAB + 1], em[:, dc, :],
                                            em[:, dc, :], ecarry[:, dc:dc + 1],
                                            ALU.add, ALU.bypass)
            p.inc(inst, "ve")
            p.wait(v, "ve")
            p.inc(v.tensor_copy(ecarry[:, :], esc[:, :, SLAB]), "ve")
            if s == NSLAB - 1:
                v.tensor_copy(tb[:, :, 0:1], esc[:, :, SLAB - 1:SLAB])
                p.inc(v.tensor_copy(tb[:, :, 1:2], esc[:, :, SLAB:SLAB + 1]), "ve")
            p.wait(v, "ve")

    @blk.gpsimd
    def _(g):
        p.wait(g, "ve")
        p.inc(g.dma_start(out=o_tb, in_=tb[:, :, :].rearrange("p c t -> p (c t)")),
              "dg", 16)
        p.wait(g, "dg")

    return p


def build_launch3():
    p = P()
    kc = p.din("kc", [128, NCH * TL])
    vc = p.din("vc", [128, NCH * TL])
    ta_all = p.din("ta_all", [NCORE * 128, NCH * 2])
    tb_all = p.din("tb_all", [NCORE * 128, NCH * 2])
    pmask = p.din("pmask", [1, NCORE])
    sflag = p.din("sflag", [1, 1])
    o_m = p.dout("o_m", [128, NCH * TL])
    o_ll = p.dout("o_ll", [128, NCH * TL])
    o_gl = p.dout("o_gl", [128, NCH * TL])
    o_td = p.dout("o_td", [128, NCH * 4])

    tstack = p.sb("tstack", [128, NCORE, NCH, 2])
    tstack2 = p.sb("tstack2", [128, NCORE, NCH, 2])
    pmr = p.sb("pmr", [1, NCORE])
    sfr = p.sb("sfr", [1, 1])
    pmc = p.sb("pmc", [128, NCORE])
    sfc = p.sb("sfc", [128, 2])
    ones1 = p.sb("ones1", [1, 128])
    macc = p.sb("macc", [128, NCH, 2])
    skin = p.sb("skin", [128, NCH])
    epse = p.sb("epse", [128, NCH])
    kslab = p.sb("kslab", [128, NCH, SLAB])
    vslab = p.sb("vslab", [128, NCH, SLAB])
    m0 = p.sb("m0", [128, NCH, SLAB + 1])
    em = p.sb("em", [128, NCH, SLAB])
    esc = p.sb("esc", [128, NCH, SLAB + 1])
    mg = p.sb("mg", [128, NCH, SLAB + 1])
    msafe = p.sb("msafe", [128, NCH, SLAB])
    rm = p.sb("rm", [128, NCH, SLAB])
    ratio = p.sb("ratio", [128, NCH, SLAB])
    lsl = p.sb("lsl", [128, NCH, SLAB])
    lls = p.sb("lls", [128, NCH, SLAB + 1])
    el = p.sb("el", [128, NCH, SLAB])
    gsl = em
    gls = m0
    mcarry = p.sb("mcarry", [128, NCH])
    ecarry = p.sb("ecarry", [128, NCH])
    lcarry = p.sb("lcarry", [128, NCH])
    gcarry = p.sb("gcarry", [128, NCH])
    td = p.sb("td", [128, NCH, 4])
    tmpc = p.sb("tmpc", [128, NCH])
    ptf = p.ps("l3ptf", [128, 512])

    blk = p.stack.enter_context(p.nc.Block())

    @blk.sync
    def _(sy):
        p.inc(sy.dma_start(out=tstack[:, :, :, :],
                           in_=ta_all.rearrange("(j p) c -> p j c", p=128)
                           .rearrange("p j (c t) -> p j c t", t=2)), "dl", 16)
        p.inc(sy.dma_start(out=tstack2[:, :, :, :],
                           in_=tb_all.rearrange("(j p) c -> p j c", p=128)
                           .rearrange("p j (c t) -> p j c t", t=2)), "dl", 16)
        p.inc(sy.dma_start(out=pmr[:], in_=pmask), "dl", 16)
        p.inc(sy.dma_start(out=sfr[:], in_=sflag), "dl", 16)
        p.wait(sy, "dl")
    ld0 = p.cnt["dl"]

    @blk.gpsimd
    def _(g):
        p.inc(g.memset(ones1[:], 1.0), "po")

    @blk.tensor
    def _(t):
        p.wait(t, "dl", ld0)
        p.wait(t, "po")
        t.matmul(ptf[:, 0:NCORE], ones1[:], pmr[:], start=True, stop=True)
        p.inc(t.matmul(ptf[:, 16:17], ones1[:], sfr[:], start=True, stop=True), "pe")

    @blk.vector
    def _(v):
        p.wait(v, "pe")
        v.tensor_copy(pmc[:, :], ptf[:, 0:NCORE])
        p.inc(v.tensor_copy(sfc[:, 0:1], ptf[:, 16:17]), "ve")
        p.wait(v, "ve")
        p.inc(v.tensor_scalar(sfc[:, 1:2], sfc[:, 0:1], -1.0, 1.0, ALU.mult, ALU.add),
              "ve")
        p.wait(v, "ve")
        for (dst, stk) in ((skin, tstack), (epse, tstack2)):
            p.inc(v.memset(macc[:, :, 0], 0.0), "ve")
            p.wait(v, "ve")
            for j in range(NCORE):
                a, b = j % 2, 1 - j % 2
                p.inc(v.scalar_tensor_tensor(macc[:, :, b], stk[:, j, :, 1],
                                             pmc[:, j:j + 1], macc[:, :, a],
                                             ALU.mult, ALU.add), "ve")
                p.wait(v, "ve")
            p.inc(v.tensor_copy(dst[:, :], macc[:, :, NCORE % 2]), "ve")
            p.wait(v, "ve")
        p.inc(v.tensor_scalar(epse[:, :], epse[:, :], EPS_WKV, None, ALU.mult), "ve")
        p.wait(v, "ve")
        v.tensor_copy(mcarry[:, :], skin[:, :])
        v.memset(ecarry[:, :], 0.0)
        v.memset(lcarry[:, :], 0.0)
        p.inc(v.memset(gcarry[:, :], 0.0), "ve")
        p.wait(v, "ve")

    for s in range(NSLAB):
        @blk.sync
        def _(sy, s=s):
            if s > 0:
                p.wait(sy, "ve")
                p.wait(sy, "dg")
            p.inc(sy.dma_start(
                out=kslab[:, :, :],
                in_=kc.rearrange("p (c t) -> p c t", c=NCH)[:, :, bts(s, SLAB)]),
                "dl", 16)
            p.inc(sy.dma_start(
                out=vslab[:, :, :],
                in_=vc.rearrange("p (c t) -> p c t", c=NCH)[:, :, bts(s, SLAB)]),
                "dl", 16)
            p.wait(sy, "dl")
        ldk = p.cnt["dl"]

        @blk.vector
        def _(v, s=s, ldk=ldk):
            p.wait(v, "dl", ldk)
            p.inc(v.tensor_copy(m0[:, :, 0:1], mcarry[:, :].rearrange("p (c o) -> p c o", o=1)),
                  "ve")
            p.wait(v, "ve")
            inst = None
            for dc in range(NCH):
                inst = v.tensor_tensor_scan(m0[:, dc, 1:SLAB + 1], kslab[:, dc, :],
                                            kslab[:, dc, :], mcarry[:, dc:dc + 1],
                                            ALU.add, ALU.bypass)
            p.inc(inst, "ve")
            p.wait(v, "ve")
            v.tensor_copy(mcarry[:, :], m0[:, :, SLAB])
            inst = None
            for dc in range(NCH):
                inst = v.tensor_tensor(em[:, dc, :], m0[:, dc, 0:SLAB],
                                       kslab[:, dc, :], ALU.max)
            p.inc(inst, "ve")
            p.wait(v, "ve")
            p.inc(v.tensor_copy(esc[:, :, 0:1], ecarry[:, :].rearrange("p (c o) -> p c o", o=1)),
                  "ve")
            p.wait(v, "ve")
            inst = None
            for dc in range(NCH):
                inst = v.tensor_tensor_scan(esc[:, dc, 1:SLAB + 1], em[:, dc, :],
                                            em[:, dc, :], ecarry[:, dc:dc + 1],
                                            ALU.add, ALU.bypass)
            p.inc(inst, "ve")
            p.wait(v, "ve")
            p.inc(v.tensor_copy(ecarry[:, :], esc[:, :, SLAB]), "ve")
            p.wait(v, "ve")
            inst = None
            for dc in range(NCH):
                inst = v.scalar_tensor_tensor(mg[:, dc, :], esc[:, dc, :], EPS_WKV,
                                              m0[:, dc, :], ALU.mult, ALU.add)
            p.inc(inst, "ve")
            p.wait(v, "ve")
            inst = None
            for dc in range(NCH):
                inst = v.tensor_scalar(mg[:, dc, :], mg[:, dc, :],
                                       epse[:, dc:dc + 1], None, ALU.add)
            p.inc(inst, "ve")
            p.wait(v, "ve")
            inst = None
            for dc in range(NCH):
                inst = v.tensor_scalar(msafe[:, dc, :], mg[:, dc, 0:SLAB], 1e-30,
                                       None, ALU.max)
            p.inc(inst, "ve")

        @blk.vector
        def _(v, s=s):
            p.wait(v, "ve")
            p.inc(v.reciprocal(rm[:, :, :], msafe[:, :, :]), "ve")
            p.wait(v, "ve")
            inst = None
            for dc in range(NCH):
                inst = v.tensor_tensor(ratio[:, dc, :], kslab[:, dc, :],
                                       rm[:, dc, :], ALU.mult)
            p.inc(inst, "ve")
            p.wait(v, "ve")
            inst = None
            for dc in range(NCH):
                inst = v.tensor_scalar(msafe[:, dc, :], ratio[:, dc, :], 1.0,
                                       None, ALU.max)
            p.inc(inst, "ve")

        @blk.scalar
        def _(sc, s=s):
            p.wait(sc, "ve")
            p.inc(sc.activation(lsl[:, :, :], msafe[:, :, :], AF.Ln), "se")

        @blk.vector
        def _(v, s=s):
            p.wait(v, "se")
            if s == 0:
                p.inc(v.tensor_scalar(lsl[:, :, 0:1], lsl[:, :, 0:1], sfc[:, 1:2],
                                      None, ALU.mult), "ve")
                p.wait(v, "ve")
            p.inc(v.tensor_copy(lls[:, :, 0:1], lcarry[:, :].rearrange("p (c o) -> p c o", o=1)),
                  "ve")
            p.wait(v, "ve")
            inst = None
            for dc in range(NCH):
                inst = v.tensor_tensor_scan(lls[:, dc, 1:SLAB + 1], lsl[:, dc, :],
                                            lsl[:, dc, :], lcarry[:, dc:dc + 1],
                                            ALU.add, ALU.bypass)
            p.inc(inst, "ve")
            p.wait(v, "ve")
            p.inc(v.tensor_copy(lcarry[:, :], lls[:, :, SLAB]), "ve")

        @blk.scalar
        def _(sc, s=s):
            p.wait(sc, "ve")
            p.inc(sc.activation(el[:, :, :], lls[:, :, 0:SLAB], AF.Exp), "se")

        @blk.vector
        def _(v, s=s):
            p.wait(v, "se")
            inst = None
            for dc in range(NCH):
                inst = v.tensor_tensor(gsl[:, dc, :], vslab[:, dc, :],
                                       ratio[:, dc, :], ALU.mult)
            p.inc(inst, "ve")
            p.wait(v, "ve")
            inst = None
            for dc in range(NCH):
                inst = v.tensor_tensor(gsl[:, dc, :], gsl[:, dc, :],
                                       el[:, dc, :], ALU.mult)
            p.inc(inst, "ve")
            p.wait(v, "ve")
            if s == 0:
                p.inc(v.tensor_scalar(tmpc[:, :], gsl[:, :, 0], sfc[:, 1:2],
                                      None, ALU.mult), "ve")
                p.wait(v, "ve")
                p.inc(v.scalar_tensor_tensor(gsl[:, :, 0], vslab[:, :, 0],
                                             sfc[:, 0:1], tmpc[:, :],
                                             ALU.mult, ALU.add), "ve")
                p.wait(v, "ve")
            p.inc(v.tensor_copy(gls[:, :, 0:1], gcarry[:, :].rearrange("p (c o) -> p c o", o=1)),
                  "ve")
            p.wait(v, "ve")
            inst = None
            for dc in range(NCH):
                inst = v.tensor_tensor_scan(gls[:, dc, 1:SLAB + 1], gsl[:, dc, :],
                                            gsl[:, dc, :], gcarry[:, dc:dc + 1],
                                            ALU.add, ALU.bypass)
            p.inc(inst, "ve")
            p.wait(v, "ve")
            p.inc(v.tensor_copy(gcarry[:, :], gls[:, :, SLAB]), "ve")
            if s == NSLAB - 1:
                v.tensor_copy(td[:, :, 0:1], lls[:, :, SLAB - 1:SLAB])
                v.tensor_copy(td[:, :, 1:2], lls[:, :, SLAB:SLAB + 1])
                v.tensor_copy(td[:, :, 2:3], gls[:, :, SLAB - 1:SLAB])
                p.inc(v.tensor_copy(td[:, :, 3:4], gls[:, :, SLAB:SLAB + 1]), "ve")
            p.wait(v, "ve")

        @blk.gpsimd
        def _(g, s=s):
            p.wait(g, "ve")
            cur = bts(s, SLAB)
            p.inc(g.dma_start(out=o_m.rearrange("p (c t) -> p c t", c=NCH)[:, :, cur],
                              in_=mg[:, :, 1:SLAB + 1]), "dg", 16)
            p.inc(g.dma_start(out=o_ll.rearrange("p (c t) -> p c t", c=NCH)[:, :, cur],
                              in_=lls[:, :, 1:SLAB + 1]), "dg", 16)
            p.inc(g.dma_start(out=o_gl.rearrange("p (c t) -> p c t", c=NCH)[:, :, cur],
                              in_=gls[:, :, 1:SLAB + 1]), "dg", 16)

    @blk.gpsimd
    def _(g):
        p.wait(g, "ve")
        p.inc(g.dma_start(out=o_td, in_=td[:, :, :].rearrange("p c t -> p (c t)")),
              "dg", 16)
        p.wait(g, "dg")

    return p


def _pmask(c):
    m = np.zeros((1, NCORE), np.float32)
    base = (c // (NCORE // B)) * (NCORE // B)
    m[0, base:c] = 1.0
    return m


def run_l2(inputs, r1):
    p = build_launch2()
    ta_all = np.concatenate([np.asarray(r1[j]["o_ta"]) for j in range(NCORE)], axis=0)
    wg_full = np.ascontiguousarray(inputs["L1_ffn_w_gate"].astype(np.float32))
    wu_full = np.ascontiguousarray(inputs["L1_ffn_w_up"].astype(np.float32))
    wd_full = np.ascontiguousarray(inputs["L1_ffn_w_down"].astype(np.float32))
    maps = [dict(kc=np.asarray(r1[c]["o_k"]), ta_all=ta_all, pmask=_pmask(c),
                 wgs=np.ascontiguousarray(wg_full[c * 512:(c + 1) * 512]),
                 wus=np.ascontiguousarray(wu_full[c * 512:(c + 1) * 512]),
                 wds=np.ascontiguousarray(wd_full[c * 128:(c + 1) * 128]))
            for c in range(NCORE)]
    return run_bass_kernel_spmd(p.nc, maps, core_ids=list(range(NCORE))).results


def run_l3(r1, r2):
    p = build_launch3()
    ta_all = np.concatenate([np.asarray(r1[j]["o_ta"]) for j in range(NCORE)], axis=0)
    tb_all = np.concatenate([np.asarray(r2[j]["o_tb"]) for j in range(NCORE)], axis=0)
    maps = []
    for c in range(NCORE):
        sf = np.array([[1.0 if _seq_start(c) else 0.0]], np.float32)
        maps.append(dict(kc=np.asarray(r1[c]["o_k"]), vc=np.asarray(r1[c]["o_v"]),
                         ta_all=ta_all, tb_all=tb_all, pmask=_pmask(c), sflag=sf))
    return run_bass_kernel_spmd(p.nc, maps, core_ids=list(range(NCORE))).results


def build_launch4():
    p = P()
    xp = p.din("xp", [TL, C])
    kc = p.din("kc", [128, NCH * TL])
    vc = p.din("vc", [128, NCH * TL])
    rc = p.din("rc", [128, NCH * TL], BF16)
    mf = p.din("mf", [128, NCH * TL])
    llf = p.din("llf", [128, NCH * TL])
    glf = p.din("glf", [128, NCH * TL])
    ta_all = p.din("ta_all", [NCORE * 128, NCH * 2])
    tb_all = p.din("tb_all", [NCORE * 128, NCH * 2])
    td_all = p.din("td_all", [NCORE * 128, NCH * 4])
    pmask = p.din("pmask", [1, NCORE])
    urow = p.din("urow", [1, C])
    o_z1 = p.dout("o_z1", [TL, C])

    tsa = p.sb("tsa", [128, NCORE, NCH, 2])
    tsb = p.sb("tsb", [128, NCORE, NCH, 2])
    tsd = p.sb("tsd", [128, NCORE, NCH, 4])
    pmr = p.sb("pmr", [1, NCORE])
    pmc = p.sb("pmc", [128, NCORE])
    ones1 = p.sb("ones1", [1, 128])
    macc = p.sb("macc", [128, NCH, 2])
    laccs = p.sb("laccs", [128, NCH, 2])
    gaccs = p.sb("gaccs", [128, NCH, 2])
    ecol = p.sb("ecol", [128, NCH])
    mlead = p.sb("mlead", [128, NCH])
    lin = p.sb("lin", [128, NCH])
    elin = p.sb("elin", [128, NCH])
    ain = p.sb("ain", [128, NCH])
    ucol = p.sb("ucol", [128, NCH])
    ustage = p.sb("ustage", [128, C])
    identf = p.sb("identf", [128, 128], F32)
    kslab = p.sb("kslab", [128, NCH, SLAB])
    vslab = p.sb("vslab", [128, NCH, SLAB])
    rslab = p.sb("rslab", [128, NCH, SLAB])
    mext = p.sb("mext", [128, NCH, SLAB + 1])
    llext = p.sb("llext", [128, NCH, SLAB + 1])
    glext = p.sb("glext", [128, NCH, SLAB + 1])
    sA = p.sb("sA", [128, NCH, SLAB])
    sB = p.sb("sB", [128, NCH, SLAB])
    sC = p.sb("sC", [128, NCH, SLAB])
    x_tm = p.sb("x_tm", [128, NTS, C])
    z1_tm = p.sb("z1_tm", [128, NTS, C])
    st = p.sb("st", [128, NTS, 4])
    st2 = p.sb("st2", [128, NTS, 4])
    junk = p.sb("junk", [128, C])
    ptf = p.ps("l4ptf", [128, 512])
    ptg = p.ps("l4ptg", [128, 512])

    blk = p.stack.enter_context(p.nc.Block())
    emit_identity(p, blk, identf)

    @blk.sync
    def _(sy):
        p.inc(sy.dma_start(out=tsa[:, :, :, :],
                           in_=ta_all.rearrange("(j p) c -> p j c", p=128)
                           .rearrange("p j (c t) -> p j c t", t=2)), "dl", 16)
        p.inc(sy.dma_start(out=tsb[:, :, :, :],
                           in_=tb_all.rearrange("(j p) c -> p j c", p=128)
                           .rearrange("p j (c t) -> p j c t", t=2)), "dl", 16)
        p.inc(sy.dma_start(out=tsd[:, :, :, :],
                           in_=td_all.rearrange("(j p) c -> p j c", p=128)
                           .rearrange("p j (c t) -> p j c t", t=4)), "dl", 16)
        p.inc(sy.dma_start(out=pmr[:], in_=pmask), "dl", 16)

    @blk.gpsimd
    def _(g):
        p.inc(g.memset(ones1[:], 1.0), "po")
        p.inc(g.memset(ustage[:, :], 0.0), "po")

    @blk.sync
    def _(sy):
        p.wait(sy, "po")
        p.inc(sy.dma_start(out=ustage[0:1, :], in_=urow), "dl", 16)
        p.wait(sy, "dl")
    ld0 = p.cnt["dl"]

    @blk.tensor
    def _(t):
        p.wait(t, "dl", ld0)
        p.wait(t, "po")
        p.inc(t.matmul(ptf[:, 0:NCORE], ones1[:], pmr[:], start=True, stop=True), "pe")
        for dc in range(NCH):
            p.inc(t.transpose(ptg[:, bts(dc % 4, 128)], ustage[:, bts(dc, 128)],
                              identf[:]), "pe")
            if dc % 4 == 3:
                pass
    pe_after_u = p.cnt["pe"]

    @blk.vector
    def _(v):
        p.wait(v, "pe", pe_after_u)
        inst = None
        for dc in range(NCH):
            inst = v.tensor_copy(ucol[:, dc:dc + 1],
                                 ptg[:, (dc % 4) * 128:(dc % 4) * 128 + 1])
        p.inc(inst, "ve")
        p.inc(v.tensor_copy(pmc[:, :], ptf[:, 0:NCORE]), "ve")
        p.wait(v, "ve")

    @blk.tensor
    def _(t):
        p.wait(t, "ve")
        inst = None
        for dc in range(4):
            inst = t.transpose(ptf[:, bts(dc, 128)], ustage[:, bts(dc, 128)],
                               identf[:])
        p.inc(inst, "pe")
    pe_u2 = p.cnt["pe"]

    @blk.vector
    def _(v):
        p.wait(v, "pe", pe_u2)
        inst = None
        for dc in range(4):
            inst = v.tensor_copy(ucol[:, dc:dc + 1], ptf[:, dc * 128:dc * 128 + 1])
        p.inc(inst, "ve")

    @blk.scalar
    def _(sc):
        p.wait(sc, "ve")
        p.inc(sc.activation(ucol[:, :], ucol[:, :], AF.Exp), "se")

    @blk.vector
    def _(v):
        p.wait(v, "ve")
        for (dst, stk, col) in ((mlead, tsa, 1), (ecol, tsb, 1)):
            p.inc(v.memset(macc[:, :, 0], 0.0), "ve")
            p.wait(v, "ve")
            for j in range(NCORE):
                a, b = j % 2, 1 - j % 2
                p.inc(v.scalar_tensor_tensor(macc[:, :, b], stk[:, j, :, col],
                                             pmc[:, j:j + 1], macc[:, :, a],
                                             ALU.mult, ALU.add), "ve")
                p.wait(v, "ve")
            p.inc(v.tensor_copy(dst[:, :], macc[:, :, NCORE % 2]), "ve")
            p.wait(v, "ve")
        p.inc(v.scalar_tensor_tensor(mlead[:, :], ecol[:, :], EPS_WKV, mlead[:, :],
                                     ALU.mult, ALU.add), "ve")
        p.wait(v, "ve")
        v.memset(laccs[:, :, 0], 0.0)
        p.inc(v.memset(gaccs[:, :, 0], 0.0), "ve")
        p.wait(v, "ve")

    for j in range(NCORE):
        a, b = j % 2, 1 - j % 2

        @blk.scalar
        def _(sc, j=j, a=a):
            p.wait(sc, "ve")
            p.inc(sc.activation(ecol[:, :], laccs[:, :, a], AF.Exp), "se")

        @blk.vector
        def _(v, j=j, a=a, b=b):
            p.wait(v, "se")
            p.inc(v.tensor_tensor(elin[:, :], ecol[:, :], tsd[:, j, :, 3], ALU.mult),
                  "ve")
            p.wait(v, "ve")
            p.inc(v.scalar_tensor_tensor(gaccs[:, :, b], elin[:, :], pmc[:, j:j + 1],
                                         gaccs[:, :, a], ALU.mult, ALU.add), "ve")
            p.inc(v.scalar_tensor_tensor(laccs[:, :, b], tsd[:, j, :, 1],
                                         pmc[:, j:j + 1], laccs[:, :, a],
                                         ALU.mult, ALU.add), "ve")
            p.wait(v, "ve")

    @blk.vector
    def _(v):
        p.wait(v, "ve")
        v.tensor_copy(lin[:, :], laccs[:, :, NCORE % 2])
        p.inc(v.tensor_copy(ain[:, :], gaccs[:, :, NCORE % 2]), "ve")

    @blk.scalar
    def _(sc):
        p.wait(sc, "ve")
        p.inc(sc.activation(elin[:, :], lin[:, :], AF.Exp), "se")
    prefix_done_ve = p.cnt["ve"]
    prefix_done_se = p.cnt["se"]

    def fmv(dram):
        return dram.rearrange("p (c t) -> p c t", c=NCH)

    slab_ve = 0
    slab_se = 0
    slab_dma = 0
    for s in range(NSLAB):
        @blk.sync
        def _(sy, s=s, sv=slab_ve):
            p.wait(sy, "ve", sv)
            p.inc(sy.dma_start(out=kslab[:, :, :], in_=fmv(kc)[:, :, bts(s, SLAB)]),
                  "dl", 16)
            p.inc(sy.dma_start(out=vslab[:, :, :], in_=fmv(vc)[:, :, bts(s, SLAB)]),
                  "dl", 16)
            lo = s * SLAB - 1
            if s == 0:
                p.inc(sy.dma_start(out=mext[:, :, 1:SLAB + 1],
                                   in_=fmv(mf)[:, :, 0:SLAB]), "dl", 16)
            else:
                p.inc(sy.dma_start(out=mext[:, :, :],
                                   in_=fmv(mf)[:, :, lo:lo + SLAB + 1]), "dl", 16)
            src = xp[s * SLAB:(s + 1) * SLAB, :].rearrange("(a p) c -> p a c", p=128)
            p.inc(sy.dma_start(out=x_tm[:, :, :], in_=src), "dl", 16)
            p.wait(sy, "dl")

        @blk.gpsimd
        def _(g, s=s, sv=slab_ve):
            p.wait(g, "ve", sv)
            p.inc(g.dma_start(out=rslab[:, :, :], in_=fmv(rc)[:, :, bts(s, SLAB)]),
                  "dp", 16)
            lo = s * SLAB - 1
            if s == 0:
                p.inc(g.dma_start(out=llext[:, :, 1:SLAB + 1],
                                  in_=fmv(llf)[:, :, 0:SLAB]), "dp", 16)
                p.inc(g.dma_start(out=glext[:, :, 1:SLAB + 1],
                                  in_=fmv(glf)[:, :, 0:SLAB]), "dp", 16)
            else:
                p.inc(g.dma_start(out=llext[:, :, :],
                                  in_=fmv(llf)[:, :, lo:lo + SLAB + 1]), "dp", 16)
                p.inc(g.dma_start(out=glext[:, :, :],
                                  in_=fmv(glf)[:, :, lo:lo + SLAB + 1]), "dp", 16)
            p.wait(g, "dp")
        ldk = p.cnt["dl"]
        ldr = p.cnt["dp"]

        @blk.vector
        def _(v, ldk=ldk):
            p.wait(v, "dl", ldk)

        @blk.scalar
        def _(sc, ldk=ldk):
            p.wait(sc, "dl", ldk)

        xs = [x_tm[:, i, :] for i in range(NTS)]
        emit_ln_batch(p, blk, xs, xs, st, st2, junk, wv=[("dl", ldk)])
        h0_done = p.cnt["ve"]

        @blk.vector
        def _(v, s=s, ldr=ldr):
            p.wait(v, "se", prefix_done_se)
            p.wait(v, "dp", ldr)
            if s == 0:
                v.tensor_copy(mext[:, :, 0:1],
                              mlead[:, :].rearrange("p (c o) -> p c o", o=1))
                v.memset(llext[:, :, 0:1], 0.0)
                p.inc(v.memset(glext[:, :, 0:1], 0.0), "ve")
                p.wait(v, "ve")
            inst = None
            for dc in range(NCH):
                inst = v.tensor_scalar(llext[:, dc, :], llext[:, dc, :],
                                       lin[:, dc:dc + 1], None, ALU.add)
            p.inc(inst, "ve")
            p.wait(v, "ve")
            inst = None
            for dc in range(NCH):
                inst = v.tensor_scalar(glext[:, dc, :], glext[:, dc, :],
                                       elin[:, dc:dc + 1], ain[:, dc:dc + 1],
                                       ALU.mult, ALU.add)
            p.inc(inst, "ve")
            p.wait(v, "ve")
            p.wait(v, "dp", ldr)
            inst = None
            for dc in range(NCH):
                inst = v.tensor_scalar(kslab[:, dc, :], kslab[:, dc, :],
                                       ucol[:, dc:dc + 1], None, ALU.mult)
            p.inc(inst, "ve")
            p.wait(v, "ve")

        @blk.scalar
        def _(sc, s=s):
            p.wait(sc, "ve")
            p.inc(sc.activation(sA[:, :, :], llext[:, :, 0:SLAB], AF.Exp,
                                scale=-1.0), "se")

        @blk.vector
        def _(v, s=s):
            p.wait(v, "se")
            fl = lambda t3: t3.rearrange("p c t -> p (c t)")
            p.inc(v.tensor_tensor(sA[:, :, :], glext[:, :, 0:SLAB], sA[:, :, :],
                                  ALU.mult), "ve")
            p.wait(v, "ve")
            p.inc(v.tensor_tensor(vslab[:, :, :], kslab[:, :, :], vslab[:, :, :],
                                  ALU.mult), "ve")
            p.wait(v, "ve")
            p.inc(v.tensor_tensor(sA[:, :, :], mext[:, :, 0:SLAB], sA[:, :, :],
                                  ALU.mult), "ve")
            p.wait(v, "ve")
            p.inc(v.tensor_tensor(sA[:, :, :], sA[:, :, :], vslab[:, :, :],
                                  ALU.add), "ve")
            p.wait(v, "ve")
            p.inc(v.tensor_tensor(sB[:, :, :], mext[:, :, 0:SLAB], kslab[:, :, :],
                                  ALU.max), "ve")
            p.wait(v, "ve")
            p.inc(v.tensor_tensor(kslab[:, :, :], mext[:, :, 0:SLAB], kslab[:, :, :],
                                  ALU.add), "ve")
            p.wait(v, "ve")
            p.inc(v.scalar_tensor_tensor(sB[:, :, :], sB[:, :, :], EPS_WKV,
                                         kslab[:, :, :], ALU.mult, ALU.add), "ve")
            p.wait(v, "ve")

        @blk.vector
        def _(v, s=s):
            p.wait(v, "ve")
            p.inc(v.reciprocal(sC[:, :, :], sB[:, :, :]), "ve")
            p.wait(v, "ve")
            p.inc(v.tensor_tensor(sA[:, :, :], sA[:, :, :], sC[:, :, :], ALU.mult),
                  "ve")
            p.wait(v, "ve")
            p.inc(v.tensor_tensor(sA[:, :, :], sA[:, :, :], rslab[:, :, :],
                                  ALU.mult), "ve")
            p.wait(v, "ve")
        u1_done = p.cnt["ve"]

        for i in range(NTS):
            @blk.tensor
            def _(t, s=s, i=i, ud=u1_done, hd=h0_done, sv=slab_ve):
                p.wait(t, "ve", ud)
                p.wait(t, "ve", sv)
                inst = None
                for dc in range(NCH):
                    bankap = ptf if dc < 4 else ptg
                    inst = t.transpose(bankap[:, bts(dc % 4, 128)],
                                       sA[:, dc, bts(i, 128)], identf[:])
                p.inc(inst, "pe")
            tpd = p.cnt["pe"]

            @blk.vector
            def _(v, s=s, i=i, tpd=tpd, hd=h0_done, dmap=slab_dma):
                p.wait(v, "pe", tpd)
                p.wait(v, "ve", hd)
                p.wait(v, "dg", dmap)
                v.tensor_tensor(z1_tm[:, i, 0:512], ptf[:, :], x_tm[:, i, 0:512],
                                ALU.add)
                p.inc(v.tensor_tensor(z1_tm[:, i, 512:1024], ptg[:, :],
                                      x_tm[:, i, 512:1024], ALU.add), "ve")
            slab_ve = p.cnt["ve"]

        @blk.gpsimd
        def _(g, s=s, sv=slab_ve):
            p.wait(g, "ve", sv)
            dst = o_z1[s * SLAB:(s + 1) * SLAB, :].rearrange("(a p) c -> p a c", p=128)
            p.inc(g.dma_start(out=dst, in_=z1_tm[:, :, :]), "dg", 16)
        slab_dma = p.cnt["dg"]
        slab_ve = p.cnt["ve"]

    @blk.gpsimd
    def _(g):
        p.wait(g, "dg")

    return p


def run_l4(inputs, x_rows, r1, r2, r3):
    p = build_launch4()
    ta_all = np.concatenate([np.asarray(r1[j]["o_ta"]) for j in range(NCORE)], axis=0)
    tb_all = np.concatenate([np.asarray(r2[j]["o_tb"]) for j in range(NCORE)], axis=0)
    td_all = np.concatenate([np.asarray(r3[j]["o_td"]) for j in range(NCORE)], axis=0)
    urow = np.ascontiguousarray(
        inputs["L0_attn_time_first"].astype(np.float32).reshape(1, C))
    maps = []
    for c in range(NCORE):
        maps.append(dict(
            xp=np.ascontiguousarray(x_rows[c * TL:(c + 1) * TL]),
            kc=np.asarray(r1[c]["o_k"]), vc=np.asarray(r1[c]["o_v"]),
            rc=np.asarray(r1[c]["o_r"]), mf=np.asarray(r3[c]["o_m"]),
            llf=np.asarray(r3[c]["o_ll"]), glf=np.asarray(r3[c]["o_gl"]),
            ta_all=ta_all, tb_all=tb_all, td_all=td_all, pmask=_pmask(c),
            urow=urow))
    return run_bass_kernel_spmd(p.nc, maps, core_ids=list(range(NCORE))).results


def emit_phase5(p, blk, z1d, z1p, wkc, wrc, prow, sflag, o_s, o_rc):
    NIC = 4 * NCH
    identb = p.sb("identb", [128, 128], BF16)
    identf = p.sb("identf", [128, 128], F32)
    wstage = p.sb("wstage", [128, C])
    wbf = p.sb("wbf", [128, C], BF16)
    wkcT = p.sb("wkcT", [128, NCH, 4 * C], BF16)
    wrcT = p.sb("wrcT", [128, NCH, C], BF16)
    ckc = p.sb("ckc", [128, NIC])
    crc = p.sb("crc", [128, NCH])
    pcol = p.sb("pcol", [128, NCH, 8])
    pstage = p.sb("pstage", [128, NCH * 128])
    b2cb = p.sb("b2cb", [128, NCH], BF16)
    halo2 = p.sb("halo2", [1, C])
    zn_h = p.sb("zn_h", [1, C])
    gr = p.sb("gr", [1, C])
    nbg = p.sb("nbg", [1, C])
    g2row = p.sb("g2row", [1, C])
    b2row = p.sb("b2row", [1, C])
    sfc = p.sb("sfc", [128, 2])
    ones1 = p.sb("ones1", [1, 128])
    znf = p.sb("znf", [128, NCH, 1 + SLAB], BF16)
    colt = p.sb("colt", [128, NCH])
    x_tm = p.sb("x_tm", [128, NTS, C])
    zn_tm = p.sb("zn_tm", [128, NTS, C], BF16)
    st = p.sb("st", [128, NTS, 4])
    st2 = p.sb("st2", [128, NTS, 4])
    junk = wstage
    dts = p.sb("dts", [128, NCH, SLAB], BF16)
    tsk = p.sb("tsk", [128, NCH, SLAB], BF16)
    tsr = p.sb("tsr", [128, NCH, SLAB], BF16)
    sout = p.sb("sout", [128, NIC, SLAB], BF16)
    stmp = p.sb("stmp", [128, SLAB])
    rcout = dts

    pbank = [p.ps(f"pb{i}", [128, 512]) for i in range(4)]
    ptb = p.ps("ptb", [128, 1024], BF16)
    ptf = p.ps("ptf", [128, 512], F32)

    emit_identity(p, blk, identb)
    emit_identity(p, blk, identf)

    @blk.gpsimd
    def _(g):
        p.inc(g.memset(pstage[:, :], 0.0), "po")
        p.inc(g.memset(ones1[:], 1.0), "po")

    @blk.sync
    def _(sy):
        p.wait(sy, "po")
        p.inc(sy.dma_start(out=pstage[0:4, :], in_=prow[:, :]), "dl", 16)
        p.inc(sy.dma_start(out=zn_h[:], in_=z1p), "dl", 16)
        p.inc(sy.dma_start(out=sfc[0:1, 0:1], in_=sflag), "dl", 16)
        p.inc(sy.dma_start(out=g2row[:], in_=prow[0:1, :]), "dl", 16)
        p.inc(sy.dma_start(out=b2row[:], in_=prow[1:2, :]), "dl", 16)
        p.wait(sy, "dl")
    ld_par = p.cnt["dl"]

    @blk.tensor
    def _(t):
        p.wait(t, "dl", ld_par)
        p.wait(t, "po")
        p.inc(t.matmul(ptf[:, 0:1], ones1[:], sfc[0:1, 0:1], start=True, stop=True),
              "pe")

    @blk.vector
    def _(v):
        p.wait(v, "pe")
        p.inc(v.tensor_copy(sfc[:, 0:1], ptf[:, 0:1]), "ve")
        p.wait(v, "ve")
        p.inc(v.tensor_scalar(sfc[:, 1:2], sfc[:, 0:1], -1.0, 1.0, ALU.mult, ALU.add),
              "ve")

    @blk.vector
    def _(v):
        p.wait(v, "dl", ld_par)
        p.inc(v.reciprocal(gr[:], g2row[:]), "ve")
        p.wait(v, "ve")
        p.inc(v.scalar_tensor_tensor(nbg[:], b2row[:], -1.0, gr[:],
                                     ALU.mult, ALU.mult), "ve")
        p.wait(v, "ve")
        p.inc(v.tensor_copy(pstage[32:33, :], nbg[:]), "ve")

    emit_ln_batch(p, blk, [zn_h[:, :]], [halo2[:, :]], st, st2, junk,
                  wv=[("dl", ld_par)])

    @blk.vector
    def _(v):
        p.wait(v, "ve")
        p.inc(v.tensor_copy(pstage[64:65, :], halo2[:, :]), "ve")
    pstage_ready = p.cnt["ve"]

    ev_ptf = [p.cnt["ve"]]
    for dc in range(NCH):
        @blk.tensor
        def _(t, dc=dc):
            p.wait(t, "ve", pstage_ready)
            p.wait(t, "ve", ev_ptf[0])
            p.inc(t.transpose(ptf[:, 0:128], pstage[:, bts(dc, 128)], identf[:]),
                  "pe")
        tpd = p.cnt["pe"]

        @blk.vector
        def _(v, dc=dc, tpd=tpd):
            p.wait(v, "pe", tpd)
            v.tensor_copy(pcol[:, dc, 0:4], ptf[:, 0:4])
            v.tensor_copy(pcol[:, dc, 5:6], ptf[:, 32:33])
            ev_ptf[0] = p.inc(v.tensor_copy(pcol[:, dc, 6:7], ptf[:, 64:65]), "ve")

    @blk.vector
    def _(v):
        p.wait(v, "ve")
        inst = None
        for dc in range(NCH):
            inst = v.tensor_tensor(colt[:, dc:dc + 1], pcol[:, dc, 6:7],
                                   pcol[:, dc, 5:6], ALU.subtract)
        p.inc(inst, "ve")
        p.wait(v, "ve")
        for dc in range(NCH):
            inst = v.scalar_tensor_tensor(znf[:, dc, 0:1], colt[:, dc:dc + 1],
                                          sfc[:, 1:2], pcol[:, dc, 5:6],
                                          ALU.mult, ALU.add)
        p.inc(inst, "ve")
        p.wait(v, "ve")
        p.inc(v.tensor_copy(b2cb[:, :], pcol[:, :, 1]), "ve")
    b2_ready = p.cnt["ve"]

    cast_done = 0
    tp_done = 0
    ev_ptb = p.cnt["ve"]
    wspec = [("kc", wkc, wkcT, 4 * NCH), ("rc", wrc, wrcT, NCH)]
    se_pre = p.cnt["se"]
    ve_pre = p.cnt["ve"]
    for nm, wd, wTd, noc in wspec:
        for oc in range(noc):
            @blk.sync
            def _(sy, wd=wd, oc=oc, cd=cast_done):
                p.wait(sy, "ve", max(cd, ve_pre))
                p.wait(sy, "se", se_pre)
                p.inc(sy.dma_start(out=wstage[:, :], in_=wd[bts(oc, 128), :]),
                      "dl", 16)
                p.wait(sy, "dl")
            ldw = p.cnt["dl"]

            @blk.vector
            def _(v, ldw=ldw, tpd=tp_done):
                p.wait(v, "dl", ldw)
                p.wait(v, "pe", tpd)
                p.inc(v.tensor_copy(wbf[:, :], wstage[:, :]), "ve")
            cast_done = p.cnt["ve"]

            @blk.tensor
            def _(t, cd=cast_done, evp=ev_ptb):
                p.wait(t, "ve", cd)
                p.wait(t, "ve", evp)
                inst = None
                for dc in range(NCH):
                    inst = t.transpose(ptb[:, bts(dc, 128)], wbf[:, bts(dc, 128)],
                                       identb[:])
                p.inc(inst, "pe")
            tp_done = p.cnt["pe"]

            @blk.vector
            def _(v, wTd=wTd, oc=oc, tpd=tp_done):
                p.wait(v, "pe", tpd)
                p.inc(v.tensor_copy(wTd[:, :, bts(oc, 128)],
                                    ptb[:, :].rearrange("p (c j) -> p c j", c=NCH)),
                      "ve")
            ev_ptb = p.cnt["ve"]
    wT_ready = p.cnt["ve"]

    @blk.tensor
    def _(t):
        p.wait(t, "ve", max(wT_ready, b2_ready))
        p.wait(t, "ve", ev_ptf[0])
        inst = None
        for oc in range(4 * NCH):
            for dc in range(NCH):
                inst = t.matmul(ptf[:, oc:oc + 1], wkcT[:, dc, bts(oc, 128)],
                                b2cb[:, dc:dc + 1],
                                start=(dc == 0), stop=(dc == NCH - 1))
        for oc in range(NCH):
            for dc in range(NCH):
                inst = t.matmul(ptf[:, 40 + oc:41 + oc], wrcT[:, dc, bts(oc, 128)],
                                b2cb[:, dc:dc + 1],
                                start=(dc == 0), stop=(dc == NCH - 1))
        p.inc(inst, "pe")
    cvec_done = p.cnt["pe"]

    @blk.vector
    def _(v):
        p.wait(v, "pe", cvec_done)
        v.tensor_copy(ckc[:, :], ptf[:, 0:4 * NCH])
        p.inc(v.tensor_copy(crc[:, :], ptf[:, 40:40 + NCH]), "ve")
        p.wait(v, "ve")
        inst = None
        for dc in range(NCH):
            inst = v.tensor_scalar(wkcT[:, dc, :], wkcT[:, dc, :], pcol[:, dc, 0:1],
                                   None, ALU.mult)
        for dc in range(NCH):
            inst = v.tensor_scalar(wrcT[:, dc, :], wrcT[:, dc, :], pcol[:, dc, 0:1],
                                   None, ALU.mult)
        p.inc(inst, "ve")
    prep_done = p.cnt["ve"]

    bank_ev = [p.cnt["se"]] * 4
    bank_ev_v = [p.cnt["ve"]] * 4
    slab_ln_ve = 0
    slab_ln_se = 0
    slab_tp = 0
    slab_mm = 0
    slab_dma = 0
    for s in range(NSLAB):
        @blk.sync
        def _(sy, s=s, lv=slab_ln_ve, ls=slab_ln_se):
            p.wait(sy, "ve", lv)
            p.wait(sy, "se", ls)
            src = z1d[s * SLAB:(s + 1) * SLAB, :].rearrange("(a p) c -> p a c", p=128)
            p.inc(sy.dma_start(out=x_tm[:, :, :], in_=src), "dl", 16)
            p.wait(sy, "dl")
        ld_x = p.cnt["dl"]

        @blk.vector
        def _(v, s=s, ld_x=ld_x, tp=slab_tp):
            p.wait(v, "dl", ld_x)
            p.wait(v, "pe", tp)
            if s > 0:
                p.inc(v.tensor_copy(znf[:, :, 0:1], znf[:, :, SLAB:SLAB + 1]), "ve")
                p.wait(v, "ve")

        @blk.scalar
        def _(sc, ld_x=ld_x):
            p.wait(sc, "dl", ld_x)

        xs = [x_tm[:, i, :] for i in range(NTS)]
        zns = [zn_tm[:, i, :] for i in range(NTS)]
        emit_ln_batch(p, blk, xs, zns, st, st2, junk,
                      wv=[("dl", ld_x), ("ve", prep_done)])
        slab_ln_ve = p.cnt["ve"]
        slab_ln_se = p.cnt["se"]
        zn_ready = p.cnt["ve"]

        for i in range(NTS):
            @blk.tensor
            def _(t, i=i, znr=zn_ready, evp=ev_ptb):
                p.wait(t, "ve", znr)
                p.wait(t, "ve", evp)
                inst = None
                for dc in range(NCH):
                    inst = t.transpose(ptb[:, bts(dc, 128)],
                                       zn_tm[:, i, bts(dc, 128)], identb[:])
                p.inc(inst, "pe")
            tpd = p.cnt["pe"]

            @blk.vector
            def _(v, i=i, tpd=tpd):
                p.wait(v, "pe", tpd)
                dst = znf[:, :, 1 + i * 128: 1 + (i + 1) * 128]
                p.inc(v.tensor_copy(dst,
                                    ptb[:, :].rearrange("p (c j) -> p c j", c=NCH)),
                      "ve")
            ev_ptb = p.cnt["ve"]
        slab_tp = p.cnt["pe"]
        znf_ready = p.cnt["ve"]

        @blk.vector
        def _(v, s=s, zr=znf_ready, mm=slab_mm, dmap=slab_dma):
            p.wait(v, "ve", zr)
            p.wait(v, "pe", mm)
            p.wait(v, "dg", dmap)
            cur = slice(1, 1 + SLAB)
            prv = slice(0, SLAB)
            inst = None
            for dc in range(NCH):
                inst = v.tensor_tensor(dts[:, dc, :], znf[:, dc, cur],
                                       znf[:, dc, prv], ALU.subtract)
            p.inc(inst, "ve")
            p.wait(v, "ve")
            for wi, buf in ((2, tsk), (3, tsr)):
                for dc in range(NCH):
                    inst = v.scalar_tensor_tensor(buf[:, dc, :], dts[:, dc, :],
                                                  pcol[:, dc, wi:wi + 1],
                                                  znf[:, dc, prv], ALU.mult, ALU.add)
            p.inc(inst, "ve")
        ts_ready = p.cnt["ve"]

        stmp_se = [p.cnt["se"]]
        gi = 0
        for oc in range(4 * NCH):
            bk = gi % 4
            gi += 1

            @blk.tensor
            def _(t, oc=oc, bk=bk, s=s, gi=gi, tsrd=ts_ready, dmap=slab_dma):
                p.wait(t, "ve", tsrd)
                p.wait(t, "se", bank_ev[bk])
                p.wait(t, "ve", bank_ev_v[bk])
                if gi == 1:
                    p.wait(t, "dg", dmap)
                inst = None
                for dc in range(NCH):
                    inst = t.matmul(pbank[bk][:, :], wkcT[:, dc, bts(oc, 128)],
                                    tsk[:, dc, :], start=(dc == 0),
                                    stop=(dc == NCH - 1))
                p.inc(inst, "pe")
            mmd = p.cnt["pe"]

            @blk.vector
            def _(v, oc=oc, bk=bk, mmd=mmd, pd=prep_done, ss=stmp_se[0]):
                p.wait(v, "pe", mmd)
                p.wait(v, "ve", pd)
                p.wait(v, "se", ss)
                bank_ev_v[bk] = p.inc(
                    v.tensor_scalar(stmp[:, :], pbank[bk][:, :], ckc[:, oc:oc + 1],
                                    0.0, ALU.add, ALU.max), "ve")

            @blk.scalar
            def _(sc, oc=oc, bk=bk):
                p.wait(sc, "ve")
                bank_ev[bk] = p.inc(
                    sc.activation(sout[:, oc, :], stmp[:, :], AF.Square), "se")
            stmp_se[0] = p.cnt["se"]

        for oc in range(NCH):
            bk = gi % 4
            gi += 1

            @blk.tensor
            def _(t, oc=oc, bk=bk, tsrd=ts_ready):
                p.wait(t, "ve", tsrd)
                p.wait(t, "se", bank_ev[bk])
                p.wait(t, "ve", bank_ev_v[bk])
                inst = None
                for dc in range(NCH):
                    inst = t.matmul(pbank[bk][:, :], wrcT[:, dc, bts(oc, 128)],
                                    tsr[:, dc, :], start=(dc == 0),
                                    stop=(dc == NCH - 1))
                p.inc(inst, "pe")
            mmd = p.cnt["pe"]

            @blk.scalar
            def _(sc, oc=oc, bk=bk, mmd=mmd, pd=prep_done):
                p.wait(sc, "pe", mmd)
                p.wait(sc, "ve", pd)
                bank_ev[bk] = p.inc(
                    sc.activation(rcout[:, oc, :], pbank[bk][:, :], AF.Sigmoid,
                                  bias=crc[:, oc:oc + 1]), "se")
        slab_mm = p.cnt["pe"]
        ev_all = p.cnt["se"]

        @blk.gpsimd
        def _(g, s=s, ev=ev_all):
            p.wait(g, "se", ev)
            p.wait(g, "ve")
            cur = bts(s, SLAB)
            p.inc(g.dma_start(
                out=o_s.rearrange("p (c t) -> p c t", c=4 * NCH)[:, :, cur],
                in_=sout[:, :, :]), "dg", 16)
            p.inc(g.dma_start(
                out=o_rc.rearrange("p (c t) -> p c t", c=NCH)[:, :, cur],
                in_=rcout[:, :, :]), "dg", 16)
        slab_dma = p.cnt["dg"]


def run_l5(inputs, r4):
    p = build_launch5()
    prow = np.ascontiguousarray(np.stack(
        [inputs["L0_attn_ln2_g"], inputs["L0_attn_ln2_b"],
         inputs["L0_attn_cmix_k"], inputs["L0_attn_cmix_r"]]).astype(np.float32))
    maps = []
    for c in range(NCORE):
        sf = np.array([[1.0 if _seq_start(c) else 0.0]], np.float32)
        z1prev = (np.asarray(r4[c - 1]["o_z1"])[TL - 1:TL]
                  if not _seq_start(c) else np.asarray(r4[c]["o_z1"])[0:1])
        maps.append(dict(
            z1d=np.asarray(r4[c]["o_z1"]), z1p=np.ascontiguousarray(z1prev),
            wkc=np.ascontiguousarray(inputs["L0_attn_Wk_c"]),
            wrc=np.ascontiguousarray(inputs["L0_attn_Wr_c"]),
            prow=prow, sflag=sf))
    return run_bass_kernel_spmd(p.nc, maps, core_ids=list(range(NCORE))).results


def emit_phase6(p, blk, z1d, xp, sdat, rcdat, wvc, wff, bffr, o_x2):
    SL6 = 256
    NS6 = TL // SL6
    NT6 = SL6 // 128
    NIC = 4 * NCH

    identb = p.sb("identb", [128, 128], BF16)
    identf = p.sb("identf", [128, 128], F32)
    wstage = p.sb("wstage", [128, 4 * C])
    wbf = p.sb("wbf", [128, 4 * C], BF16)
    wvcT = p.sb("wvcT", [128, NIC, C], BF16)
    wqT = p.sb("wqT", [128, NCH, C], BF16)
    wsc = p.sb("wsc", [128, NCH])
    wrq = p.sb("wrq", [128, NCH])
    bb = p.sb("bb", [128, C])
    ones1 = p.sb("ones1", [1, 128])
    bfr = p.sb("bfr", [1, C])
    k0 = p.sb("k0", [128, 1]); k1 = p.sb("k1", [128, 1])
    kap = p.sb("kap", [128, 1]); lam = p.sb("lam", [128, 1])
    r1t = p.sb("r1t", [128, C]); r2t = p.sb("r2t", [128, C]); r3t = p.sb("r3t", [128, C])
    sslab = p.sb("sslab", [128, NIC, SL6], BF16)
    rcslab = p.sb("rcslab", [128, NCH, SL6])
    z1_tm = p.sb("z1_tm", [128, NT6, C])
    x_tm = p.sb("x_tm", [128, NT6, C])
    x1_tm = p.sb("x1_tm", [128, NT6, C])
    h_tm = p.sb("h_tm", [128, NT6, C])
    xq_tm = p.sb("xq_tm", [128, NT6, C], BF16)
    xqT = p.sb("xqT", [128, NCH, SL6], BF16)
    u2 = p.sb("u2", [128, NCH, SL6])
    ffs = p.sb("ffs", [128, NCH, SL6])
    x2_tm = z1_tm
    st = p.sb("st", [128, NT6, 4])
    st2 = p.sb("st2", [128, NT6, 4])
    stq = p.sb("stq", [128, NT6, 4])
    junk = wstage

    pbank = [p.ps(f"pb{i}", [128, 512]) for i in range(4)]
    ptb = p.ps("ptb", [128, 1024], BF16)
    ptf = p.ps("ptf", [128, 512], F32)
    ptg = p.ps("ptg", [128, 512], F32)

    emit_identity(p, blk, identb)
    emit_identity(p, blk, identf)

    @blk.gpsimd
    def _(g):
        p.inc(g.memset(ones1[:], 1.0), "po")

    @blk.sync
    def _(sy):
        p.inc(sy.dma_start(out=bfr[:], in_=bffr), "dl", 16)
        p.wait(sy, "dl")
    ldb = p.cnt["dl"]

    @blk.vector
    def _(v):
        p.wait(v, "dl", ldb)
        p.inc(v.tensor_scalar(k0[0:1, 0:1], bfr[0:1, 0:1], 0.0, 0.5,
                              ALU.mult, ALU.add), "ve")
        p.wait(v, "ve")
        p.inc(v.tensor_scalar(k1[0:1, 0:1], k0[0:1, 0:1], C23, -C23,
                              ALU.add, ALU.add), "ve")
        p.wait(v, "ve")
        p.inc(v.tensor_scalar(kap[0:1, 0:1], k1[0:1, 0:1], 2.0, 0.0,
                              ALU.mult, ALU.add), "ve")
        p.wait(v, "ve")
        p.inc(v.tensor_scalar(lam[0:1, 0:1], kap[0:1, 0:1], -1.0, 1.0,
                              ALU.mult, ALU.add), "ve")
        p.wait(v, "ve")
    @blk.tensor
    def _(t):
        p.wait(t, "ve")
        p.wait(t, "po")
        t.matmul(ptf[:, 0:1], ones1[:], kap[0:1, 0:1], start=True, stop=True)
        p.inc(t.matmul(ptf[:, 1:2], ones1[:], lam[0:1, 0:1], start=True, stop=True),
              "pe")

    @blk.vector
    def _(v):
        p.wait(v, "pe")
        v.tensor_copy(kap[:, :], ptf[:, 0:1])
        p.inc(v.tensor_copy(lam[:, :], ptf[:, 1:2]), "ve")
    kap_ready = p.cnt["ve"]

    @blk.tensor
    def _(t):
        p.wait(t, "ve", kap_ready)
        t.matmul(ptf[:, 0:512], ones1[:], bfr[0:1, 0:512], start=True, stop=True)
        p.inc(t.matmul(ptg[:, 0:512], ones1[:], bfr[0:1, 512:1024],
                       start=True, stop=True), "pe")
    bbd = p.cnt["pe"]

    @blk.vector
    def _(v, bbd=bbd):
        p.wait(v, "pe", bbd)
        v.tensor_copy(bb[:, 0:512], ptf[:, :])
        p.inc(v.tensor_copy(bb[:, 512:1024], ptg[:, :]), "ve")
    bb_ready = p.cnt["ve"]

    cast_done = 0
    tp_done = 0
    ev_ptb = p.cnt["ve"]
    for oc in range(NCH):
        @blk.sync
        def _(sy, oc=oc, cd=cast_done):
            p.wait(sy, "ve", cd)
            p.inc(sy.dma_start(out=wstage[:, :], in_=wvc[bts(oc, 128), :]),
                  "dl", 16)
            p.wait(sy, "dl")
        ldw = p.cnt["dl"]

        @blk.vector
        def _(v, ldw=ldw, tpd=tp_done):
            p.wait(v, "dl", ldw)
            p.wait(v, "pe", tpd)
            p.inc(v.tensor_copy(wbf[:, :], wstage[:, :]), "ve")
        cast_done = p.cnt["ve"]

        for grp in range(4):
            @blk.tensor
            def _(t, oc=oc, grp=grp, cd=cast_done, evp=ev_ptb):
                p.wait(t, "ve", cd)
                p.wait(t, "ve", evp)
                inst = None
                for j in range(NCH):
                    ic = grp * NCH + j
                    inst = t.transpose(ptb[:, bts(j, 128)], wbf[:, bts(ic, 128)],
                                       identb[:])
                p.inc(inst, "pe")
            tp_done = p.cnt["pe"]

            @blk.vector
            def _(v, oc=oc, grp=grp, tpd=tp_done):
                p.wait(v, "pe", tpd)
                dst = wvcT[:, grp * NCH:(grp + 1) * NCH, bts(oc, 128)]
                p.inc(v.tensor_copy(dst,
                                    ptb[:, :].rearrange("p (c j) -> p c j", c=NCH)),
                      "ve")
            ev_ptb = p.cnt["ve"]
    wvc_ready = p.cnt["ve"]

    for oc in range(NCH):
        @blk.sync
        def _(sy, oc=oc, cd=cast_done):
            p.wait(sy, "ve", cd)
            p.inc(sy.dma_start(out=wstage[:, 0:C], in_=wff[bts(oc, 128), :]),
                  "dl", 16)
            p.wait(sy, "dl")
        ldw = p.cnt["dl"]

        @blk.vector
        def _(v, oc=oc, ldw=ldw):
            p.wait(v, "dl", ldw)
            p.inc(v.reduce_max(wsc[:, oc:oc + 1], wstage[:, 0:C], AX.X,
                               apply_absolute_value=True), "ve")
            p.wait(v, "ve")
            p.inc(v.tensor_scalar(wsc[:, oc:oc + 1], wsc[:, oc:oc + 1], 1e-8,
                                  1.0 / 127.0, ALU.max, ALU.mult), "ve")
            p.wait(v, "ve")

        @blk.scalar
        def _(sc, oc=oc):
            p.wait(sc, "ve")
            p.inc(sc.activation(wrq[:, oc:oc + 1], wsc[:, oc:oc + 1], AF.Ln), "se")
            p.wait(sc, "se")
            p.inc(sc.activation(wrq[:, oc:oc + 1], wrq[:, oc:oc + 1], AF.Exp,
                                scale=-1.0), "se")

        @blk.vector
        def _(v, oc=oc, kr=kap_ready, tpd=tp_done):
            p.wait(v, "se")
            p.wait(v, "ve", kr)
            p.wait(v, "pe", tpd)
            p.inc(v.tensor_scalar(r1t[:, :], wstage[:, 0:C], wrq[:, oc:oc + 1],
                                  None, ALU.mult), "ve")
            p.wait(v, "ve")
            v.tensor_scalar(r2t[:, :], r1t[:, :], lam[:, 0:1], C23, ALU.mult, ALU.add)
            p.inc(v.tensor_scalar(r3t[:, :], r1t[:, :], kap[:, 0:1], C52,
                                  ALU.mult, ALU.add), "ve")
            p.wait(v, "ve")
            p.inc(v.tensor_single_scalar(r2t[:, :], r2t[:, :], -C23, ALU.add), "ve")
            p.wait(v, "ve")
            p.inc(v.scalar_tensor_tensor(r1t[:, :], r3t[:, :], -C52, r2t[:, :],
                                         ALU.add, ALU.add), "ve")
            p.wait(v, "ve")
            p.inc(v.tensor_scalar(wbf[:, 0:C], r1t[:, :], 127.0, -128.0,
                                  ALU.min, ALU.max), "ve")
        cast_done = p.cnt["ve"]

        @blk.tensor
        def _(t, oc=oc, cd=cast_done, evp=ev_ptb):
            p.wait(t, "ve", cd)
            p.wait(t, "ve", evp)
            inst = None
            for dc in range(NCH):
                inst = t.transpose(ptb[:, bts(dc, 128)], wbf[:, bts(dc, 128)],
                                   identb[:])
            p.inc(inst, "pe")
        tp_done = p.cnt["pe"]

        @blk.vector
        def _(v, oc=oc, tpd=tp_done):
            p.wait(v, "pe", tpd)
            p.inc(v.tensor_copy(wqT[:, :, bts(oc, 128)],
                                ptb[:, :].rearrange("p (c j) -> p c j", c=NCH)),
                  "ve")
        ev_ptb = p.cnt["ve"]
    prep_done = p.cnt["ve"]


    def fmv(dram, nc_):
        return dram.rearrange("p (c t) -> p c t", c=nc_)

    bank_ev_v = [p.cnt["ve"]] * 4
    ev_ptfg = p.cnt["ve"]
    slab_ve = 0
    slab_se = 0
    slab_dma = 0
    for s in range(NS6):
        @blk.sync
        def _(sy, s=s, sv=slab_ve, ss=slab_se, dmap=slab_dma):
            p.wait(sy, "ve", sv)
            p.wait(sy, "se", ss)
            p.wait(sy, "dg", dmap)
            p.inc(sy.dma_start(out=sslab[:, :, :],
                               in_=fmv(sdat, NIC)[:, :, bts(s, SL6)]), "dl", 16)
            p.inc(sy.dma_start(
                out=z1_tm[:, :, :],
                in_=z1d[s * SL6:(s + 1) * SL6, :].rearrange("(a p) c -> p a c", p=128)),
                "dl", 16)
            p.inc(sy.dma_start(
                out=x_tm[:, :, :],
                in_=xp[s * SL6:(s + 1) * SL6, :].rearrange("(a p) c -> p a c", p=128)),
                "dl", 16)
            p.wait(sy, "dl")

        @blk.gpsimd
        def _(g, s=s, sv=slab_ve):
            p.wait(g, "ve", sv)
            p.inc(g.dma_start(out=rcslab[:, :, :],
                              in_=fmv(rcdat, NCH)[:, :, bts(s, SL6)]), "dp", 16)
            p.wait(g, "dp")
        ldk = p.cnt["dl"]
        ldr = p.cnt["dp"]

        for oc in range(NCH):
            bk = oc % 4

            @blk.tensor
            def _(t, oc=oc, bk=bk, ldk=ldk, wr=wvc_ready, bev=bank_ev_v[bk],
                  dmap=slab_dma):
                p.wait(t, "dl", ldk)
                p.wait(t, "ve", wr)
                p.wait(t, "ve", bev)
                if oc == 0:
                    p.wait(t, "dg", dmap)
                inst = None
                for ic in range(NIC):
                    inst = t.matmul(pbank[bk][:, 0:SL6], wvcT[:, ic, bts(oc, 128)],
                                    sslab[:, ic, :], start=(ic == 0),
                                    stop=(ic == NIC - 1))
                p.inc(inst, "pe")
            mmd = p.cnt["pe"]

            @blk.vector
            def _(v, oc=oc, bk=bk, mmd=mmd, ldr=ldr):
                p.wait(v, "pe", mmd)
                p.wait(v, "dp", ldr)
                bank_ev_v[bk] = p.inc(
                    v.tensor_tensor(u2[:, oc, :], pbank[bk][:, 0:SL6],
                                    rcslab[:, oc, :], ALU.mult), "ve")
        u2_done = p.cnt["ve"]

        for i in range(NT6):
            @blk.tensor
            def _(t, i=i, ud=u2_done, evp=ev_ptfg):
                p.wait(t, "ve", ud)
                p.wait(t, "ve", evp)
                inst = None
                for dc in range(NCH):
                    bank = ptf if dc < 4 else ptg
                    inst = t.transpose(bank[:, bts(dc % 4, 128)],
                                       u2[:, dc, bts(i, 128)], identf[:])
                p.inc(inst, "pe")
            tpd = p.cnt["pe"]

            @blk.vector
            def _(v, i=i, tpd=tpd):
                p.wait(v, "pe", tpd)
                v.tensor_tensor(x1_tm[:, i, 0:512], ptf[:, :], z1_tm[:, i, 0:512],
                                ALU.add)
                p.inc(v.tensor_tensor(x1_tm[:, i, 512:1024], ptg[:, :],
                                      z1_tm[:, i, 512:1024], ALU.add), "ve")
                p.wait(v, "ve")
                p.inc(v.tensor_tensor(x1_tm[:, i, :], x1_tm[:, i, :], x_tm[:, i, :],
                                      ALU.add), "ve")
            ev_ptfg = p.cnt["ve"]
        x1_done = p.cnt["ve"]

        x1s = [x1_tm[:, i, :] for i in range(NT6)]
        hs = [h_tm[:, i, :] for i in range(NT6)]
        emit_ln_batch(p, blk, x1s, hs, st, st2, junk, wv=[("ve", x1_done)])
        h_done = p.cnt["ve"]

        @blk.vector
        def _(v, s=s, hd=h_done):
            p.wait(v, "ve", hd)
            inst = None
            for i in range(NT6):
                inst = v.reduce_max(stq[:, i, 0:1], h_tm[:, i, :], AX.X,
                                    apply_absolute_value=True)
            p.inc(inst, "ve")
            p.wait(v, "ve")
            p.inc(v.tensor_scalar(stq[:, 0:NT6, 1:2], stq[:, 0:NT6, 0:1], 1e-8,
                                  1.0 / 127.0, ALU.max, ALU.mult), "ve")

        @blk.scalar
        def _(sc, s=s):
            p.wait(sc, "ve")
            p.inc(sc.activation(stq[:, 0:NT6, 2:3], stq[:, 0:NT6, 1:2], AF.Ln), "se")
            p.wait(sc, "se")
            p.inc(sc.activation(stq[:, 0:NT6, 3:4], stq[:, 0:NT6, 2:3], AF.Exp,
                                scale=-1.0), "se")

        @blk.vector
        def _(v, s=s):
            p.wait(v, "se")
            for i in range(NT6):
                p.inc(v.tensor_scalar(junk[:, 0:C], h_tm[:, i, :], stq[:, i, 3:4],
                                      None, ALU.mult), "ve")
                p.wait(v, "ve")
                v.tensor_scalar(r2t[:, :], junk[:, 0:C], lam[:, 0:1], C23,
                                ALU.mult, ALU.add)
                p.inc(v.tensor_scalar(r3t[:, :], junk[:, 0:C], kap[:, 0:1], C52,
                                      ALU.mult, ALU.add), "ve")
                p.wait(v, "ve")
                p.inc(v.tensor_single_scalar(r2t[:, :], r2t[:, :], -C23, ALU.add),
                      "ve")
                p.wait(v, "ve")
                p.inc(v.scalar_tensor_tensor(r1t[:, :], r3t[:, :], -C52, r2t[:, :],
                                             ALU.add, ALU.add), "ve")
                p.wait(v, "ve")
                p.inc(v.tensor_scalar(xq_tm[:, i, :], r1t[:, :], 127.0, -128.0,
                                      ALU.min, ALU.max), "ve")
                p.wait(v, "ve")
        xq_done = p.cnt["ve"]

        for i in range(NT6):
            @blk.tensor
            def _(t, i=i, xd=xq_done, evp=ev_ptb):
                p.wait(t, "ve", xd)
                p.wait(t, "ve", evp)
                inst = None
                for dc in range(NCH):
                    inst = t.transpose(ptb[:, bts(dc, 128)],
                                       xq_tm[:, i, bts(dc, 128)], identb[:])
                p.inc(inst, "pe")
            tpd = p.cnt["pe"]

            @blk.vector
            def _(v, i=i, tpd=tpd):
                p.wait(v, "pe", tpd)
                dst = xqT[:, :, bts(i, 128)]
                p.inc(v.tensor_copy(dst,
                                    ptb[:, :].rearrange("p (c j) -> p c j", c=NCH)),
                      "ve")
            ev_ptb = p.cnt["ve"]
        xqT_done = p.cnt["ve"]

        for oc in range(NCH):
            bk = oc % 4

            @blk.tensor
            def _(t, oc=oc, bk=bk, xd=xqT_done, pd=prep_done, bev=bank_ev_v[bk]):
                p.wait(t, "ve", xd)
                p.wait(t, "ve", pd)
                p.wait(t, "ve", bev)
                inst = None
                for dc in range(NCH):
                    inst = t.matmul(pbank[bk][:, 0:SL6], wqT[:, dc, bts(oc, 128)],
                                    xqT[:, dc, :], start=(dc == 0),
                                    stop=(dc == NCH - 1))
                p.inc(inst, "pe")
            mmd = p.cnt["pe"]

            @blk.vector
            def _(v, oc=oc, bk=bk, mmd=mmd):
                p.wait(v, "pe", mmd)
                bank_ev_v[bk] = p.inc(
                    v.tensor_scalar(ffs[:, oc, :], pbank[bk][:, 0:SL6],
                                    wsc[:, oc:oc + 1], None, ALU.mult), "ve")
        ff_done = p.cnt["ve"]

        for i in range(NT6):
            @blk.tensor
            def _(t, i=i, fd=ff_done, evp=ev_ptfg):
                p.wait(t, "ve", fd)
                p.wait(t, "ve", evp)
                inst = None
                for dc in range(NCH):
                    bank = ptf if dc < 4 else ptg
                    inst = t.transpose(bank[:, bts(dc % 4, 128)],
                                       ffs[:, dc, bts(i, 128)], identf[:])
                p.inc(inst, "pe")
            tpd = p.cnt["pe"]

            @blk.vector
            def _(v, i=i, tpd=tpd):
                p.wait(v, "pe", tpd)
                v.tensor_scalar(x2_tm[:, i, 0:512], ptf[:, :], stq[:, i, 1:2],
                                None, ALU.mult)
                p.inc(v.tensor_scalar(x2_tm[:, i, 512:1024], ptg[:, :],
                                      stq[:, i, 1:2], None, ALU.mult), "ve")
                p.wait(v, "ve")
                p.inc(v.tensor_tensor(x2_tm[:, i, :], x2_tm[:, i, :], bb[:, :],
                                      ALU.add), "ve")
                p.wait(v, "ve")
                p.inc(v.tensor_tensor(x2_tm[:, i, :], x2_tm[:, i, :], x1_tm[:, i, :],
                                      ALU.add), "ve")
            ev_ptfg = p.cnt["ve"]
        slab_ve = p.cnt["ve"]
        slab_se = p.cnt["se"]

        @blk.gpsimd
        def _(g, s=s, sv=slab_ve):
            p.wait(g, "ve", sv)
            dst = o_x2[s * SL6:(s + 1) * SL6, :].rearrange("(a p) c -> p a c", p=128)
            p.inc(g.dma_start(out=dst, in_=x2_tm[:, :, :]), "dg", 16)
        slab_dma = p.cnt["dg"]


def run_l6(inputs, x_rows, r4, r5):
    p = build_launch6()
    maps = []
    for c in range(NCORE):
        maps.append(dict(
            z1d=np.asarray(r4[c]["o_z1"]),
            xp=np.ascontiguousarray(x_rows[c * TL:(c + 1) * TL]),
            sdat=np.asarray(r5[c]["o_s"]), rcdat=np.asarray(r5[c]["o_rc"]),
            wvc=np.ascontiguousarray(inputs["L0_attn_Wv_c"]),
            wff=np.ascontiguousarray(inputs["L0_ffn_weight"]),
            bffr=np.ascontiguousarray(
                inputs["L0_ffn_bias"].astype(np.float32).reshape(1, C))))
    return run_bass_kernel_spmd(p.nc, maps, core_ids=list(range(NCORE))).results


def emit_phase7(p, blk, x2d, wgt, brow, o_phiq, o_kvz):

    identb = p.sb("identb", [128, 128], BF16)
    wstage = p.sb("wstage", [128, C])
    wbf = p.sb("wbf", [128, C], BF16)
    wT = {n: p.sb("wT_" + n, [128, NCH, C], BF16) for n in "qkv"}
    brow_q = p.sb("brow_q", [1, C], BF16)
    brow_k = p.sb("brow_k", [1, C], BF16)
    brow_v = p.sb("brow_v", [1, C], BF16)
    onesb = p.sb("onesb", [1, 512], BF16)
    x_tm = p.sb("x_tm", [128, NTS, C])
    h2b = p.sb("h2b", [128, NTS, C], BF16)
    h2T = p.sb("h2T", [128, NCH, SLAB], BF16)
    st = p.sb("st", [128, NTS, 4])
    st2 = p.sb("st2", [128, NTS, 4])
    junk = p.sb("junk", [128, C])
    sA = p.sb("sA", [128, SLAB])
    sB = p.sb("sB", [128, SLAB])
    sC = p.sb("sC", [128, SLAB])
    phiq = p.sb("phiq", [128, NCH, SLAB], BF16)
    phik = p.sb("phik", [128, NTS, C], BF16)
    vtm = p.sb("vtm", [128, NTS, C], BF16)
    vaug = p.sb("vaug", [128, 16, 65], BF16)
    kvzs = p.sb("kvzs", [64, 16, 65])

    pbank = [p.ps(f"pb{i}", [128, 512]) for i in range(3)]
    ptb = p.ps("ptb", [128, 1024], BF16)
    kvzp = [p.ps(f"kvz{i}", [64, 4 * 128]) for i in range(4)]

    emit_identity(p, blk, identb)

    @blk.gpsimd
    def _(g):
        p.inc(g.memset(onesb[:], 1.0), "po")

    @blk.gpsimd
    def _(g):
        for j, dstt in enumerate((brow_q, brow_k, brow_v)):
            p.inc(g.dma_start(out=dstt[0:1, :], in_=brow[j:j + 1, :]), "dp", 16)
        p.wait(g, "dp")
    ld_par = p.cnt["dp"]

    @blk.vector
    def _(v):
        p.inc(v.memset(vaug[:, :, 64:65], 1.0), "ve")
    brows_done = p.cnt["ve"]

    cast_done = 0
    tp_done = 0
    ev_ptb = p.cnt["ve"]
    for n in "qkv":
        for oc in range(NCH):
            @blk.sync
            def _(sy, n=n, oc=oc, cd=cast_done):
                p.wait(sy, "ve", max(cd, brows_done))
                p.wait(sy, "dp", ld_par)
                p.inc(sy.dma_start(out=wstage[:, :], in_=wgt[n][bts(oc, 128), :]),
                      "dl", 16)
                p.wait(sy, "dl")
            ldw = p.cnt["dl"]

            @blk.vector
            def _(v, ldw=ldw, tpd=tp_done):
                p.wait(v, "dl", ldw)
                p.wait(v, "pe", tpd)
                p.inc(v.tensor_copy(wbf[:, :], wstage[:, :]), "ve")
            cast_done = p.cnt["ve"]

            @blk.tensor
            def _(t, cd=cast_done, evp=ev_ptb):
                p.wait(t, "ve", cd)
                p.wait(t, "ve", evp)
                inst = None
                for dc in range(NCH):
                    inst = t.transpose(ptb[:, bts(dc, 128)], wbf[:, bts(dc, 128)],
                                       identb[:])
                p.inc(inst, "pe")
            tp_done = p.cnt["pe"]

            @blk.vector
            def _(v, n=n, oc=oc, tpd=tp_done):
                p.wait(v, "pe", tpd)
                p.inc(v.tensor_copy(wT[n][:, :, bts(oc, 128)],
                                    ptb[:, :].rearrange("p (c j) -> p c j", c=NCH)),
                      "ve")
            ev_ptb = p.cnt["ve"]
    prep_done = p.cnt["ve"]

    bank_ev = [p.cnt["ve"]] * 3
    slab_ln_ve = 0
    slab_ln_se = 0
    slab_tp = 0
    prev_pe = 0
    phiq_dma = 0
    for s in range(NSLAB):
        @blk.sync
        def _(sy, s=s, lv=slab_ln_ve, ls=slab_ln_se):
            p.wait(sy, "ve", lv)
            p.wait(sy, "se", ls)
            src = x2d[s * SLAB:(s + 1) * SLAB, :].rearrange("(a p) c -> p a c", p=128)
            p.inc(sy.dma_start(out=x_tm[:, :, :], in_=src), "dl", 16)
            p.wait(sy, "dl")
        ld_x = p.cnt["dl"]

        @blk.vector
        def _(v, ld_x=ld_x, tp=prev_pe):
            p.wait(v, "dl", ld_x)
            p.wait(v, "pe", tp)

        @blk.scalar
        def _(sc, ld_x=ld_x):
            p.wait(sc, "dl", ld_x)

        xs = [x_tm[:, i, :] for i in range(NTS)]
        h2s = [h2b[:, i, :] for i in range(NTS)]
        emit_ln_batch(p, blk, xs, h2s, st, st2, junk, wv=[("dl", ld_x)])
        slab_ln_ve = p.cnt["ve"]
        slab_ln_se = p.cnt["se"]
        zn_ready = p.cnt["ve"]

        for i in range(NTS):
            @blk.tensor
            def _(t, i=i, znr=zn_ready, evp=ev_ptb):
                p.wait(t, "ve", znr)
                p.wait(t, "ve", evp)
                inst = None
                for dc in range(NCH):
                    inst = t.transpose(ptb[:, bts(dc, 128)],
                                       h2b[:, i, bts(dc, 128)], identb[:])
                p.inc(inst, "pe")
            tpd = p.cnt["pe"]

            @blk.vector
            def _(v, i=i, tpd=tpd):
                p.wait(v, "pe", tpd)
                dst = h2T[:, :, bts(i, 128)]
                p.inc(v.tensor_copy(dst,
                                    ptb[:, :].rearrange("p (c j) -> p c j", c=NCH)),
                      "ve")
            ev_ptb = p.cnt["ve"]
        slab_tp = p.cnt["pe"]
        h2T_ready = p.cnt["ve"]

        for oc in range(NCH):
            bk = oc % 3

            @blk.tensor
            def _(t, oc=oc, bk=bk, hr=h2T_ready, pd=prep_done, bev=bank_ev[bk],
                  pq=phiq_dma):
                p.wait(t, "ve", max(hr, pd))
                p.wait(t, "ve", bev)
                if oc == 0:
                    p.wait(t, "dg", pq)
                for dc in range(NCH):
                    t.matmul(pbank[bk][:, :], wT["q"][:, dc, bts(oc, 128)],
                             h2T[:, dc, :], start=(dc == 0), stop=False)
                inst = t.matmul(pbank[bk][:, :], brow_q[0:1, bts(oc, 128)],
                                onesb[0:1, :], start=False, stop=True)
                p.inc(inst, "pe")
            mmd = p.cnt["pe"]

            @blk.vector
            def _(v, oc=oc, bk=bk, mmd=mmd):
                p.wait(v, "pe", mmd)
                p.wait(v, "se")
                p.wait(v, "ve")
                v.tensor_scalar(sA[:, :], pbank[bk][:, :], 0.0, None, ALU.min)
                p.inc(v.tensor_scalar(sB[:, :], pbank[bk][:, :], 0.0, None, ALU.max),
                      "ve")
                p.wait(v, "ve")

            @blk.scalar
            def _(sc, oc=oc):
                p.wait(sc, "ve")
                p.inc(sc.activation(sC[:, :], sA[:, :], AF.Exp), "se")

            @blk.vector
            def _(v, oc=oc, bk=bk):
                p.wait(v, "se")
                bank_ev[bk] = p.inc(
                    v.tensor_tensor(phiq[:, oc, :], sC[:, :], sB[:, :], ALU.add),
                    "ve")
        for which, dst in (("k", phik), ("v", vtm)):
            for i in range(NTS):
                for hf in range(2):
                    bk = (i * 2 + hf) % 3

                    @blk.tensor
                    def _(t, which=which, i=i, hf=hf, bk=bk, hr=h2T_ready,
                          bev=bank_ev[bk]):
                        p.wait(t, "ve", hr)
                        p.wait(t, "ve", bev)
                        for dc in range(NCH):
                            t.matmul(pbank[bk][:, :],
                                     h2T[:, dc, bts(i, 128)],
                                     wT[which][:, dc, bts(hf, 512)],
                                     start=(dc == 0), stop=False)
                        brt = brow_k if which == "k" else brow_v
                        inst = t.matmul(pbank[bk][:, :], onesb[0:1, 0:128],
                                        brt[0:1, bts(hf, 512)],
                                        start=False, stop=True)
                        p.inc(inst, "pe")
                    mmd = p.cnt["pe"]

                    if which == "k":
                        @blk.vector
                        def _(v, i=i, hf=hf, bk=bk, mmd=mmd):
                            p.wait(v, "pe", mmd)
                            p.wait(v, "se")
                            p.wait(v, "ve")
                            v.tensor_scalar(sA[:, :], pbank[bk][:, :], 0.0, None,
                                            ALU.min)
                            p.inc(v.tensor_scalar(sB[:, :], pbank[bk][:, :], 0.0,
                                                  None, ALU.max), "ve")
                            p.wait(v, "ve")

                        @blk.scalar
                        def _(sc):
                            p.wait(sc, "ve")
                            p.inc(sc.activation(sC[:, :], sA[:, :], AF.Exp), "se")

                        @blk.vector
                        def _(v, i=i, hf=hf, bk=bk):
                            p.wait(v, "se")
                            bank_ev[bk] = p.inc(
                                v.tensor_tensor(phik[:, i, bts(hf, 512)], sC[:, :],
                                                sB[:, :], ALU.add), "ve")
                    else:
                        @blk.vector
                        def _(v, i=i, hf=hf, bk=bk, mmd=mmd):
                            p.wait(v, "pe", mmd)
                            bank_ev[bk] = p.inc(
                                v.tensor_copy(vtm[:, i, bts(hf, 512)],
                                              pbank[bk][:, :]), "ve")
        kv_done = p.cnt["ve"]

        for i in range(NTS):
            @blk.vector
            def _(v, i=i, kd=kv_done):
                p.wait(v, "ve", kd)
                p.wait(v, "pe")
                p.inc(v.tensor_copy(vaug[:, :, 0:64],
                                    vtm[:, i, :].rearrange("p (h d) -> p h d", d=64)),
                      "ve")
            vg = p.cnt["ve"]

            @blk.tensor
            def _(t, s=s, i=i, vg=vg):
                p.wait(t, "ve", vg)
                inst = None
                for h in range(16):
                    inst = t.matmul(
                        kvzp[h // 4][0:64, (h % 4) * 128:(h % 4) * 128 + 65],
                        phik[:, i, h * 64:(h + 1) * 64], vaug[:, h, :],
                        start=(s == 0 and i == 0 and h % 4 == 0),
                        stop=(s == NSLAB - 1 and i == NTS - 1 and h % 4 == 3),
                        skip_group_check=True)
                p.inc(inst, "pe")

        prev_pe = p.cnt["pe"]

        @blk.gpsimd
        def _(g, s=s):
            p.wait(g, "ve", p.cnt["ve"])
            cur = bts(s, SLAB)
            p.inc(g.dma_start(
                out=o_phiq.rearrange("p (c t) -> p c t", c=NCH)[:, :, cur],
                in_=phiq[:, :, :]), "dg", 16)
        phiq_dma = p.cnt["dg"]

    kvz_pe = p.cnt["pe"]

    @blk.vector
    def _(v):
        p.wait(v, "pe", kvz_pe)
        inst = None
        for b in range(4):
            inst = v.tensor_copy(kvzs[:, 4 * b:4 * (b + 1), :],
                                 kvzp[b][:, :].rearrange("p (h x) -> p h x", h=4)
                                 [:, :, 0:65])
        p.inc(inst, "ve")

    @blk.gpsimd
    def _(g):
        p.wait(g, "ve")
        p.inc(g.dma_start(out=o_kvz, in_=kvzs[:, :, :].rearrange("p h x -> p (h x)")),
              "dg", 16)


def run_l7(inputs, r6):
    p = build_launch7()
    brow = np.ascontiguousarray(np.stack(
        [inputs["L1_attn_bq"], inputs["L1_attn_bk"],
         inputs["L1_attn_bv"]]).astype(np.float32))
    maps = []
    for c in range(NCORE):
        maps.append(dict(
            x2d=np.asarray(r6[c]["o_x2"]),
            wq=np.ascontiguousarray(inputs["L1_attn_Wq"]),
            wk=np.ascontiguousarray(inputs["L1_attn_Wk"]),
            wv=np.ascontiguousarray(inputs["L1_attn_Wv"]), brow=brow,
            wgs=np.ascontiguousarray(wg_full[c * 512:(c + 1) * 512]),
            wus=np.ascontiguousarray(wu_full[c * 512:(c + 1) * 512]),
            wds=np.ascontiguousarray(wd_full[c * 128:(c + 1) * 128]),
            wsums=wsums_all))
    return run_bass_kernel_spmd(p.nc, maps, core_ids=list(range(NCORE))).results



def emit_phaseWQ(p, blk, wgs, wus, wds, wsums, o_wgq, o_wuq, o_wdq, o_wmeans):
    identb = p.sb("identb", [128, 128], BF16)
    wstage = p.sb("wstage", [128, 4 * C])
    wbf = p.sb("wbf", [128, 4 * C], BF16)
    r1t = p.sb("r1t", [128, C]); r2t = p.sb("r2t", [128, C]); r3t = p.sb("r3t", [128, C])
    onesr = p.sb("onesr", [1, 128])
    onescol = p.sb("onescol", [128, 1])
    wsb = p.sb("wsb", [NCORE, 4])
    w3row = p.sb("w3row", [1, 4])
    w3rec = p.sb("w3rec", [1, 4])
    swcol = p.sb("swcol", [128, 4])
    stg = p.sb("stg", [128, NCH, 128], BF16)
    k0 = p.sb("k0", [128, 1]); k1 = p.sb("k1", [128, 1])
    kap = p.sb("kap", [128, 1]); lam = p.sb("lam", [128, 1])
    ptb = p.ps("ptb", [128, 1024], BF16)
    ptf = p.ps("ptf", [128, 512])

    emit_identity(p, blk, identb)

    @blk.gpsimd
    def _(g):
        p.inc(g.memset(onesr[:], 1.0), "po")
        p.inc(g.memset(onescol[:], 1.0), "po")

    @blk.sync
    def _(sy):
        p.inc(sy.dma_start(out=wsb[:, :], in_=wsums), "dl", 16)
        p.wait(sy, "dl")
    ld_ws = p.cnt["dl"]

    @blk.vector
    def _(v):
        p.wait(v, "po")
        p.inc(v.tensor_scalar(k0[:], onescol[:], 0.0, 0.5, ALU.mult, ALU.add), "ve")
        p.wait(v, "ve")
        p.inc(v.tensor_scalar(k1[:], k0[:], C23, -C23, ALU.add, ALU.add), "ve")
        p.wait(v, "ve")
        p.inc(v.tensor_scalar(kap[:], k1[:], 2.0, 0.0, ALU.mult, ALU.add), "ve")
        p.wait(v, "ve")
        p.inc(v.tensor_scalar(lam[:], kap[:], -1.0, 1.0, ALU.mult, ALU.add), "ve")
        p.wait(v, "ve")
    kap_ready = p.cnt["ve"]

    @blk.tensor
    def _(t, lw=ld_ws):
        p.wait(t, "dl", lw)
        p.wait(t, "po")
        p.inc(t.matmul(ptf[0:1, 0:4], onescol[0:NCORE, 0:1], wsb[:, :],
                       start=True, stop=True), "pe")
    smd = p.cnt["pe"]

    @blk.vector
    def _(v, smd=smd):
        p.wait(v, "pe", smd)
        p.inc(v.tensor_scalar(w3row[0:1, :], ptf[0:1, 0:4],
                              1.0 / float(4 * C * C), 1e-5, ALU.mult, ALU.max), "ve")
        p.wait(v, "ve")
        p.inc(v.reciprocal(w3rec[0:1, :], w3row[0:1, :]), "ve")
        p.wait(v, "ve")
    mrow_done = p.cnt["ve"]

    @blk.gpsimd
    def _(g, md=mrow_done):
        p.wait(g, "ve", md)
        p.inc(g.dma_start(out=o_wmeans, in_=w3row[0:1, :]), "dg", 16)
        p.wait(g, "dg")

    @blk.tensor
    def _(t):
        p.wait(t, "ve")
        p.inc(t.matmul(ptf[:, 4:8], onesr[0:1, :], w3rec[0:1, :],
                       start=True, stop=True), "pe")
    bcd = p.cnt["pe"]

    @blk.vector
    def _(v, bcd=bcd):
        p.wait(v, "pe", bcd)
        p.inc(v.tensor_copy(swcol[:, :], ptf[:, 4:8]), "ve")
        p.wait(v, "ve")
    sw_done = p.cnt["ve"]

    jobs = ([("g", wgs[bts(j, 128), :], C, 0, o_wgq, j) for j in range(4)]
            + [("u", wus[bts(j, 128), :], C, 1, o_wuq, j) for j in range(4)]
            + [("d", wds, 4 * C, 2, o_wdq, 0)])
    wstage2 = [wstage, p.sb("wstage_b", [128, 4 * C])]
    wbf2 = [wbf, p.sb("wbf_b", [128, 4 * C], BF16)]
    quant_done = [0, 0]
    tpq_done = [p.cnt["pe"], p.cnt["pe"]]
    ev_ptb = p.cnt["ve"]
    for jidx, (kind, src, wid, sci, dst, j) in enumerate(jobs):
        par = jidx % 2
        wst, wb = wstage2[par], wbf2[par]

        @blk.sync
        def _(sy, src=src, wid=wid, wst=wst, qd=quant_done[par]):
            p.wait(sy, "ve", qd)
            p.inc(sy.dma_start(out=wst[:, 0:wid], in_=src), "dl", 16)
            p.wait(sy, "dl")
        ldw = p.cnt["dl"]

        nq = wid // C
        for q in range(nq):
            @blk.vector
            def _(v, q=q, sci=sci, ldw=ldw, swd=sw_done, wst=wst,
                  tpd=tpq_done[par], pse=p.cnt["se"]):
                p.wait(v, "dl", ldw)
                p.wait(v, "ve", swd)
                p.wait(v, "pe", tpd)
                p.wait(v, "se", pse)
                p.inc(v.tensor_scalar(r1t[:, :], wst[:, bts(q, C)],
                                      swcol[:, sci:sci + 1], None, ALU.mult), "ve")

            @blk.scalar
            def _(sc, q=q):
                p.wait(sc, "ve")
                p.inc(sc.activation(r3t[:, :], r1t[:, :], AF.Sign), "se")

            @blk.vector
            def _(v, q=q, wb=wb):
                p.wait(v, "se")
                p.inc(v.tensor_tensor(r2t[:, :], r1t[:, :], r3t[:, :], ALU.mult),
                      "ve")
                p.wait(v, "ve")
                p.inc(v.tensor_scalar(r2t[:, :], r2t[:, :], -0.5, 1e38,
                                      ALU.add, ALU.mult), "ve")
                p.wait(v, "ve")
                p.inc(v.tensor_scalar(r2t[:, :], r2t[:, :], 0.0, 1.0,
                                      ALU.max, ALU.min), "ve")
                p.wait(v, "ve")
                p.inc(v.tensor_tensor(wb[:, bts(q, C)], r2t[:, :], r3t[:, :],
                                      ALU.mult), "ve")
                p.wait(v, "ve")
        quant_done[par] = p.cnt["ve"]
        cast_done = p.cnt["ve"]

        for grp in range(wid // 1024):
            @blk.tensor
            def _(t, grp=grp, cd=cast_done, evp=ev_ptb, wb=wb):
                p.wait(t, "ve", max(cd, evp))
                inst = None
                for jj in range(NCH):
                    ic = grp * NCH + jj
                    inst = t.transpose(ptb[:, bts(jj, 128)], wb[:, bts(ic, 128)],
                                       identb[:])
                p.inc(inst, "pe")
            tpd2 = p.cnt["pe"]

            @blk.vector
            def _(v, tpd2=tpd2, dmap=p.cnt["dg"]):
                p.wait(v, "pe", tpd2)
                p.wait(v, "dg", dmap)
                p.inc(v.tensor_copy(stg[:, :, :],
                                    ptb[:, :].rearrange("p (c j) -> p c j", c=NCH)),
                      "ve")
            cpd = p.cnt["ve"]

            @blk.gpsimd
            def _(g, kind=kind, dst=dst, j=j, grp=grp, cpd=cpd):
                p.wait(g, "ve", cpd)
                if kind == "d":
                    dd = (dst.rearrange("p (c t) -> p c t", c=4 * NCH)
                          [:, grp * NCH:(grp + 1) * NCH, :])
                else:
                    dd = (dst.rearrange("p (c t) -> p c t", c=NCH)
                          [:, :, bts(j, 128)])
                p.inc(g.dma_start(out=dd, in_=stg[:, :, :]), "dg", 16)
            ev_ptb = cpd
        tpq_done[par] = p.cnt["pe"]

    @blk.gpsimd
    def _(g):
        p.wait(g, "dg")


def build_launch567():
    p = P()
    z1d = p.din("z1d", [TL, C])
    z1p = p.din("z1p", [1, C])
    wkc = p.din("wkc", [4 * C, C])
    wrc = p.din("wrc", [C, C])
    prow5 = p.din("prow5", [4, C])
    sflag = p.din("sflag", [1, 1])
    xp = p.din("xp", [TL, C])
    wvc = p.din("wvc", [C, 4 * C])
    wff = p.din("wff", [C, C])
    bffr = p.din("bffr", [1, C])
    wgt = {n: p.din("w" + n, [C, C]) for n in "qkv"}
    brow = p.din("brow", [3, C])
    wgs = p.din("wgs", [4 * 128, C])
    wus = p.din("wus", [4 * 128, C])
    wds = p.din("wds", [128, 4 * C])
    wsums = p.din("wsums", [NCORE, 4])
    o_x2 = p.dout("o_x2", [TL, C])
    o_phiq = p.dout("o_phiq", [128, NCH * TL], BF16)
    o_kvz = p.dout("o_kvz", [64, 16 * 65])
    o_wgq = p.dout("o_wgq", [128, NCH * 512], BF16)
    o_wuq = p.dout("o_wuq", [128, NCH * 512], BF16)
    o_wdq = p.dout("o_wdq", [128, 4 * NCH * 128], BF16)
    o_wmeans = p.dout("o_wmeans", [1, 4])
    s_int = p.nc.dram_tensor("s_int", [128, 4 * NCH * TL], BF16).ap()
    rc_int = p.nc.dram_tensor("rc_int", [128, NCH * TL], BF16).ap()

    blk = p.stack.enter_context(p.nc.Block())
    p.prefix = "p5_"
    p.push()
    emit_phase5(p, blk, z1d, z1p, wkc, wrc, prow5, sflag, s_int, rc_int)
    p.pop()
    emit_barrier(p, blk)
    p.prefix = "p6_"
    p.push()
    emit_phase6(p, blk, z1d, xp, s_int, rc_int, wvc, wff, bffr, o_x2)
    p.pop()
    emit_barrier(p, blk)
    p.prefix = "p7_"
    p.push()
    emit_phase7(p, blk, o_x2, wgt, brow, o_phiq, o_kvz)
    p.pop()
    emit_barrier(p, blk)
    p.prefix = "wq_"
    p.push()
    emit_phaseWQ(p, blk, wgs, wus, wds, wsums, o_wgq, o_wuq, o_wdq, o_wmeans)
    p.pop()

    @blk.gpsimd
    def _(g):
        p.wait(g, "dg")
        p.wait(g, "dl")
        p.wait(g, "dp")

    return p


def run_l567(inputs, x_rows, r4, r2):
    p = build_launch567()
    wsums_all = np.concatenate(
        [np.asarray(r2[j]["o_wsums"]) for j in range(NCORE)], axis=0)
    wg_full = np.ascontiguousarray(inputs["L1_ffn_w_gate"].astype(np.float32))
    wu_full = np.ascontiguousarray(inputs["L1_ffn_w_up"].astype(np.float32))
    wd_full = np.ascontiguousarray(inputs["L1_ffn_w_down"].astype(np.float32))
    prow5 = np.ascontiguousarray(np.stack(
        [inputs["L0_attn_ln2_g"], inputs["L0_attn_ln2_b"],
         inputs["L0_attn_cmix_k"], inputs["L0_attn_cmix_r"]]).astype(np.float32))
    brow = np.ascontiguousarray(np.stack(
        [inputs["L1_attn_bq"], inputs["L1_attn_bk"],
         inputs["L1_attn_bv"]]).astype(np.float32))
    maps = []
    for c in range(NCORE):
        sf = np.array([[1.0 if _seq_start(c) else 0.0]], np.float32)
        z1prev = (np.asarray(r4[c - 1]["o_z1"])[TL - 1:TL]
                  if not _seq_start(c) else np.asarray(r4[c]["o_z1"])[0:1])
        maps.append(dict(
            z1d=np.asarray(r4[c]["o_z1"]), z1p=np.ascontiguousarray(z1prev),
            wkc=np.ascontiguousarray(inputs["L0_attn_Wk_c"]),
            wrc=np.ascontiguousarray(inputs["L0_attn_Wr_c"]),
            prow5=prow5, sflag=sf,
            xp=np.ascontiguousarray(x_rows[c * TL:(c + 1) * TL]),
            wvc=np.ascontiguousarray(inputs["L0_attn_Wv_c"]),
            wff=np.ascontiguousarray(inputs["L0_ffn_weight"]),
            bffr=np.ascontiguousarray(
                inputs["L0_ffn_bias"].astype(np.float32).reshape(1, C)),
            wq=np.ascontiguousarray(inputs["L1_attn_Wq"]),
            wk=np.ascontiguousarray(inputs["L1_attn_Wk"]),
            wv=np.ascontiguousarray(inputs["L1_attn_Wv"]), brow=brow,
            wgs=np.ascontiguousarray(wg_full[c * 512:(c + 1) * 512]),
            wus=np.ascontiguousarray(wu_full[c * 512:(c + 1) * 512]),
            wds=np.ascontiguousarray(wd_full[c * 128:(c + 1) * 128]),
            wsums=wsums_all))
    return run_bass_kernel_spmd(p.nc, maps, core_ids=list(range(NCORE))).results


def emit_phase8(p, blk, x2d, phiqd, kvz_all, bmask, wo, borow, o_x3):
    _po_guard = True

    identb = p.sb("identb", [128, 128], BF16)
    woT = p.sb("woT", [128, NCH, C], BF16)
    bob = p.sb("bob", [1, C], BF16)
    onesb = p.sb("onesb", [1, 512], BF16)
    kstack = p.sb("kstack", [128, NCORE, 1040])
    bmc = p.sb("bmc", [128, NCORE])
    macc = p.sb("macc", [128, 1040, 2])
    kvzb = p.sb("kvzb", [128, 16, 65], BF16)
    x2_tm = p.sb("x2_tm", [128, NTS, C])
    phiqs = p.sb("phiqs", [128, NCH, SLAB], BF16)
    attb = p.sb("attb", [128, NTS, C], BF16)
    attT = p.sb("attT", [128, NCH, SLAB], BF16)
    att64 = p.sb("att64", [128, 16, 64], BF16)
    normr = p.sb("normr", [128, 16])
    normq = p.sb("normq", [128, 16])

    pbank = [p.ps(f"pb{i}", [128, 512]) for i in range(4)]
    ptb = p.ps("ptb", [128, 1024], BF16)

    emit_identity(p, blk, identb)

    @blk.gpsimd
    def _(g):
        p.inc(g.memset(onesb[:], 1.0), "po")
        p.inc(g.dma_start(out=bob[0:1, :], in_=borow), "dp", 16)
        p.wait(g, "dp")
    ldp = p.cnt["dp"]

    @blk.tensor
    def _(t):
        p.wait(t, "po")

    @blk.sync
    def _(sy):
        src_ = kvz_all.rearrange("(j p) c -> p j c", p=64)
        p.inc(sy.dma_start(out=kstack[0:64, :, :], in_=src_), "dl", 16)
        p.inc(sy.dma_start(out=kstack[64:128, :, :], in_=src_), "dl", 16)
        p.inc(sy.dma_start(out=bmc[:, :], in_=bmask), "dl", 16)
        p.wait(sy, "dl")
    ld0 = p.cnt["dl"]

    @blk.vector
    def _(v):
        p.wait(v, "dl", ld0)
        p.inc(v.memset(macc[:, :, 0], 0.0), "ve")
        p.wait(v, "ve")
        for j in range(NCORE):
            a, b = j % 2, 1 - j % 2
            p.inc(v.scalar_tensor_tensor(macc[:, :, b], kstack[:, j, :],
                                         bmc[:, j:j + 1], macc[:, :, a],
                                         ALU.mult, ALU.add), "ve")
            p.wait(v, "ve")
        p.inc(v.tensor_copy(kvzb[:, :, :],
                            macc[:, :, NCORE % 2].rearrange("p (h x) -> p h x", h=16)),
              "ve")
    kvz_done = p.cnt["ve"]

    cast_done = 0
    tp_done = 0
    ev_ptb = p.cnt["ve"]
    for oc in range(NCH):
        @blk.sync
        def _(sy, oc=oc, cd=cast_done):
            p.wait(sy, "ve", cd)
            p.inc(sy.dma_start(out=x2_tm[:, 0, :], in_=wo[bts(oc, 128), :]), "dl", 16)
            p.wait(sy, "dl")
        ldw = p.cnt["dl"]

        @blk.vector
        def _(v, ldw=ldw, tpd=tp_done):
            p.wait(v, "dl", ldw)
            p.wait(v, "pe", tpd)
            p.inc(v.tensor_copy(attb[:, 0, :], x2_tm[:, 0, :]), "ve")
        cast_done = p.cnt["ve"]

        @blk.tensor
        def _(t, cd=cast_done, evp=ev_ptb):
            p.wait(t, "ve", cd)
            p.wait(t, "ve", evp)
            inst = None
            for dc in range(NCH):
                inst = t.transpose(ptb[:, bts(dc, 128)], attb[:, 0, bts(dc, 128)],
                                   identb[:])
            p.inc(inst, "pe")
        tp_done = p.cnt["pe"]

        @blk.vector
        def _(v, oc=oc, tpd=tp_done):
            p.wait(v, "pe", tpd)
            p.inc(v.tensor_copy(woT[:, :, bts(oc, 128)],
                                ptb[:, :].rearrange("p (c j) -> p c j", c=NCH)), "ve")
        ev_ptb = p.cnt["ve"]
    prep_done = p.cnt["ve"]

    bank_ev = [p.cnt["ve"]] * 4
    slab_end_ve = 0
    slab_dma = 0
    prev_pe = 0
    prep_se = p.cnt.get("se", 0)
    for s in range(NSLAB):
        @blk.sync
        def _(sy, s=s, sv=max(slab_end_ve, prep_done), dmap=slab_dma,
              pse=prep_se):
            p.wait(sy, "ve", sv)
            p.wait(sy, "se", pse)
            p.wait(sy, "dg", dmap)
            p.inc(sy.dma_start(
                out=x2_tm[:, :, :],
                in_=x2d[s * SLAB:(s + 1) * SLAB, :].rearrange("(a p) c -> p a c", p=128)),
                "dl", 16)
            p.inc(sy.dma_start(
                out=phiqs[:, :, :],
                in_=phiqd.rearrange("p (c t) -> p c t", c=NCH)[:, :, bts(s, SLAB)]),
                "dl", 16)
            p.wait(sy, "dl")
        ld_x = p.cnt["dl"]

        for i in range(NTS):
            for h in range(16):
                bk = (i * 16 + h) % 4

                @blk.tensor
                def _(t, i=i, h=h, bk=bk, ld_x=ld_x, kd=kvz_done, bev=bank_ev[bk]):
                    p.wait(t, "dl", ld_x)
                    p.wait(t, "ve", max(kd, bev))
                    base = 64 * (h % 2)
                    p.inc(t.matmul(pbank[bk][:, 0:65],
                                   phiqs[base:base + 64, h // 2, bts(i, 128)],
                                   kvzb[base:base + 64, h, :],
                                   start=True, stop=True), "pe")
                mmd = p.cnt["pe"]

                @blk.vector
                def _(v, i=i, h=h, bk=bk, mmd=mmd):
                    p.wait(v, "pe", mmd)
                    v.tensor_copy(att64[:, h, :], pbank[bk][:, 0:64])
                    bank_ev[bk] = p.inc(
                        v.tensor_copy(normr[:, h:h + 1], pbank[bk][:, 64:65]), "ve")

            @blk.vector
            def _(v, i=i):
                p.wait(v, "ve")
                p.inc(v.tensor_single_scalar(normq[:, :], normr[:, :],
                                             1e-6, ALU.add), "ve")
                p.wait(v, "ve")
                p.inc(v.reciprocal(normq[:, :], normq[:, :]), "ve")
                p.wait(v, "ve")
                inst = None
                for h in range(16):
                    inst = v.tensor_scalar(attb[:, i, h * 64:(h + 1) * 64],
                                           att64[:, h, :], normq[:, h:h + 1],
                                           None, ALU.mult)
                p.inc(inst, "ve")
                p.wait(v, "ve")
        att_done = p.cnt["ve"]

        for i in range(NTS):
            @blk.tensor
            def _(t, i=i, ad=att_done, evp=ev_ptb):
                p.wait(t, "ve", max(ad, evp))
                inst = None
                for dc in range(NCH):
                    inst = t.transpose(ptb[:, bts(dc, 128)],
                                       attb[:, i, bts(dc, 128)], identb[:])
                p.inc(inst, "pe")
            tpd = p.cnt["pe"]

            @blk.vector
            def _(v, i=i, tpd=tpd):
                p.wait(v, "pe", tpd)
                p.inc(v.tensor_copy(attT[:, :, bts(i, 128)],
                                    ptb[:, :].rearrange("p (c j) -> p c j", c=NCH)),
                      "ve")
            ev_ptb = p.cnt["ve"]
        attT_done = p.cnt["ve"]

        for i in range(NTS):
            for hf in range(2):
                bk = (i * 2 + hf) % 4

                @blk.tensor
                def _(t, i=i, hf=hf, bk=bk, ad=attT_done, pd=prep_done,
                      bev=bank_ev[bk], ldp=ldp):
                    p.wait(t, "ve", max(ad, pd, bev))
                    p.wait(t, "dp", ldp)
                    for dc in range(NCH):
                        t.matmul(pbank[bk][:, :], attT[:, dc, bts(i, 128)],
                                 woT[:, dc, bts(hf, 512)], start=(dc == 0),
                                 stop=False)
                    inst = t.matmul(pbank[bk][:, :], onesb[0:1, 0:128],
                                    bob[0:1, bts(hf, 512)], start=False, stop=True)
                    p.inc(inst, "pe")
                mmd = p.cnt["pe"]

                @blk.vector
                def _(v, i=i, hf=hf, bk=bk, mmd=mmd, ld_x=ld_x):
                    p.wait(v, "pe", mmd)
                    p.wait(v, "dl", ld_x)
                    bank_ev[bk] = p.inc(
                        v.tensor_tensor(x2_tm[:, i, bts(hf, 512)], pbank[bk][:, :],
                                        x2_tm[:, i, bts(hf, 512)], ALU.add), "ve")
        slab_end_ve = p.cnt["ve"]
        prev_pe = p.cnt["pe"]

        @blk.gpsimd
        def _(g, s=s, sv=slab_end_ve):
            p.wait(g, "ve", sv)
            dst = o_x3[s * SLAB:(s + 1) * SLAB, :].rearrange("(a p) c -> p a c", p=128)
            p.inc(g.dma_start(out=dst, in_=x2_tm[:, :, :]), "dg", 16)
        slab_dma = p.cnt["dg"]


def run_l8(inputs, r6, r7):
    p = build_launch8()
    kvz_all = np.concatenate([np.asarray(r7[j]["o_kvz"]) for j in range(NCORE)],
                             axis=0)
    maps = []
    for c in range(NCORE):
        bm = np.zeros((128, NCORE), np.float32)
        base = (c // (NCORE // B)) * (NCORE // B)
        bm[:, base:base + NCORE // B] = 1.0
        maps.append(dict(
            x2d=np.asarray(r6[c]["o_x2"]), phiqd=np.asarray(r7[c]["o_phiq"]),
            kvz_all=kvz_all, bmask=bm,
            wo=np.ascontiguousarray(inputs["L1_attn_Wo"]),
            borow=np.ascontiguousarray(
                inputs["L1_attn_bo"].astype(np.float32).reshape(1, C))))
    return run_bass_kernel_spmd(p.nc, maps, core_ids=list(range(NCORE))).results


def emit_phase9(p, blk, x3d, wgq_all, wuq_all, swcg, o_xq2, o_qsc):
    NIC = 4 * NCH
    identb = p.sb("identb", [128, 128], BF16)
    wgT = p.sb("wgT", [128, NCH, 4 * C], BF16)
    wuT = p.sb("wuT", [128, NCH, 4 * C], BF16)
    gbuf = p.sb("gbuf", [128, 4 * C])
    ubuf = p.sb("ubuf", [128, 4 * C], BF16)
    xqb = p.sb("xqb", [128, C], BF16)
    onesf = p.sb("onesf", [128, 1])
    k0 = p.sb("k0", [128, 1]); k1 = p.sb("k1", [128, 1])
    kap = p.sb("kap", [128, 1]); lam = p.sb("lam", [128, 1])
    r1t = p.sb("r1t", [128, C]); r2t = p.sb("r2t", [128, C]); r3t = p.sb("r3t", [128, C])
    x3_tm = p.sb("x3_tm", [128, C])
    h_tm = p.sb("h_tm", [128, C])
    xqTall = p.sb("xqTall", [128, NCH, TL], BF16)
    xq2b = ubuf
    st = p.sb("st", [128, 1, 4])
    st2 = p.sb("st2", [128, 1, 4])
    stq = p.sb("stq", [128, NT, 8])
    qsc = p.sb("qsc", [128, NT])

    pbank = [p.ps(f"pb{i}", [128, 512]) for i in range(4)]
    ptb = p.ps("ptb", [128, 1024], BF16)

    emit_identity(p, blk, identb)

    @blk.gpsimd
    def _(g):
        p.inc(g.memset(onesf[:], 1.0), "po")
        p.inc(g.dma_start(out=wgT[:, :, :].rearrange("p c t -> p (c t)"),
                          in_=wgq_all), "dg", 16)
        p.inc(g.dma_start(out=wuT[:, :, :].rearrange("p c t -> p (c t)"),
                          in_=wuq_all), "dg", 16)
    wT_ld = p.cnt["dg"]

    @blk.vector
    def _(v):
        p.wait(v, "po")
        p.inc(v.tensor_scalar(k0[:], onesf[:], 0.0, 0.5, ALU.mult, ALU.add), "ve")
        p.wait(v, "ve")
        p.inc(v.tensor_scalar(k1[:], k0[:], C23, -C23, ALU.add, ALU.add), "ve")
        p.wait(v, "ve")
        p.inc(v.tensor_scalar(kap[:], k1[:], 2.0, 0.0, ALU.mult, ALU.add), "ve")
        p.wait(v, "ve")
        p.inc(v.tensor_scalar(lam[:], kap[:], -1.0, 1.0, ALU.mult, ALU.add), "ve")
        p.wait(v, "ve")
    kap_ready = p.cnt["ve"]

    ev_ptb = p.cnt["ve"]
    prep_done = 0

    st9 = {"tile_x": 0, "ev_ptb": ev_ptb, "gu_free_ve": p.cnt["ve"],
           "tile_dma": 0, "a_tp": [0] * (NT + 1)}
    a_xqT = [0] * NT
    a_scale = [0] * NT
    bank_ev = [p.cnt["se"]] * 4
    bank_ev_v = [p.cnt["ve"]] * 4

    def emit_A(i):
        @blk.sync
        def _(sy, i=i, tx=st9["tile_x"]):
            p.wait(sy, "ve", tx)
            p.wait(sy, "se", p.cnt["se"])
            p.inc(sy.dma_start(out=x3_tm[:, :], in_=x3d[bts(i, 128), :]), "dl", 16)
            p.wait(sy, "dl")
        ld_x = p.cnt["dl"]

        @blk.vector
        def _(v, ld_x=ld_x):
            p.wait(v, "dl", ld_x)

        @blk.scalar
        def _(sc, ld_x=ld_x):
            p.wait(sc, "dl", ld_x)

        emit_ln_batch(p, blk, [x3_tm[:, :]], [h_tm[:, :]], st, st2, r1t,
                      wv=[("dl", ld_x), ("ve", p.cnt["ve"]), ("se", p.cnt["se"])])
        h_done = p.cnt["ve"]

        @blk.vector
        def _(v, i=i, hd=h_done):
            p.wait(v, "ve", hd)
            p.inc(v.reduce_max(stq[:, i, 2:3], h_tm[:, :], AX.X,
                               apply_absolute_value=True), "ve")
            p.wait(v, "ve")
            p.inc(v.tensor_scalar(stq[:, i, 3:4], stq[:, i, 2:3], 1e-5, None,
                                  ALU.max), "ve")
            p.wait(v, "ve")
            p.inc(v.tensor_scalar(stq[:, i, 6:7], stq[:, i, 3:4], swcg[:, 0:1],
                                  1.0 / 127.0, ALU.mult, ALU.mult), "ve")
            p.inc(v.tensor_scalar(stq[:, i, 7:8], stq[:, i, 3:4], swcg[:, 1:2],
                                  1.0 / 127.0, ALU.mult, ALU.mult), "ve")
        a_scale[i] = p.cnt["ve"]

        @blk.scalar
        def _(sc, i=i):
            p.wait(sc, "ve")
            p.inc(sc.activation(stq[:, i, 4:5], stq[:, i, 3:4], AF.Ln), "se")
            p.wait(sc, "se")
            p.inc(sc.activation(stq[:, i, 4:5], stq[:, i, 4:5], AF.Exp, scale=-1.0),
                  "se")

        @blk.vector
        def _(v, i=i, atp=st9["a_tp"][i]):
            p.wait(v, "se")
            p.wait(v, "pe", atp)
            p.inc(v.tensor_scalar(stq[:, i, 5:6], stq[:, i, 4:5], 127.0, None,
                                  ALU.mult), "ve")
            p.wait(v, "ve")
            p.inc(v.tensor_scalar(r1t[:, :], h_tm[:, :], stq[:, i, 5:6], None,
                                  ALU.mult), "ve")
            p.wait(v, "ve")
            v.tensor_scalar(r2t[:, :], r1t[:, :], lam[:, 0:1], C23, ALU.mult, ALU.add)
            p.inc(v.tensor_scalar(r3t[:, :], r1t[:, :], kap[:, 0:1], C52,
                                  ALU.mult, ALU.add), "ve")
            p.wait(v, "ve")
            p.inc(v.tensor_single_scalar(r2t[:, :], r2t[:, :], -C23, ALU.add), "ve")
            p.wait(v, "ve")
            p.inc(v.scalar_tensor_tensor(r1t[:, :], r3t[:, :], -C52, r2t[:, :],
                                         ALU.add, ALU.add), "ve")
            p.wait(v, "ve")
            p.inc(v.tensor_scalar(xqb[:, :], r1t[:, :], 127.0, -128.0,
                                  ALU.min, ALU.max), "ve")
            p.wait(v, "ve")
        xq_done = p.cnt["ve"]

        @blk.tensor
        def _(t, i=i, xd=xq_done, evp=st9["ev_ptb"]):
            p.wait(t, "ve", max(xd, evp))
            inst = None
            for dc in range(NCH):
                inst = t.transpose(ptb[:, bts(dc, 128)], xqb[:, bts(dc, 128)],
                                   identb[:])
            p.inc(inst, "pe")
        st9["a_tp"][i + 1] = p.cnt["pe"]
        tpd = p.cnt["pe"]

        @blk.vector
        def _(v, i=i, tpd=tpd):
            p.wait(v, "pe", tpd)
            p.inc(v.tensor_copy(xqTall[:, :, bts(i, 128)],
                                ptb[:, :].rearrange("p (c j) -> p c j", c=NCH)), "ve")
        st9["ev_ptb"] = p.cnt["ve"]
        a_xqT[i] = p.cnt["ve"]
        st9["tile_x"] = p.cnt["ve"]

    def emit_B(i):
        for wi, (wTd,) in enumerate(((wgT,), (wuT,))):
            for og in range(8):
                bk = (wi * 8 + og) % 4

                @blk.tensor
                def _(t, wTd=wTd, og=og, bk=bk, i=i, sa=a_xqT[i],
                      bes=bank_ev[bk], bev=bank_ev_v[bk]):
                    p.wait(t, "ve", sa)
                    if i == 0:
                        p.wait(t, "dg", wT_ld)
                    p.wait(t, "se", bes)
                    p.wait(t, "ve", bev)
                    inst = None
                    for dc in range(NCH):
                        inst = t.matmul(pbank[bk][:, :], xqTall[:, dc, bts(i, 128)],
                                        wTd[:, dc, bts(og, 512)], start=(dc == 0),
                                        stop=(dc == NCH - 1))
                    p.inc(inst, "pe")
                mmd = p.cnt["pe"]

                if wi == 0:
                    @blk.scalar
                    def _(sc, og=og, bk=bk, mmd=mmd, i=i, gf=st9["gu_free_ve"],
                          asc=a_scale[i]):
                        p.wait(sc, "pe", mmd)
                        p.wait(sc, "ve", max(gf, asc))
                        bank_ev[bk] = p.inc(
                            sc.activation(gbuf[:, bts(og, 512)], pbank[bk][:, :],
                                          AF.Silu, scale=stq[:, i, 6:7]), "se")
                else:
                    @blk.vector
                    def _(v, og=og, bk=bk, mmd=mmd, i=i, gf=st9["gu_free_ve"],
                          asc=a_scale[i]):
                        p.wait(v, "pe", mmd)
                        p.wait(v, "ve", max(gf, asc))
                        bank_ev_v[bk] = p.inc(
                            v.tensor_scalar(ubuf[:, bts(og, 512)], pbank[bk][:, :],
                                            stq[:, i, 7:8], None, ALU.mult), "ve")
        evac_done_se = p.cnt["se"]
        evac_done_ve = p.cnt["ve"]

        @blk.vector
        def _(v, i=i, es=evac_done_se, ev=evac_done_ve):
            p.wait(v, "se", es)
            p.wait(v, "ve", ev)
            p.inc(v.tensor_tensor(gbuf[:, :], gbuf[:, :], ubuf[:, :], ALU.mult),
                  "ve")
            p.wait(v, "ve")
            p.inc(v.reduce_max(stq[:, i, 2:3], gbuf[:, :], AX.X,
                               apply_absolute_value=True), "ve")
            p.wait(v, "ve")
            p.inc(v.tensor_scalar(stq[:, i, 3:4], stq[:, i, 2:3], 1e-5, None,
                                  ALU.max), "ve")
            p.wait(v, "ve")
            p.inc(v.tensor_scalar(qsc[:, i:i + 1], stq[:, i, 3:4], 1.0 / 127.0,
                                  None, ALU.mult), "ve")

        @blk.scalar
        def _(sc, i=i):
            p.wait(sc, "ve")
            p.inc(sc.activation(stq[:, i, 4:5], stq[:, i, 3:4], AF.Ln), "se")
            p.wait(sc, "se")
            p.inc(sc.activation(stq[:, i, 4:5], stq[:, i, 4:5], AF.Exp, scale=-1.0),
                  "se")
        sc2_done = p.cnt["se"]

        for hq in range(4):
            @blk.vector
            def _(v, i=i, hq=hq, s2=sc2_done):
                p.wait(v, "se", s2)
                p.inc(v.tensor_scalar(r1t[:, :], gbuf[:, bts(hq, C)],
                                      stq[:, i, 4:5], 127.0, ALU.mult, ALU.mult),
                      "ve")
                p.wait(v, "ve")
                v.tensor_scalar(r2t[:, :], r1t[:, :], lam[:, 0:1], C23,
                                ALU.mult, ALU.add)
                p.inc(v.tensor_scalar(r3t[:, :], r1t[:, :], kap[:, 0:1], C52,
                                      ALU.mult, ALU.add), "ve")
                p.wait(v, "ve")
                p.inc(v.tensor_single_scalar(r2t[:, :], r2t[:, :], -C23, ALU.add),
                      "ve")
                p.wait(v, "ve")
                p.inc(v.scalar_tensor_tensor(r1t[:, :], r3t[:, :], -C52, r2t[:, :],
                                             ALU.add, ALU.add), "ve")
                p.wait(v, "ve")
                p.inc(v.tensor_scalar(xq2b[:, bts(hq, C)], r1t[:, :], 127.0, -128.0,
                                      ALU.min, ALU.max), "ve")
                p.wait(v, "ve")
        st9["gu_free_ve"] = p.cnt["ve"]
        xq2_done = p.cnt["ve"]

        for grp in range(4):
            @blk.tensor
            def _(t, i=i, grp=grp, xd=xq2_done, evp=st9["ev_ptb"]):
                p.wait(t, "ve", max(xd, evp))
                inst = None
                for j in range(NCH):
                    ic = grp * NCH + j
                    inst = t.transpose(ptb[:, bts(j, 128)], xq2b[:, bts(ic, 128)],
                                       identb[:])
                p.inc(inst, "pe")
            tpd = p.cnt["pe"]

            @blk.vector
            def _(v, i=i, grp=grp, tpd=tpd, dmap=p.cnt["dg"]):
                p.wait(v, "pe", tpd)
                p.wait(v, "dg", dmap)
                p.inc(v.tensor_copy(xqTall[:, :, bts(i, 128)],
                                    ptb[:, :].rearrange("p (c j) -> p c j", c=NCH)),
                      "ve")
            cpd = p.cnt["ve"]

            @blk.gpsimd
            def _(g, i=i, grp=grp, cpd=cpd):
                p.wait(g, "ve", cpd)
                dst = (o_xq2.rearrange("p (c t) -> p c t", c=4 * NCH)
                       [:, grp * NCH:(grp + 1) * NCH, bts(i, 128)])
                p.inc(g.dma_start(out=dst, in_=xqTall[:, :, bts(i, 128)]),
                      "dg", 16)
            st9["ev_ptb"] = cpd

    emit_A(0)
    for i in range(NT):
        if i + 1 < NT:
            emit_A(i + 1)
        emit_B(i)

    @blk.gpsimd
    def _(g):
        p.wait(g, "ve")
        p.inc(g.dma_start(out=o_qsc, in_=qsc[:, :]), "dg", 16)


def run_l9(inputs, r8):
    p = build_launch9()
    maps = []
    for c in range(NCORE):
        maps.append(dict(
            x3d=np.asarray(r8[c]["o_x3"]),
            wg=np.ascontiguousarray(inputs["L1_ffn_w_gate"]),
            wu=np.ascontiguousarray(inputs["L1_ffn_w_up"])))
    return run_bass_kernel_spmd(p.nc, maps, core_ids=list(range(NCORE))).results


def emit_phase10(p, blk, x3d, xq2d, qscd, wdq_all, swcg, o_out):
    NIC = 4 * NCH

    wdT = p.sb("wdT", [128, NIC, C], BF16)
    qsc = p.sb("qsc", [128, NT])
    dsc = p.sb("dsc", [128, NT])
    x3_tm = p.sb("x3_tm", [128, C])
    xq2T = p.sb("xq2T", [128, NIC, 128], BF16)
    y_tm = p.sb("y_tm", [128, C])
    pbank = [p.ps(f"pb{i}", [128, 512]) for i in range(4)]

    @blk.gpsimd
    def _(g):
        p.inc(g.dma_start(out=wdT[:, :, :].rearrange("p c t -> p (c t)"),
                          in_=wdq_all), "dg", 16)
    wT_ld = p.cnt["dg"]

    @blk.sync
    def _(sy):
        p.inc(sy.dma_start(out=qsc[:, :], in_=qscd), "dl", 16)
        p.wait(sy, "dl")
    ldq = p.cnt["dl"]

    @blk.vector
    def _(v, ldq=ldq):
        p.wait(v, "dl", ldq)
        p.inc(v.tensor_scalar(dsc[:, :], qsc[:, :], swcg[:, 2:3], None, ALU.mult),
              "ve")
        p.wait(v, "ve")
    prep_done = p.cnt["ve"]

    @blk.tensor
    def _(t, wl=wT_ld):
        p.wait(t, "dg", wl)

    bank_ev = [p.cnt["ve"]] * 4
    tile_end = 0
    tile_dma = 0
    for i in range(NT):
        @blk.sync
        def _(sy, i=i, sv=tile_end, dmap=tile_dma):
            p.wait(sy, "ve", sv)
            p.wait(sy, "dg", dmap)
            p.inc(sy.dma_start(out=x3_tm[:, :], in_=x3d[bts(i, 128), :]), "dl", 16)
            p.inc(sy.dma_start(
                out=xq2T[:, :, :],
                in_=xq2d.rearrange("p (c t) -> p c t", c=NIC)[:, :, bts(i, 128)]),
                "dl", 16)
            p.wait(sy, "dl")
        ld_x = p.cnt["dl"]

        for hf in range(2):
            bk = hf % 4

            @blk.tensor
            def _(t, i=i, hf=hf, bk=bk, ld_x=ld_x, pd=prep_done, bev=bank_ev[bk]):
                p.wait(t, "dl", ld_x)
                p.wait(t, "ve", max(pd, bev))
                inst = None
                for ic in range(NIC):
                    inst = t.matmul(pbank[bk][:, :], xq2T[:, ic, :],
                                    wdT[:, ic, bts(hf, 512)], start=(ic == 0),
                                    stop=(ic == NIC - 1))
                p.inc(inst, "pe")
            mmd = p.cnt["pe"]

            @blk.vector
            def _(v, i=i, hf=hf, bk=bk, mmd=mmd, ld_x=ld_x):
                p.wait(v, "pe", mmd)
                p.wait(v, "dl", ld_x)
                p.inc(v.tensor_scalar(y_tm[:, bts(hf, 512)], pbank[bk][:, :],
                                      dsc[:, i:i + 1], None, ALU.mult), "ve")
                p.wait(v, "ve")
                bank_ev[bk] = p.inc(
                    v.tensor_tensor(y_tm[:, bts(hf, 512)], y_tm[:, bts(hf, 512)],
                                    x3_tm[:, bts(hf, 512)], ALU.add), "ve")
        tile_end = p.cnt["ve"]

        @blk.gpsimd
        def _(g, i=i, sv=tile_end):
            p.wait(g, "ve", sv)
            p.inc(g.dma_start(out=o_out[bts(i, 128), :], in_=y_tm[:, :]), "dg", 16)
        tile_dma = p.cnt["dg"]


def run_l10(inputs, r8, r9):
    p = build_launch10()
    maps = []
    for c in range(NCORE):
        maps.append(dict(
            x3d=np.asarray(r8[c]["o_x3"]), xq2d=np.asarray(r9[c]["o_xq2"]),
            qscd=np.asarray(r9[c]["o_qsc"]),
            wd=np.ascontiguousarray(inputs["L1_ffn_w_down"])))
    return run_bass_kernel_spmd(p.nc, maps, core_ids=list(range(NCORE))).results



def build_launch8910():
    NIC = 4 * NCH
    p = P()
    x2d = p.din("x2d", [TL, C])
    phiqd = p.din("phiqd", [128, NCH * TL], BF16)
    kvz_all = p.din("kvz_all", [NCORE * 64, 16 * 65])
    bmask = p.din("bmask", [128, NCORE])
    wo = p.din("wo", [C, C])
    borow = p.din("borow", [1, C])
    wgq_all = p.din("wgq_all", [128, NCH * 4 * C], BF16)
    wuq_all = p.din("wuq_all", [128, NCH * 4 * C], BF16)
    wdq_all = p.din("wdq_all", [128, NIC * C], BF16)
    wmeans = p.din("wmeans", [1, 4])
    o_out = p.dout("o_out", [TL, C])
    x3_int = p.nc.dram_tensor("x3_int", [TL, C], F32).ap()
    xq2_int = p.nc.dram_tensor("xq2_int", [128, NIC * TL], BF16).ap()
    qsc_int = p.nc.dram_tensor("qsc_int", [128, NT], F32).ap()

    blk = p.stack.enter_context(p.nc.Block())

    wmrow = p.sb("wmrow", [1, 4])
    swcg = p.sb("swcg", [128, 4])
    onesr1 = p.sb("onesr1", [1, 128])
    ptw = p.ps("ptw", [128, 512])

    @blk.gpsimd
    def _(g):
        p.inc(g.memset(onesr1[:], 1.0), "po")

    @blk.sync
    def _(sy):
        p.inc(sy.dma_start(out=wmrow[:, :], in_=wmeans), "dl", 16)
        p.wait(sy, "dl")
    ld_wm = p.cnt["dl"]

    @blk.tensor
    def _(t, lw=ld_wm):
        p.wait(t, "dl", lw)
        p.wait(t, "po")
        p.inc(t.matmul(ptw[:, 0:4], onesr1[0:1, :], wmrow[0:1, :],
                       start=True, stop=True), "pe")
    bcw = p.cnt["pe"]

    @blk.vector
    def _(v, bcw=bcw):
        p.wait(v, "pe", bcw)
        p.inc(v.tensor_copy(swcg[:, :], ptw[:, 0:4]), "ve")
        p.wait(v, "ve")

    p.prefix = "p8_"
    p.push()
    emit_phase8(p, blk, x2d, phiqd, kvz_all, bmask, wo, borow, x3_int)
    p.pop()
    emit_barrier(p, blk)
    p.prefix = "p9_"
    p.push()
    emit_phase9(p, blk, x3_int, wgq_all, wuq_all, swcg, xq2_int, qsc_int)
    p.pop()
    emit_barrier(p, blk)
    p.prefix = "pA_"
    p.push()
    emit_phase10(p, blk, x3_int, xq2_int, qsc_int, wdq_all, swcg, o_out)
    p.pop()

    @blk.gpsimd
    def _(g):
        p.wait(g, "dg")
        p.wait(g, "dl")
        p.wait(g, "dp")

    return p


def run_l8910(inputs, r567):
    p = build_launch8910()
    kvz_all = np.concatenate([np.asarray(r567[j]["o_kvz"]) for j in range(NCORE)],
                             axis=0)
    NIC = 4 * NCH
    wgq_all = np.concatenate(
        [np.asarray(r567[j]["o_wgq"]).reshape(128, NCH, 512) for j in range(NCORE)],
        axis=2).reshape(128, NCH * 4 * C)
    wuq_all = np.concatenate(
        [np.asarray(r567[j]["o_wuq"]).reshape(128, NCH, 512) for j in range(NCORE)],
        axis=2).reshape(128, NCH * 4 * C)
    wdq_all = np.concatenate(
        [np.asarray(r567[j]["o_wdq"]).reshape(128, NIC, 128) for j in range(NCORE)],
        axis=2).reshape(128, NIC * C)
    wmeans = np.ascontiguousarray(np.asarray(r567[0]["o_wmeans"]))
    maps = []
    for c in range(NCORE):
        bm = np.zeros((128, NCORE), np.float32)
        base = (c // (NCORE // B)) * (NCORE // B)
        bm[:, base:base + NCORE // B] = 1.0
        maps.append(dict(
            x2d=np.asarray(r567[c]["o_x2"]), phiqd=np.asarray(r567[c]["o_phiq"]),
            kvz_all=kvz_all, bmask=bm,
            wo=np.ascontiguousarray(inputs["L1_attn_Wo"]),
            borow=np.ascontiguousarray(
                inputs["L1_attn_bo"].astype(np.float32).reshape(1, C)),
            wgq_all=wgq_all, wuq_all=wuq_all, wdq_all=wdq_all, wmeans=wmeans))
    return run_bass_kernel_spmd(p.nc, maps, core_ids=list(range(NCORE))).results


def kernel(**inputs):
    x = np.asarray(inputs["L0_attn_x"], np.float32)
    _ = inputs["L0_attn_time_decay"]
    rows = np.ascontiguousarray(x.reshape(B * T, C))
    r1 = run_l1(inputs, rows)
    r2 = run_l2(inputs, r1)
    r3 = run_l3(r1, r2)
    r4 = run_l4(inputs, rows, r1, r2, r3)
    r567 = run_l567(inputs, rows, r4, r2)
    r8910 = run_l8910(inputs, r567)
    out = np.concatenate([np.asarray(r8910[c]["o_out"]) for c in range(NCORE)],
                         axis=0)
    return np.ascontiguousarray(out.reshape(B, T, C).astype(np.float32))



# version 15 (final)
# speedup vs baseline: 1.0250x; 1.0250x over previous
import numpy as np
import ml_dtypes
import concourse.bass as bass
import concourse.mybir as mybir
from concourse.bass_utils import run_bass_kernel_spmd
from contextlib import ExitStack

F32 = mybir.dt.float32
BF16 = mybir.dt.bfloat16
AF = mybir.ActivationFunctionType
ALU = mybir.AluOpType
AX = mybir.AxisListType
bts = bass.ts

NCORE = 8
B, T, C = 2, 8192, 1024
TL = (B * T) // NCORE
NT = TL // 128
NCH = C // 128
SLAB = 512
NSLAB = TL // SLAB
NTS = SLAB // 128
EPS_LN = 1e-5
EPS_WKV = 1e-8
C23 = float(3 * 2**22)
C52 = float(3 * 2**51)
RC = 1.0 / float(C)
BF = ml_dtypes.bfloat16


class P:

    def __init__(self):
        self.nc = bass.Bass("TRN2", target_bir_lowering=False)
        self.stack = ExitStack()
        self.cur = self.stack
        self.sems = {}
        self.cnt = {}

    def push(self):
        self.cur = ExitStack()
        return self.cur

    def pop(self):
        self.cur.close()
        self.cur = self.stack

    def sem(self, name):
        if name not in self.sems:
            self.sems[name] = self.stack.enter_context(self.nc.semaphore(name))
            self.cnt[name] = 0
        return self.sems[name]

    def inc(self, inst, name, k=1):
        inst.then_inc(self.sem(name), k)
        self.cnt[name] += k
        return self.cnt[name]

    def wait(self, eng, name, v=None):
        eng.wait_ge(self.sem(name), self.cnt[name] if v is None else v)

    def sb(self, name, shape, dt=F32):
        return self.cur.enter_context(
            self.nc.sbuf_tensor(getattr(self, "prefix", "") + name, shape, dt))

    def ps(self, name, shape, dt=F32):
        return self.cur.enter_context(
            self.nc.psum_tensor(getattr(self, "prefix", "") + name, shape, dt))

    def din(self, name, shape, dt=F32):
        return self.nc.dram_tensor(name, shape, dt, kind="ExternalInput").ap()

    def dout(self, name, shape, dt=F32):
        return self.nc.dram_tensor(name, shape, dt, kind="ExternalOutput").ap()


def emit_barrier(p, blk):
    sems = [s for s in ("pe", "ve", "se", "po", "dl", "dg", "dp") if s in p.cnt]

    @blk.tensor
    def _(t):
        for s in sems:
            p.wait(t, s)

    @blk.vector
    def _(v):
        for s in sems:
            p.wait(v, s)

    @blk.scalar
    def _(sc):
        for s in sems:
            p.wait(sc, s)

    @blk.gpsimd
    def _(g):
        for s in sems:
            p.wait(g, s)

    @blk.sync
    def _(sy):
        for s in sems:
            p.wait(sy, s)


def emit_identity(p, blk, ident):
    @blk.gpsimd
    def _(g):
        p.inc(g.memset(ident[:], 0.0), "po")
        p.wait(g, "po")
        p.inc(g.affine_select(out=ident[:], in_=ident[:], compare_op=ALU.not_equal,
                              fill=1.0, base=0, pattern=[[-1, 128]],
                              channel_multiplier=1), "po")


def emit_ln_batch(p, blk, xs, outs, st, st2, junk, wv=()):
    n = len(xs)

    @blk.vector
    def _(v):
        for nm, val in wv:
            p.wait(v, nm, val)
        inst = None
        for i in range(n):
            inst = v.reduce_sum(st[:xs[i].shape[0], i, 0:1], xs[i], AX.X)
        p.inc(inst, "ve")

    @blk.scalar
    def _(s):
        for nm, val in wv:
            p.wait(s, nm, val)
        for i in range(n):
            p.inc(s.activation(junk[:xs[i].shape[0], 0:xs[i].shape[1]], xs[i], AF.Square,
                               accum_out=st[:xs[i].shape[0], i, 1:2]), "se")
            p.wait(s, "se")

    pr = max(x.shape[0] for x in xs)

    @blk.vector
    def _(v):
        p.wait(v, "ve")
        p.wait(v, "se")
        p.inc(v.tensor_scalar(st[:pr, 0:n, 2:3], st[:pr, 0:n, 0:1], -RC, None,
                              ALU.mult), "ve")
        p.wait(v, "ve")
        p.inc(v.tensor_tensor(st2[:pr, 0:n, 0:1], st[:pr, 0:n, 2:3],
                              st[:pr, 0:n, 2:3], ALU.mult), "ve")
        p.wait(v, "ve")
        p.inc(v.scalar_tensor_tensor(st2[:pr, 0:n, 1:2], st[:pr, 0:n, 1:2], RC,
                                     st2[:pr, 0:n, 0:1], ALU.mult, ALU.subtract),
              "ve")

    @blk.scalar
    def _(s):
        p.wait(s, "ve")
        p.inc(s.activation(st2[:pr, 0:n, 2:3], st2[:pr, 0:n, 1:2], AF.Ln,
                           bias=np.float32(EPS_LN)), "se")
        p.wait(s, "se")
        p.inc(s.activation(st[:pr, 0:n, 3:4], st2[:pr, 0:n, 2:3], AF.Exp,
                           scale=-0.5), "se")

    @blk.vector
    def _(v):
        p.wait(v, "se")
        inst = None
        for i in range(n):
            inst = v.tensor_scalar(outs[i], xs[i], st[:xs[i].shape[0], i, 2:3],
                                   st[:xs[i].shape[0], i, 3:4], ALU.add, ALU.mult)
        p.inc(inst, "ve")


def build_launch1():
    p = P()
    xp = p.din("xp", [TL + 1, C])
    wts = {n: p.din("w" + n, [C, C]) for n in "kvr"}
    prow = p.din("prow", [5, C])
    sflag = p.din("sflag", [1, 1])
    o_k = p.dout("o_k", [128, NCH * TL])
    o_v = p.dout("o_v", [128, NCH * TL])
    o_r = p.dout("o_r", [128, NCH * TL], BF16)
    o_ta = p.dout("o_ta", [128, NCH * 2])

    identb = p.sb("identb", [128, 128], BF16)
    identf = p.sb("identf", [128, 128], F32)
    wstage = p.sb("wstage", [128, C])
    wbf = p.sb("wbf", [128, C], BF16)
    wT = {n: p.sb("wT_" + n, [128, NCH, C], BF16) for n in "kvr"}
    cks = p.sb("cks", [128, NCH, 3])
    pcol = p.sb("pcol", [128, NCH, 8])
    pstage = p.sb("pstage", [128, NCH * 128])
    b1cb = p.sb("b1cb", [128, NCH], BF16)
    halo2 = p.sb("halo2", [1, C])
    zn_h = p.sb("zn_h", [1, C])
    gr = p.sb("gr", [1, C])
    nbg = p.sb("nbg", [1, C])
    g1row = p.sb("g1row", [1, C])
    b1row = p.sb("b1row", [1, C])
    sfc = p.sb("sfc", [128, 2])
    ones1 = p.sb("ones1", [1, 128])
    znf = p.sb("znf", [128, NCH, 1 + SLAB], BF16)
    colt = p.sb("colt", [128, NCH])
    x_tm = p.sb("x_tm", [128, NTS, C])
    zn_tm = p.sb("zn_tm", [128, NTS, C], BF16)
    st = p.sb("st", [128, NTS, 4])
    st2 = p.sb("st2", [128, NTS, 4])
    junk = p.sb("junk", [128, C])
    dts = p.sb("dts", [128, NCH, SLAB], BF16)
    tsv = {n: p.sb("tsv_" + n, [128, NCH, SLAB], BF16) for n in "kvr"}
    kout = p.sb("kout", [128, NCH, SLAB])
    vout = p.sb("vout", [128, NCH, SLAB])
    rout = p.sb("rout", [128, NCH, SLAB], BF16)
    sksum = p.sb("sksum", [128, NCH])
    carry = p.sb("carry", [128, NCH, 1])
    ta = p.sb("ta", [128, NCH, 2])

    pbank = [p.ps(f"pb{i}", [128, 512]) for i in range(4)]
    ptb = p.ps("ptb", [128, 1024], BF16)
    ptf = p.ps("ptf", [128, 512], F32)

    blk = p.stack.enter_context(p.nc.Block())
    emit_identity(p, blk, identb)
    emit_identity(p, blk, identf)

    @blk.sync
    def _(sy):
        p.inc(sy.dma_start(out=zn_h[:], in_=xp[0:1, :]), "dl", 16)
        p.inc(sy.dma_start(out=sfc[0:1, 0:1], in_=sflag), "dl", 16)
        p.inc(sy.dma_start(out=g1row[:], in_=prow[0:1, :]), "dl", 16)
        p.inc(sy.dma_start(out=b1row[:], in_=prow[1:2, :]), "dl", 16)
        p.wait(sy, "dl")

    @blk.gpsimd
    def _(g):
        p.inc(g.memset(pstage[:, :], 0.0), "po")
        p.inc(g.memset(ones1[:], 1.0), "po")

    @blk.sync
    def _(sy):
        p.wait(sy, "po")
        p.inc(sy.dma_start(out=pstage[0:5, :], in_=prow[:, :]), "dl", 16)
        p.wait(sy, "dl")
    ld_par = p.cnt["dl"]

    @blk.tensor
    def _(t):
        p.wait(t, "dl", ld_par)
        p.wait(t, "po")
        p.inc(t.matmul(ptf[:, 0:1], ones1[:], sfc[0:1, 0:1], start=True, stop=True),
              "pe")

    @blk.vector
    def _(v):
        p.wait(v, "pe")
        p.inc(v.tensor_copy(sfc[:, 0:1], ptf[:, 0:1]), "ve")
        p.wait(v, "ve")
        p.inc(v.tensor_scalar(sfc[:, 1:2], sfc[:, 0:1], -1.0, 1.0, ALU.mult, ALU.add),
              "ve")

    @blk.vector
    def _(v):
        p.wait(v, "dl", ld_par)
        p.inc(v.reciprocal(gr[:], g1row[:]), "ve")
        p.wait(v, "ve")
        p.inc(v.scalar_tensor_tensor(nbg[:], b1row[:], -1.0, gr[:],
                                     ALU.mult, ALU.mult), "ve")
        p.wait(v, "ve")
        p.wait(v, "po")
        p.inc(v.tensor_copy(pstage[32:33, :], nbg[:]), "ve")

    emit_ln_batch(p, blk, [zn_h[:, :]], [halo2[:, :]], st, st2, junk,
                  wv=[("dl", ld_par)])
    emit_ln_batch(p, blk, [halo2[:, :]], [zn_h[:, :]], st, st2, junk,
                  wv=[("ve", p.cnt["ve"])])

    @blk.vector
    def _(v):
        p.wait(v, "ve")
        p.wait(v, "po")
        p.inc(v.tensor_copy(pstage[64:65, :], zn_h[:, :]), "ve")
    pstage_ready = p.cnt["ve"]

    ev_ptf = [p.cnt["ve"], p.cnt["ve"]]
    for dc in range(NCH):
        @blk.tensor
        def _(t, dc=dc, evf=None):
            p.wait(t, "ve", pstage_ready)
            p.wait(t, "ve", ev_ptf[0])
            p.inc(t.transpose(ptf[:, 0:128], pstage[:, bts(dc, 128)],
                              identf[:]), "pe")
        tpd = p.cnt["pe"]

        @blk.vector
        def _(v, dc=dc, tpd=tpd):
            p.wait(v, "pe", tpd)
            v.tensor_copy(pcol[:, dc, 0:5], ptf[:, 0:5])
            v.tensor_copy(pcol[:, dc, 5:6], ptf[:, 32:33])
            ev_ptf[0] = p.inc(
                v.tensor_copy(pcol[:, dc, 6:7], ptf[:, 64:65]), "ve")
    ev_ptf[1] = ev_ptf[0]

    @blk.vector
    def _(v):
        p.wait(v, "ve")
        inst = None
        for dc in range(NCH):
            inst = v.tensor_tensor(colt[:, dc:dc + 1], pcol[:, dc, 6:7],
                                   pcol[:, dc, 5:6], ALU.subtract)
        p.inc(inst, "ve")
        p.wait(v, "ve")
        for dc in range(NCH):
            inst = v.scalar_tensor_tensor(znf[:, dc, 0:1], colt[:, dc:dc + 1],
                                          sfc[:, 1:2], pcol[:, dc, 5:6],
                                          ALU.mult, ALU.add)
        p.inc(inst, "ve")
        p.wait(v, "ve")
        p.inc(v.tensor_copy(b1cb[:, :], pcol[:, :, 1]), "ve")
    b1_ready = p.cnt["ve"]

    cast_done = 0
    tp_done = 0
    ev_ptb = p.cnt["ve"]
    for wi, n in enumerate("kvr"):
        for oc in range(NCH):
            @blk.sync
            def _(sy, n=n, oc=oc, cd=cast_done):
                p.wait(sy, "ve", cd)
                p.inc(sy.dma_start(out=wstage[:, :], in_=wts[n][bts(oc, 128), :]),
                      "dl", 16)
                p.wait(sy, "dl")
            ldw = p.cnt["dl"]

            @blk.vector
            def _(v, ldw=ldw, tpd=tp_done):
                p.wait(v, "dl", ldw)
                p.wait(v, "pe", tpd)
                p.inc(v.tensor_copy(wbf[:, :], wstage[:, :]), "ve")
            cast_done = p.cnt["ve"]

            @blk.tensor
            def _(t, cd=cast_done, evp=ev_ptb):
                p.wait(t, "ve", cd)
                p.wait(t, "ve", evp)
                inst = None
                for dc in range(NCH):
                    inst = t.transpose(ptb[:, bts(dc, 128)], wbf[:, bts(dc, 128)],
                                       identb[:])
                p.inc(inst, "pe")
            tp_done = p.cnt["pe"]

            @blk.vector
            def _(v, n=n, oc=oc, tpd=tp_done):
                p.wait(v, "pe", tpd)
                p.inc(v.tensor_copy(wT[n][:, :, bts(oc, 128)],
                                    ptb[:, :].rearrange("p (c j) -> p c j", c=NCH)),
                      "ve")
            ev_ptb = p.cnt["ve"]
    wT_ready = p.cnt["ve"]

    @blk.tensor
    def _(t):
        p.wait(t, "ve", max(wT_ready, b1_ready))
        p.wait(t, "ve", max(ev_ptf))
        inst = None
        for wi, n in enumerate("kvr"):
            for oc in range(NCH):
                for dc in range(NCH):
                    inst = t.matmul(ptf[:, oc * 3 + wi: oc * 3 + wi + 1],
                                    wT[n][:, dc, bts(oc, 128)], b1cb[:, dc:dc + 1],
                                    start=(dc == 0), stop=(dc == NCH - 1))
        p.inc(inst, "pe")
    cvec_done = p.cnt["pe"]

    @blk.vector
    def _(v):
        p.wait(v, "pe", cvec_done)
        p.inc(v.tensor_copy(cks[:, :, :],
                            ptf[:, 0:24].rearrange("p (o j) -> p o j", j=3)), "ve")
        p.wait(v, "ve")
        inst = None
        for n in "kvr":
            for dc in range(NCH):
                inst = v.tensor_scalar(wT[n][:, dc, :], wT[n][:, dc, :],
                                       pcol[:, dc, 0:1], None, ALU.mult)
        p.inc(inst, "ve")
        inst = v.memset(carry[:, :, :], 0.0)
        p.inc(inst, "ve")
    prep_done = p.cnt["ve"]
    cks_ready = prep_done
    carry_ok = prep_done

    bank_ev = [p.cnt["se"]] * 4
    slab_ln_ve = [0] * NSLAB
    slab_ln_se = [0] * NSLAB
    slab_tp = [0] * NSLAB
    slab_mm = [0] * NSLAB
    slab_ev = [0] * NSLAB
    slab_dma = [0] * NSLAB
    slab_scan = [0] * NSLAB
    outbuf = {"k": kout, "v": vout, "r": rout}
    func = {"k": AF.Exp, "v": AF.Identity, "r": AF.Sigmoid}
    ckj = {"k": 0, "v": 1, "r": 2}

    for s in range(NSLAB):
        @blk.sync
        def _(sy, s=s):
            if s > 0:
                p.wait(sy, "ve", slab_ln_ve[s - 1])
                p.wait(sy, "se", slab_ln_se[s - 1])
            src = xp[1 + s * SLAB: 1 + (s + 1) * SLAB, :].rearrange(
                "(a p) c -> p a c", p=128)
            p.inc(sy.dma_start(out=x_tm[:, :, :], in_=src), "dl", 16)
            p.wait(sy, "dl")
        ld_x = p.cnt["dl"]

        @blk.vector
        def _(v, s=s):
            p.wait(v, "dl", ld_x)
            if s > 0:
                p.wait(v, "pe", slab_tp[s - 1])

        @blk.scalar
        def _(sc, s=s):
            p.wait(sc, "dl", ld_x)

        xs = [x_tm[:, i, :] for i in range(NTS)]
        zns = [zn_tm[:, i, :] for i in range(NTS)]
        emit_ln_batch(p, blk, xs, xs, st, st2, junk, wv=[("dl", ld_x)])
        emit_ln_batch(p, blk, xs, zns, st, st2, junk,
                      wv=[("ve", p.cnt["ve"]), ("se", p.cnt["se"])])
        slab_ln_ve[s] = p.cnt["ve"]
        slab_ln_se[s] = p.cnt["se"]
        zn_ready = p.cnt["ve"]

        if s > 0:
            @blk.vector
            def _(v, s=s):
                p.inc(v.tensor_copy(znf[:, :, 0:1], znf[:, :, SLAB:SLAB + 1]), "ve")
                p.wait(v, "ve")

        for i in range(NTS):
            @blk.tensor
            def _(t, s=s, i=i, znr=zn_ready, evp=ev_ptb):
                p.wait(t, "ve", znr)
                p.wait(t, "ve", evp)
                inst = None
                for dc in range(NCH):
                    inst = t.transpose(ptb[:, bts(dc, 128)],
                                       zn_tm[:, i, bts(dc, 128)], identb[:])
                p.inc(inst, "pe")
            tpd = p.cnt["pe"]

            @blk.vector
            def _(v, s=s, i=i, tpd=tpd):
                p.wait(v, "pe", tpd)
                dst = znf[:, :, 1 + i * 128: 1 + (i + 1) * 128]
                p.inc(v.tensor_copy(dst,
                                    ptb[:, :].rearrange("p (c j) -> p c j", c=NCH)),
                      "ve")
            ev_ptb = p.cnt["ve"]
        slab_tp[s] = p.cnt["pe"]
        znf_ready = p.cnt["ve"]

        @blk.vector
        def _(v, s=s):
            p.wait(v, "ve", znf_ready)
            if s > 0:
                p.wait(v, "pe", slab_mm[s - 1])
            cur = slice(1, 1 + SLAB)
            prv = slice(0, SLAB)
            inst = None
            for dc in range(NCH):
                inst = v.tensor_tensor(dts[:, dc, :], znf[:, dc, cur],
                                       znf[:, dc, prv], ALU.subtract)
            p.inc(inst, "ve")
            p.wait(v, "ve")
            for wi, n in enumerate("kvr"):
                for dc in range(NCH):
                    inst = v.scalar_tensor_tensor(tsv[n][:, dc, :], dts[:, dc, :],
                                                  pcol[:, dc, 2 + wi: 3 + wi],
                                                  znf[:, dc, prv], ALU.mult, ALU.add)
            p.inc(inst, "ve")
        ts_ready = p.cnt["ve"]

        gi = 0
        for wi, n in enumerate("kvr"):
            for oc in range(NCH):
                bk = gi % 4

                @blk.tensor
                def _(t, n=n, oc=oc, bk=bk, s=s, gi=gi, tsr=ts_ready):
                    p.wait(t, "ve", tsr)
                    p.wait(t, "se", bank_ev[bk])
                    if gi == 0 and s > 0:
                        p.wait(t, "dg", slab_dma[s - 1])
                        p.wait(t, "ve", slab_scan[s - 1])
                    inst = None
                    for dc in range(NCH):
                        inst = t.matmul(pbank[bk][:, :], wT[n][:, dc, bts(oc, 128)],
                                        tsv[n][:, dc, :], start=(dc == 0),
                                        stop=(dc == NCH - 1))
                    p.inc(inst, "pe")
                mmd = p.cnt["pe"]
                gi += 1

                @blk.scalar
                def _(sc, n=n, oc=oc, bk=bk, mmd=mmd):
                    p.wait(sc, "pe", mmd)
                    p.wait(sc, "ve", cks_ready)
                    j = ckj[n]
                    bank_ev[bk] = p.inc(
                        sc.activation(outbuf[n][:, oc, :], pbank[bk][:, :], func[n],
                                      bias=cks[:, oc, j:j + 1]), "se")
        slab_mm[s] = p.cnt["pe"]
        slab_ev[s] = p.cnt["se"]

        @blk.vector
        def _(v, s=s):
            p.wait(v, "se", slab_ev[s])
            p.wait(v, "ve", carry_ok)
            inst = None
            for dc in range(NCH):
                inst = v.reduce_sum(sksum[:, dc:dc + 1], kout[:, dc, :], AX.X)
            p.inc(inst, "ve")
            p.wait(v, "ve")
            p.inc(v.tensor_tensor(carry[:, :, 0], carry[:, :, 0], sksum[:, :],
                                  ALU.add), "ve")
            p.wait(v, "ve")
            if s == NSLAB - 1:
                v.tensor_copy(ta[:, :, 1], carry[:, :, 0])
                p.inc(v.tensor_tensor(ta[:, :, 0], carry[:, :, 0],
                                      kout[:, :, SLAB - 1], ALU.subtract), "ve")
                p.wait(v, "ve")
        carry_ok = p.cnt["ve"]
        slab_scan[s] = p.cnt["ve"]

        @blk.gpsimd
        def _(g, s=s):
            p.wait(g, "se", slab_ev[s])
            cur = slice(s * SLAB, (s + 1) * SLAB)
            p.inc(g.dma_start(out=o_k.rearrange("p (c t) -> p c t", c=NCH)[:, :, cur],
                              in_=kout[:, :, :]), "dg", 16)
            p.inc(g.dma_start(out=o_v.rearrange("p (c t) -> p c t", c=NCH)[:, :, cur],
                              in_=vout[:, :, :]), "dg", 16)
            p.inc(g.dma_start(out=o_r.rearrange("p (c t) -> p c t", c=NCH)[:, :, cur],
                              in_=rout[:, :, :]), "dg", 16)
        slab_dma[s] = p.cnt["dg"]

    @blk.gpsimd
    def _(g):
        p.wait(g, "ve", slab_scan[NSLAB - 1])
        p.inc(g.dma_start(out=o_ta, in_=ta[:, :, :].rearrange("p c t -> p (c t)")),
              "dg", 16)
        p.wait(g, "dg")

    return p


def _seq_start(c):
    return (c % (NCORE // B)) == 0


def _shard_rows(x_rows, halo_rows=1):
    out = []
    for c in range(NCORE):
        s = c * TL
        lead = []
        for j in range(halo_rows, 0, -1):
            i = s if _seq_start(c) else s - j
            lead.append(x_rows[i:i + 1])
        out.append(np.ascontiguousarray(
            np.concatenate(lead + [x_rows[s:s + TL]], axis=0)))
    return out


def run_l1(inputs, x_rows):
    p = build_launch1()
    shards = _shard_rows(x_rows, 1)
    prow = np.ascontiguousarray(np.stack(
        [inputs["L0_attn_ln1_g"], inputs["L0_attn_ln1_b"], inputs["L0_attn_mix_k"],
         inputs["L0_attn_mix_v"], inputs["L0_attn_mix_r"]]).astype(np.float32))
    maps = []
    for c in range(NCORE):
        sf = np.array([[1.0 if _seq_start(c) else 0.0]], np.float32)
        maps.append(dict(
            xp=shards[c], wk=np.ascontiguousarray(inputs["L0_attn_Wk_t"]),
            wv=np.ascontiguousarray(inputs["L0_attn_Wv_t"]),
            wr=np.ascontiguousarray(inputs["L0_attn_Wr_t"]), prow=prow, sflag=sf))
    res = run_bass_kernel_spmd(p.nc, maps, core_ids=list(range(NCORE)))
    return res.results




def build_launch2():
    p = P()
    kc = p.din("kc", [128, NCH * TL])
    ta_all = p.din("ta_all", [NCORE * 128, NCH * 2])
    pmask = p.din("pmask", [1, NCORE])
    wgs = p.din("wgs", [4 * 128, C])
    wus = p.din("wus", [4 * 128, C])
    wds = p.din("wds", [128, 4 * C])
    o_tb = p.dout("o_tb", [128, NCH * 2])
    o_wsums = p.dout("o_wsums", [1, 4])

    tstack = p.sb("tstack", [128, NCORE, NCH, 2])
    pmr = p.sb("pmr", [1, NCORE])
    pmc = p.sb("pmc", [128, NCORE])
    ones1 = p.sb("ones1", [1, 128])
    macc = p.sb("macc", [128, NCH, 2])
    kslab = p.sb("kslab", [128, NCH, SLAB])
    m0 = p.sb("m0", [128, NCH, SLAB + 1])
    em = p.sb("em", [128, NCH, SLAB])
    esc = p.sb("esc", [128, NCH, SLAB + 1])
    mcarry = p.sb("mcarry", [128, NCH])
    ecarry = p.sb("ecarry", [128, NCH])
    tb = p.sb("tb", [128, NCH, 2])
    wqs = [p.sb("wqs0", [128, 4 * C]), p.sb("wqs1", [128, 4 * C])]
    wjunk = p.sb("wjunk", [128, 4 * C])
    wcs = p.sb("wcs", [128, 12])
    w3 = p.sb("w3", [128, 4])
    onescol = p.sb("onescol", [128, 1])
    wsrow = p.sb("wsrow", [1, 4])
    ptf = p.ps("l2ptf", [128, 512])
    ptw = p.ps("l2ptw", [128, 512])

    blk = p.stack.enter_context(p.nc.Block())

    @blk.sync
    def _(sy):
        p.inc(sy.dma_start(out=tstack[:, :, :, :],
                           in_=ta_all.rearrange("(j p) c -> p j c", p=128)
                           .rearrange("p j (c t) -> p j c t", t=2)), "dl", 16)
        p.inc(sy.dma_start(out=pmr[:], in_=pmask), "dl", 16)
        p.wait(sy, "dl")
    ld0 = p.cnt["dl"]

    @blk.gpsimd
    def _(g):
        p.inc(g.memset(ones1[:], 1.0), "po")
        p.inc(g.memset(onescol[:], 1.0), "po")

    wchunks = ([(wgs[bts(j, 128), :], C) for j in range(4)]
               + [(wus[bts(j, 128), :], C) for j in range(4)]
               + [(wds, 4 * C)])
    buf_free = [0, 0]
    for k, (src, wid) in enumerate(wchunks):
        buf = wqs[k % 2]

        @blk.gpsimd
        def _(g, src=src, wid=wid, buf=buf, bf=buf_free[k % 2]):
            p.wait(g, "se", bf)
            p.inc(g.dma_start(out=buf[:, 0:wid], in_=src), "dg", 16)
            p.wait(g, "dg")
        ldw = p.cnt["dg"]

        @blk.scalar
        def _(sc, k=k, wid=wid, buf=buf, ldw=ldw, pse=p.cnt.get("se", 0)):
            p.wait(sc, "dg", ldw)
            p.wait(sc, "se", pse)
            p.inc(sc.activation(wjunk[:, 0:wid], buf[:, 0:wid], AF.Abs,
                                accum_out=wcs[:, k:k + 1]), "se")
        buf_free[k % 2] = p.cnt["se"]
    wabs_done = p.cnt["se"]

    @blk.vector
    def _(v, ad=wabs_done):
        p.wait(v, "se", ad)
        p.inc(v.memset(w3[:, :], 0.0), "ve")
        p.wait(v, "ve")
        p.inc(v.reduce_sum(w3[:, 0:1], wcs[:, 0:4], AX.X), "ve")
        p.inc(v.reduce_sum(w3[:, 1:2], wcs[:, 4:8], AX.X), "ve")
        p.inc(v.tensor_copy(w3[:, 2:3], wcs[:, 8:9]), "ve")
        p.wait(v, "ve")
    w3_done = p.cnt["ve"]

    @blk.tensor
    def _(t, wd=w3_done):
        p.wait(t, "ve", wd)
        p.wait(t, "po")
        p.wait(t, "pe")
        p.inc(t.matmul(ptw[0:1, 8:12], onescol[:, 0:1], w3[:, :],
                       start=True, stop=True), "pe")
    wsum_pe = p.cnt["pe"]

    @blk.vector
    def _(v, wp=wsum_pe):
        p.wait(v, "pe", wp)
        p.inc(v.tensor_copy(wsrow[0:1, :], ptw[0:1, 8:12]), "ve")
        p.wait(v, "ve")
    wsrow_done = p.cnt["ve"]

    @blk.gpsimd
    def _(g, wd=wsrow_done):
        p.wait(g, "ve", wd)
        p.inc(g.dma_start(out=o_wsums, in_=wsrow[0:1, :]), "dg", 16)
        p.wait(g, "dg")

    @blk.tensor
    def _(t):
        p.wait(t, "dl", ld0)
        p.wait(t, "po")
        p.inc(t.matmul(ptf[:, 0:NCORE], ones1[:], pmr[:], start=True, stop=True), "pe")

    @blk.vector
    def _(v):
        p.wait(v, "pe")
        p.inc(v.tensor_copy(pmc[:, :], ptf[:, 0:NCORE]), "ve")
        p.wait(v, "ve")
        p.inc(v.memset(macc[:, :, 0], 0.0), "ve")
        p.wait(v, "ve")
        for j in range(NCORE):
            a, b = j % 2, 1 - j % 2
            p.inc(v.scalar_tensor_tensor(macc[:, :, b], tstack[:, j, :, 1],
                                         pmc[:, j:j + 1], macc[:, :, a],
                                         ALU.mult, ALU.add), "ve")
            p.wait(v, "ve")
        p.inc(v.tensor_copy(mcarry[:, :], macc[:, :, NCORE % 2]), "ve")
        inst = v.memset(ecarry[:, :], 0.0)
        p.inc(inst, "ve")
        p.wait(v, "ve")

    ksl2 = [kslab, p.sb("kslabB", [128, NCH, SLAB])]
    comp_end = [0] * NSLAB
    for s in range(NSLAB):
        ks = ksl2[s % 2]

        @blk.sync
        def _(sy, s=s, ks=ks, kf=(comp_end[s - 2] if s >= 2 else 0)):
            p.wait(sy, "ve", kf)
            p.inc(sy.dma_start(
                out=ks[:, :, :],
                in_=kc.rearrange("p (c t) -> p c t", c=NCH)[:, :, bts(s, SLAB)]),
                "dl", 16)
            p.wait(sy, "dl")
        ldk = p.cnt["dl"]

        @blk.vector
        def _(v, s=s, ks=ks, ldk=ldk):
            p.wait(v, "dl", ldk)
            p.inc(v.tensor_copy(m0[:, :, 0:1], mcarry[:, :].rearrange("p (c o) -> p c o", o=1)),
                  "ve")
            p.wait(v, "ve")
            inst = None
            for dc in range(NCH):
                inst = v.tensor_tensor_scan(m0[:, dc, 1:SLAB + 1], ks[:, dc, :],
                                            ks[:, dc, :], mcarry[:, dc:dc + 1],
                                            ALU.add, ALU.bypass)
            p.inc(inst, "ve")
            p.wait(v, "ve")
            p.inc(v.tensor_copy(mcarry[:, :], m0[:, :, SLAB]), "ve")
            inst = None
            for dc in range(NCH):
                inst = v.tensor_tensor(em[:, dc, :], m0[:, dc, 0:SLAB],
                                       ks[:, dc, :], ALU.max)
            p.inc(inst, "ve")
            p.wait(v, "ve")
            inst = None
            for dc in range(NCH):
                inst = v.tensor_tensor_scan(esc[:, dc, 1:SLAB + 1], em[:, dc, :],
                                            em[:, dc, :], ecarry[:, dc:dc + 1],
                                            ALU.add, ALU.bypass)
            p.inc(inst, "ve")
            p.wait(v, "ve")
            p.inc(v.tensor_copy(ecarry[:, :], esc[:, :, SLAB]), "ve")
            if s == NSLAB - 1:
                v.tensor_copy(tb[:, :, 0:1], esc[:, :, SLAB - 1:SLAB])
                p.inc(v.tensor_copy(tb[:, :, 1:2], esc[:, :, SLAB:SLAB + 1]), "ve")
            p.wait(v, "ve")
        comp_end[s] = p.cnt["ve"]

    @blk.gpsimd
    def _(g):
        p.wait(g, "ve")
        p.inc(g.dma_start(out=o_tb, in_=tb[:, :, :].rearrange("p c t -> p (c t)")),
              "dg", 16)
        p.wait(g, "dg")

    return p


def build_launch3():
    p = P()
    kc = p.din("kc", [128, NCH * TL])
    vc = p.din("vc", [128, NCH * TL])
    ta_all = p.din("ta_all", [NCORE * 128, NCH * 2])
    tb_all = p.din("tb_all", [NCORE * 128, NCH * 2])
    pmask = p.din("pmask", [1, NCORE])
    sflag = p.din("sflag", [1, 1])
    o_m = p.dout("o_m", [128, NCH * TL])
    o_ll = p.dout("o_ll", [128, NCH * TL])
    o_gl = p.dout("o_gl", [128, NCH * TL])
    o_td = p.dout("o_td", [128, NCH * 4])

    tstack = p.sb("tstack", [128, NCORE, NCH, 2])
    tstack2 = p.sb("tstack2", [128, NCORE, NCH, 2])
    pmr = p.sb("pmr", [1, NCORE])
    sfr = p.sb("sfr", [1, 1])
    pmc = p.sb("pmc", [128, NCORE])
    sfc = p.sb("sfc", [128, 2])
    ones1 = p.sb("ones1", [1, 128])
    macc = p.sb("macc", [128, NCH, 2])
    skin = p.sb("skin", [128, NCH])
    epse = p.sb("epse", [128, NCH])
    kslab = p.sb("kslab", [128, NCH, SLAB])
    vslab = p.sb("vslab", [128, NCH, SLAB])
    m0 = p.sb("m0", [128, NCH, SLAB + 1])
    em = p.sb("em", [128, NCH, SLAB])
    esc = p.sb("esc", [128, NCH, SLAB + 1])
    mg = p.sb("mg", [128, NCH, SLAB + 1])
    msafe = p.sb("msafe", [128, NCH, SLAB])
    rm = p.sb("rm", [128, NCH, SLAB])
    ratio = p.sb("ratio", [128, NCH, SLAB])
    lsl = p.sb("lsl", [128, NCH, SLAB])
    lls = p.sb("lls", [128, NCH, SLAB + 1])
    el = p.sb("el", [128, NCH, SLAB])
    gsl = em
    gls = m0
    mcarry = p.sb("mcarry", [128, NCH])
    ecarry = p.sb("ecarry", [128, NCH])
    lcarry = p.sb("lcarry", [128, NCH])
    gcarry = p.sb("gcarry", [128, NCH])
    td = p.sb("td", [128, NCH, 4])
    tmpc = p.sb("tmpc", [128, NCH])
    ptf = p.ps("l3ptf", [128, 512])

    blk = p.stack.enter_context(p.nc.Block())

    @blk.sync
    def _(sy):
        p.inc(sy.dma_start(out=tstack[:, :, :, :],
                           in_=ta_all.rearrange("(j p) c -> p j c", p=128)
                           .rearrange("p j (c t) -> p j c t", t=2)), "dl", 16)
        p.inc(sy.dma_start(out=tstack2[:, :, :, :],
                           in_=tb_all.rearrange("(j p) c -> p j c", p=128)
                           .rearrange("p j (c t) -> p j c t", t=2)), "dl", 16)
        p.inc(sy.dma_start(out=pmr[:], in_=pmask), "dl", 16)
        p.inc(sy.dma_start(out=sfr[:], in_=sflag), "dl", 16)
        p.wait(sy, "dl")
    ld0 = p.cnt["dl"]

    @blk.gpsimd
    def _(g):
        p.inc(g.memset(ones1[:], 1.0), "po")

    @blk.tensor
    def _(t):
        p.wait(t, "dl", ld0)
        p.wait(t, "po")
        t.matmul(ptf[:, 0:NCORE], ones1[:], pmr[:], start=True, stop=True)
        p.inc(t.matmul(ptf[:, 16:17], ones1[:], sfr[:], start=True, stop=True), "pe")

    @blk.vector
    def _(v):
        p.wait(v, "pe")
        v.tensor_copy(pmc[:, :], ptf[:, 0:NCORE])
        p.inc(v.tensor_copy(sfc[:, 0:1], ptf[:, 16:17]), "ve")
        p.wait(v, "ve")
        p.inc(v.tensor_scalar(sfc[:, 1:2], sfc[:, 0:1], -1.0, 1.0, ALU.mult, ALU.add),
              "ve")
        p.wait(v, "ve")
        for (dst, stk) in ((skin, tstack), (epse, tstack2)):
            p.inc(v.memset(macc[:, :, 0], 0.0), "ve")
            p.wait(v, "ve")
            for j in range(NCORE):
                a, b = j % 2, 1 - j % 2
                p.inc(v.scalar_tensor_tensor(macc[:, :, b], stk[:, j, :, 1],
                                             pmc[:, j:j + 1], macc[:, :, a],
                                             ALU.mult, ALU.add), "ve")
                p.wait(v, "ve")
            p.inc(v.tensor_copy(dst[:, :], macc[:, :, NCORE % 2]), "ve")
            p.wait(v, "ve")
        p.inc(v.tensor_scalar(epse[:, :], epse[:, :], EPS_WKV, None, ALU.mult), "ve")
        p.wait(v, "ve")
        v.tensor_copy(mcarry[:, :], skin[:, :])
        v.memset(ecarry[:, :], 0.0)
        v.memset(lcarry[:, :], 0.0)
        p.inc(v.memset(gcarry[:, :], 0.0), "ve")
        p.wait(v, "ve")

    for s in range(NSLAB):
        @blk.sync
        def _(sy, s=s):
            if s > 0:
                p.wait(sy, "ve")
                p.wait(sy, "dg")
            p.inc(sy.dma_start(
                out=kslab[:, :, :],
                in_=kc.rearrange("p (c t) -> p c t", c=NCH)[:, :, bts(s, SLAB)]),
                "dl", 16)
            p.inc(sy.dma_start(
                out=vslab[:, :, :],
                in_=vc.rearrange("p (c t) -> p c t", c=NCH)[:, :, bts(s, SLAB)]),
                "dl", 16)
            p.wait(sy, "dl")
        ldk = p.cnt["dl"]

        @blk.vector
        def _(v, s=s, ldk=ldk):
            p.wait(v, "dl", ldk)
            p.inc(v.tensor_copy(m0[:, :, 0:1], mcarry[:, :].rearrange("p (c o) -> p c o", o=1)),
                  "ve")
            p.wait(v, "ve")
            inst = None
            for dc in range(NCH):
                inst = v.tensor_tensor_scan(m0[:, dc, 1:SLAB + 1], kslab[:, dc, :],
                                            kslab[:, dc, :], mcarry[:, dc:dc + 1],
                                            ALU.add, ALU.bypass)
            p.inc(inst, "ve")
            p.wait(v, "ve")
            v.tensor_copy(mcarry[:, :], m0[:, :, SLAB])
            inst = None
            for dc in range(NCH):
                inst = v.tensor_tensor(em[:, dc, :], m0[:, dc, 0:SLAB],
                                       kslab[:, dc, :], ALU.max)
            p.inc(inst, "ve")
            p.wait(v, "ve")
            p.inc(v.tensor_copy(esc[:, :, 0:1], ecarry[:, :].rearrange("p (c o) -> p c o", o=1)),
                  "ve")
            p.wait(v, "ve")
            inst = None
            for dc in range(NCH):
                inst = v.tensor_tensor_scan(esc[:, dc, 1:SLAB + 1], em[:, dc, :],
                                            em[:, dc, :], ecarry[:, dc:dc + 1],
                                            ALU.add, ALU.bypass)
            p.inc(inst, "ve")
            p.wait(v, "ve")
            p.inc(v.tensor_copy(ecarry[:, :], esc[:, :, SLAB]), "ve")
            p.wait(v, "ve")
            inst = None
            for dc in range(NCH):
                inst = v.scalar_tensor_tensor(mg[:, dc, :], esc[:, dc, :], EPS_WKV,
                                              m0[:, dc, :], ALU.mult, ALU.add)
            p.inc(inst, "ve")
            p.wait(v, "ve")
            inst = None
            for dc in range(NCH):
                inst = v.tensor_scalar(mg[:, dc, :], mg[:, dc, :],
                                       epse[:, dc:dc + 1], None, ALU.add)
            p.inc(inst, "ve")
            p.wait(v, "ve")
            inst = None
            for dc in range(NCH):
                inst = v.tensor_scalar(msafe[:, dc, :], mg[:, dc, 0:SLAB], 1e-30,
                                       None, ALU.max)
            p.inc(inst, "ve")

        @blk.scalar
        def _(sc, s=s):
            p.wait(sc, "ve")
            p.inc(sc.activation(rm[:, :, :], msafe[:, :, :], AF.Ln), "se")
            p.wait(sc, "se")
            p.inc(sc.activation(rm[:, :, :], rm[:, :, :], AF.Exp, scale=-1.0), "se")

        @blk.vector
        def _(v, s=s):
            p.wait(v, "se")
            inst = None
            for dc in range(NCH):
                inst = v.tensor_tensor(ratio[:, dc, :], kslab[:, dc, :],
                                       rm[:, dc, :], ALU.mult)
            p.inc(inst, "ve")
            p.wait(v, "ve")
            inst = None
            for dc in range(NCH):
                inst = v.tensor_scalar(msafe[:, dc, :], ratio[:, dc, :], 1.0,
                                       None, ALU.max)
            p.inc(inst, "ve")

        @blk.scalar
        def _(sc, s=s):
            p.wait(sc, "ve")
            p.inc(sc.activation(lsl[:, :, :], msafe[:, :, :], AF.Ln), "se")

        @blk.vector
        def _(v, s=s):
            p.wait(v, "se")
            if s == 0:
                p.inc(v.tensor_scalar(lsl[:, :, 0:1], lsl[:, :, 0:1], sfc[:, 1:2],
                                      None, ALU.mult), "ve")
                p.wait(v, "ve")
            p.inc(v.tensor_copy(lls[:, :, 0:1], lcarry[:, :].rearrange("p (c o) -> p c o", o=1)),
                  "ve")
            p.wait(v, "ve")
            inst = None
            for dc in range(NCH):
                inst = v.tensor_tensor_scan(lls[:, dc, 1:SLAB + 1], lsl[:, dc, :],
                                            lsl[:, dc, :], lcarry[:, dc:dc + 1],
                                            ALU.add, ALU.bypass)
            p.inc(inst, "ve")
            p.wait(v, "ve")
            p.inc(v.tensor_copy(lcarry[:, :], lls[:, :, SLAB]), "ve")

        @blk.scalar
        def _(sc, s=s):
            p.wait(sc, "ve")
            p.inc(sc.activation(el[:, :, :], lls[:, :, 0:SLAB], AF.Exp), "se")

        @blk.vector
        def _(v, s=s):
            p.wait(v, "se")
            inst = None
            for dc in range(NCH):
                inst = v.tensor_tensor(gsl[:, dc, :], vslab[:, dc, :],
                                       ratio[:, dc, :], ALU.mult)
            p.inc(inst, "ve")
            p.wait(v, "ve")
            inst = None
            for dc in range(NCH):
                inst = v.tensor_tensor(gsl[:, dc, :], gsl[:, dc, :],
                                       el[:, dc, :], ALU.mult)
            p.inc(inst, "ve")
            p.wait(v, "ve")
            if s == 0:
                p.inc(v.tensor_scalar(tmpc[:, :], gsl[:, :, 0], sfc[:, 1:2],
                                      None, ALU.mult), "ve")
                p.wait(v, "ve")
                p.inc(v.scalar_tensor_tensor(gsl[:, :, 0], vslab[:, :, 0],
                                             sfc[:, 0:1], tmpc[:, :],
                                             ALU.mult, ALU.add), "ve")
                p.wait(v, "ve")
            p.inc(v.tensor_copy(gls[:, :, 0:1], gcarry[:, :].rearrange("p (c o) -> p c o", o=1)),
                  "ve")
            p.wait(v, "ve")
            inst = None
            for dc in range(NCH):
                inst = v.tensor_tensor_scan(gls[:, dc, 1:SLAB + 1], gsl[:, dc, :],
                                            gsl[:, dc, :], gcarry[:, dc:dc + 1],
                                            ALU.add, ALU.bypass)
            p.inc(inst, "ve")
            p.wait(v, "ve")
            p.inc(v.tensor_copy(gcarry[:, :], gls[:, :, SLAB]), "ve")
            if s == NSLAB - 1:
                v.tensor_copy(td[:, :, 0:1], lls[:, :, SLAB - 1:SLAB])
                v.tensor_copy(td[:, :, 1:2], lls[:, :, SLAB:SLAB + 1])
                v.tensor_copy(td[:, :, 2:3], gls[:, :, SLAB - 1:SLAB])
                p.inc(v.tensor_copy(td[:, :, 3:4], gls[:, :, SLAB:SLAB + 1]), "ve")
            p.wait(v, "ve")

        @blk.gpsimd
        def _(g, s=s):
            p.wait(g, "ve")
            cur = bts(s, SLAB)
            p.inc(g.dma_start(out=o_m.rearrange("p (c t) -> p c t", c=NCH)[:, :, cur],
                              in_=mg[:, :, 1:SLAB + 1]), "dg", 16)
            p.inc(g.dma_start(out=o_ll.rearrange("p (c t) -> p c t", c=NCH)[:, :, cur],
                              in_=lls[:, :, 1:SLAB + 1]), "dg", 16)
            p.inc(g.dma_start(out=o_gl.rearrange("p (c t) -> p c t", c=NCH)[:, :, cur],
                              in_=gls[:, :, 1:SLAB + 1]), "dg", 16)

    @blk.gpsimd
    def _(g):
        p.wait(g, "ve")
        p.inc(g.dma_start(out=o_td, in_=td[:, :, :].rearrange("p c t -> p (c t)")),
              "dg", 16)
        p.wait(g, "dg")

    return p


def _pmask(c):
    m = np.zeros((1, NCORE), np.float32)
    base = (c // (NCORE // B)) * (NCORE // B)
    m[0, base:c] = 1.0
    return m


def run_l2(inputs, r1):
    p = build_launch2()
    ta_all = np.concatenate([np.asarray(r1[j]["o_ta"]) for j in range(NCORE)], axis=0)
    wg_full = np.ascontiguousarray(inputs["L1_ffn_w_gate"].astype(np.float32))
    wu_full = np.ascontiguousarray(inputs["L1_ffn_w_up"].astype(np.float32))
    wd_full = np.ascontiguousarray(inputs["L1_ffn_w_down"].astype(np.float32))
    maps = [dict(kc=np.asarray(r1[c]["o_k"]), ta_all=ta_all, pmask=_pmask(c),
                 wgs=np.ascontiguousarray(wg_full[c * 512:(c + 1) * 512]),
                 wus=np.ascontiguousarray(wu_full[c * 512:(c + 1) * 512]),
                 wds=np.ascontiguousarray(wd_full[c * 128:(c + 1) * 128]))
            for c in range(NCORE)]
    return run_bass_kernel_spmd(p.nc, maps, core_ids=list(range(NCORE))).results


def run_l3(r1, r2):
    p = build_launch3()
    ta_all = np.concatenate([np.asarray(r1[j]["o_ta"]) for j in range(NCORE)], axis=0)
    tb_all = np.concatenate([np.asarray(r2[j]["o_tb"]) for j in range(NCORE)], axis=0)
    maps = []
    for c in range(NCORE):
        sf = np.array([[1.0 if _seq_start(c) else 0.0]], np.float32)
        maps.append(dict(kc=np.asarray(r1[c]["o_k"]), vc=np.asarray(r1[c]["o_v"]),
                         ta_all=ta_all, tb_all=tb_all, pmask=_pmask(c), sflag=sf))
    return run_bass_kernel_spmd(p.nc, maps, core_ids=list(range(NCORE))).results


def build_launch4():
    p = P()
    xp = p.din("xp", [TL, C])
    kc = p.din("kc", [128, NCH * TL])
    vc = p.din("vc", [128, NCH * TL])
    rc = p.din("rc", [128, NCH * TL], BF16)
    mf = p.din("mf", [128, NCH * TL])
    llf = p.din("llf", [128, NCH * TL])
    glf = p.din("glf", [128, NCH * TL])
    ta_all = p.din("ta_all", [NCORE * 128, NCH * 2])
    tb_all = p.din("tb_all", [NCORE * 128, NCH * 2])
    td_all = p.din("td_all", [NCORE * 128, NCH * 4])
    pmask = p.din("pmask", [1, NCORE])
    urow = p.din("urow", [1, C])
    o_z1 = p.dout("o_z1", [TL, C])

    tsa = p.sb("tsa", [128, NCORE, NCH, 2])
    tsb = p.sb("tsb", [128, NCORE, NCH, 2])
    tsd = p.sb("tsd", [128, NCORE, NCH, 4])
    pmr = p.sb("pmr", [1, NCORE])
    pmc = p.sb("pmc", [128, NCORE])
    ones1 = p.sb("ones1", [1, 128])
    macc = p.sb("macc", [128, NCH, 2])
    laccs = p.sb("laccs", [128, NCH, 2])
    gaccs = p.sb("gaccs", [128, NCH, 2])
    ecol = p.sb("ecol", [128, NCH])
    mlead = p.sb("mlead", [128, NCH])
    lin = p.sb("lin", [128, NCH])
    elin = p.sb("elin", [128, NCH])
    ain = p.sb("ain", [128, NCH])
    ucol = p.sb("ucol", [128, NCH])
    ustage = p.sb("ustage", [128, C])
    identf = p.sb("identf", [128, 128], F32)
    kslab = p.sb("kslab", [128, NCH, SLAB])
    vslab = p.sb("vslab", [128, NCH, SLAB])
    rslab = p.sb("rslab", [128, NCH, SLAB])
    mext = p.sb("mext", [128, NCH, SLAB + 1])
    llext = p.sb("llext", [128, NCH, SLAB + 1])
    glext = p.sb("glext", [128, NCH, SLAB + 1])
    sA = p.sb("sA", [128, NCH, SLAB])
    sB = p.sb("sB", [128, NCH, SLAB])
    sC = p.sb("sC", [128, NCH, SLAB])
    x_tm = p.sb("x_tm", [128, NTS, C])
    z1_tm = p.sb("z1_tm", [128, NTS, C])
    st = p.sb("st", [128, NTS, 4])
    st2 = p.sb("st2", [128, NTS, 4])
    junk = p.sb("junk", [128, C])
    ptf = p.ps("l4ptf", [128, 512])
    ptg = p.ps("l4ptg", [128, 512])

    blk = p.stack.enter_context(p.nc.Block())
    emit_identity(p, blk, identf)

    @blk.sync
    def _(sy):
        p.inc(sy.dma_start(out=tsa[:, :, :, :],
                           in_=ta_all.rearrange("(j p) c -> p j c", p=128)
                           .rearrange("p j (c t) -> p j c t", t=2)), "dl", 16)
        p.inc(sy.dma_start(out=tsb[:, :, :, :],
                           in_=tb_all.rearrange("(j p) c -> p j c", p=128)
                           .rearrange("p j (c t) -> p j c t", t=2)), "dl", 16)
        p.inc(sy.dma_start(out=tsd[:, :, :, :],
                           in_=td_all.rearrange("(j p) c -> p j c", p=128)
                           .rearrange("p j (c t) -> p j c t", t=4)), "dl", 16)
        p.inc(sy.dma_start(out=pmr[:], in_=pmask), "dl", 16)

    @blk.gpsimd
    def _(g):
        p.inc(g.memset(ones1[:], 1.0), "po")
        p.inc(g.memset(ustage[:, :], 0.0), "po")

    @blk.sync
    def _(sy):
        p.wait(sy, "po")
        p.inc(sy.dma_start(out=ustage[0:1, :], in_=urow), "dl", 16)
        p.wait(sy, "dl")
    ld0 = p.cnt["dl"]

    @blk.tensor
    def _(t):
        p.wait(t, "dl", ld0)
        p.wait(t, "po")
        p.inc(t.matmul(ptf[:, 0:NCORE], ones1[:], pmr[:], start=True, stop=True), "pe")
        for dc in range(NCH):
            p.inc(t.transpose(ptg[:, bts(dc % 4, 128)], ustage[:, bts(dc, 128)],
                              identf[:]), "pe")
            if dc % 4 == 3:
                pass
    pe_after_u = p.cnt["pe"]

    @blk.vector
    def _(v):
        p.wait(v, "pe", pe_after_u)
        inst = None
        for dc in range(NCH):
            inst = v.tensor_copy(ucol[:, dc:dc + 1],
                                 ptg[:, (dc % 4) * 128:(dc % 4) * 128 + 1])
        p.inc(inst, "ve")
        p.inc(v.tensor_copy(pmc[:, :], ptf[:, 0:NCORE]), "ve")
        p.wait(v, "ve")

    @blk.tensor
    def _(t):
        p.wait(t, "ve")
        inst = None
        for dc in range(4):
            inst = t.transpose(ptf[:, bts(dc, 128)], ustage[:, bts(dc, 128)],
                               identf[:])
        p.inc(inst, "pe")
    pe_u2 = p.cnt["pe"]

    @blk.vector
    def _(v):
        p.wait(v, "pe", pe_u2)
        inst = None
        for dc in range(4):
            inst = v.tensor_copy(ucol[:, dc:dc + 1], ptf[:, dc * 128:dc * 128 + 1])
        p.inc(inst, "ve")

    @blk.scalar
    def _(sc):
        p.wait(sc, "ve")
        p.inc(sc.activation(ucol[:, :], ucol[:, :], AF.Exp), "se")

    @blk.vector
    def _(v):
        p.wait(v, "ve")
        for (dst, stk, col) in ((mlead, tsa, 1), (ecol, tsb, 1)):
            p.inc(v.memset(macc[:, :, 0], 0.0), "ve")
            p.wait(v, "ve")
            for j in range(NCORE):
                a, b = j % 2, 1 - j % 2
                p.inc(v.scalar_tensor_tensor(macc[:, :, b], stk[:, j, :, col],
                                             pmc[:, j:j + 1], macc[:, :, a],
                                             ALU.mult, ALU.add), "ve")
                p.wait(v, "ve")
            p.inc(v.tensor_copy(dst[:, :], macc[:, :, NCORE % 2]), "ve")
            p.wait(v, "ve")
        p.inc(v.scalar_tensor_tensor(mlead[:, :], ecol[:, :], EPS_WKV, mlead[:, :],
                                     ALU.mult, ALU.add), "ve")
        p.wait(v, "ve")
        v.memset(laccs[:, :, 0], 0.0)
        p.inc(v.memset(gaccs[:, :, 0], 0.0), "ve")
        p.wait(v, "ve")

    for j in range(NCORE):
        a, b = j % 2, 1 - j % 2

        @blk.scalar
        def _(sc, j=j, a=a):
            p.wait(sc, "ve")
            p.inc(sc.activation(ecol[:, :], laccs[:, :, a], AF.Exp), "se")

        @blk.vector
        def _(v, j=j, a=a, b=b):
            p.wait(v, "se")
            p.inc(v.tensor_tensor(elin[:, :], ecol[:, :], tsd[:, j, :, 3], ALU.mult),
                  "ve")
            p.wait(v, "ve")
            p.inc(v.scalar_tensor_tensor(gaccs[:, :, b], elin[:, :], pmc[:, j:j + 1],
                                         gaccs[:, :, a], ALU.mult, ALU.add), "ve")
            p.inc(v.scalar_tensor_tensor(laccs[:, :, b], tsd[:, j, :, 1],
                                         pmc[:, j:j + 1], laccs[:, :, a],
                                         ALU.mult, ALU.add), "ve")
            p.wait(v, "ve")

    @blk.vector
    def _(v):
        p.wait(v, "ve")
        v.tensor_copy(lin[:, :], laccs[:, :, NCORE % 2])
        p.inc(v.tensor_copy(ain[:, :], gaccs[:, :, NCORE % 2]), "ve")

    @blk.scalar
    def _(sc):
        p.wait(sc, "ve")
        p.inc(sc.activation(elin[:, :], lin[:, :], AF.Exp), "se")
    prefix_done_ve = p.cnt["ve"]
    prefix_done_se = p.cnt["se"]

    def fmv(dram):
        return dram.rearrange("p (c t) -> p c t", c=NCH)

    slab_ve = 0
    slab_se = 0
    slab_dma = 0
    for s in range(NSLAB):
        @blk.sync
        def _(sy, s=s, sv=slab_ve):
            p.wait(sy, "ve", sv)
            p.inc(sy.dma_start(out=kslab[:, :, :], in_=fmv(kc)[:, :, bts(s, SLAB)]),
                  "dl", 16)
            p.inc(sy.dma_start(out=vslab[:, :, :], in_=fmv(vc)[:, :, bts(s, SLAB)]),
                  "dl", 16)
            lo = s * SLAB - 1
            if s == 0:
                p.inc(sy.dma_start(out=mext[:, :, 1:SLAB + 1],
                                   in_=fmv(mf)[:, :, 0:SLAB]), "dl", 16)
            else:
                p.inc(sy.dma_start(out=mext[:, :, :],
                                   in_=fmv(mf)[:, :, lo:lo + SLAB + 1]), "dl", 16)
            src = xp[s * SLAB:(s + 1) * SLAB, :].rearrange("(a p) c -> p a c", p=128)
            p.inc(sy.dma_start(out=x_tm[:, :, :], in_=src), "dl", 16)
            p.wait(sy, "dl")

        @blk.gpsimd
        def _(g, s=s, sv=slab_ve):
            p.wait(g, "ve", sv)
            p.inc(g.dma_start(out=rslab[:, :, :], in_=fmv(rc)[:, :, bts(s, SLAB)]),
                  "dp", 16)
            lo = s * SLAB - 1
            if s == 0:
                p.inc(g.dma_start(out=llext[:, :, 1:SLAB + 1],
                                  in_=fmv(llf)[:, :, 0:SLAB]), "dp", 16)
                p.inc(g.dma_start(out=glext[:, :, 1:SLAB + 1],
                                  in_=fmv(glf)[:, :, 0:SLAB]), "dp", 16)
            else:
                p.inc(g.dma_start(out=llext[:, :, :],
                                  in_=fmv(llf)[:, :, lo:lo + SLAB + 1]), "dp", 16)
                p.inc(g.dma_start(out=glext[:, :, :],
                                  in_=fmv(glf)[:, :, lo:lo + SLAB + 1]), "dp", 16)
            p.wait(g, "dp")
        ldk = p.cnt["dl"]
        ldr = p.cnt["dp"]

        @blk.vector
        def _(v, ldk=ldk):
            p.wait(v, "dl", ldk)

        @blk.scalar
        def _(sc, ldk=ldk):
            p.wait(sc, "dl", ldk)

        xs = [x_tm[:, i, :] for i in range(NTS)]
        emit_ln_batch(p, blk, xs, xs, st, st2, junk, wv=[("dl", ldk)])
        h0_done = p.cnt["ve"]

        @blk.vector
        def _(v, s=s, ldr=ldr):
            p.wait(v, "se", prefix_done_se)
            p.wait(v, "dp", ldr)
            if s == 0:
                v.tensor_copy(mext[:, :, 0:1],
                              mlead[:, :].rearrange("p (c o) -> p c o", o=1))
                v.memset(llext[:, :, 0:1], 0.0)
                p.inc(v.memset(glext[:, :, 0:1], 0.0), "ve")
                p.wait(v, "ve")
            inst = None
            for dc in range(NCH):
                inst = v.tensor_scalar(llext[:, dc, :], llext[:, dc, :],
                                       lin[:, dc:dc + 1], None, ALU.add)
            p.inc(inst, "ve")
            p.wait(v, "ve")
            inst = None
            for dc in range(NCH):
                inst = v.tensor_scalar(glext[:, dc, :], glext[:, dc, :],
                                       elin[:, dc:dc + 1], ain[:, dc:dc + 1],
                                       ALU.mult, ALU.add)
            p.inc(inst, "ve")
            p.wait(v, "ve")
            p.wait(v, "dp", ldr)
            inst = None
            for dc in range(NCH):
                inst = v.tensor_scalar(kslab[:, dc, :], kslab[:, dc, :],
                                       ucol[:, dc:dc + 1], None, ALU.mult)
            p.inc(inst, "ve")
            p.wait(v, "ve")

        @blk.scalar
        def _(sc, s=s):
            p.wait(sc, "ve")
            p.inc(sc.activation(sA[:, :, :], llext[:, :, 0:SLAB], AF.Exp,
                                scale=-1.0), "se")

        @blk.vector
        def _(v, s=s):
            p.wait(v, "se")
            fl = lambda t3: t3.rearrange("p c t -> p (c t)")
            p.inc(v.tensor_tensor(sA[:, :, :], glext[:, :, 0:SLAB], sA[:, :, :],
                                  ALU.mult), "ve")
            p.wait(v, "ve")
            p.inc(v.tensor_tensor(vslab[:, :, :], kslab[:, :, :], vslab[:, :, :],
                                  ALU.mult), "ve")
            p.wait(v, "ve")
            p.inc(v.tensor_tensor(sA[:, :, :], mext[:, :, 0:SLAB], sA[:, :, :],
                                  ALU.mult), "ve")
            p.wait(v, "ve")
            p.inc(v.tensor_tensor(sA[:, :, :], sA[:, :, :], vslab[:, :, :],
                                  ALU.add), "ve")
            p.wait(v, "ve")
            p.inc(v.tensor_tensor(sB[:, :, :], mext[:, :, 0:SLAB], kslab[:, :, :],
                                  ALU.max), "ve")
            p.wait(v, "ve")
            p.inc(v.tensor_tensor(kslab[:, :, :], mext[:, :, 0:SLAB], kslab[:, :, :],
                                  ALU.add), "ve")
            p.wait(v, "ve")
            p.inc(v.scalar_tensor_tensor(sB[:, :, :], sB[:, :, :], EPS_WKV,
                                         kslab[:, :, :], ALU.mult, ALU.add), "ve")
            p.wait(v, "ve")

        @blk.scalar
        def _(sc, s=s):
            p.wait(sc, "ve")
            p.inc(sc.activation(sC[:, :, :], sB[:, :, :], AF.Ln), "se")
            p.wait(sc, "se")
            p.inc(sc.activation(sB[:, :, :], sC[:, :, :], AF.Exp, scale=-1.0), "se")

        @blk.vector
        def _(v, s=s):
            p.wait(v, "se")
            p.inc(v.tensor_tensor(sA[:, :, :], sA[:, :, :], sB[:, :, :], ALU.mult),
                  "ve")
            p.wait(v, "ve")
            p.inc(v.tensor_tensor(sA[:, :, :], sA[:, :, :], rslab[:, :, :],
                                  ALU.mult), "ve")
            p.wait(v, "ve")
        u1_done = p.cnt["ve"]

        for i in range(NTS):
            @blk.tensor
            def _(t, s=s, i=i, ud=u1_done, hd=h0_done, sv=slab_ve):
                p.wait(t, "ve", ud)
                p.wait(t, "ve", sv)
                inst = None
                for dc in range(NCH):
                    bankap = ptf if dc < 4 else ptg
                    inst = t.transpose(bankap[:, bts(dc % 4, 128)],
                                       sA[:, dc, bts(i, 128)], identf[:])
                p.inc(inst, "pe")
            tpd = p.cnt["pe"]

            @blk.vector
            def _(v, s=s, i=i, tpd=tpd, hd=h0_done, dmap=slab_dma):
                p.wait(v, "pe", tpd)
                p.wait(v, "ve", hd)
                p.wait(v, "dg", dmap)
                v.tensor_tensor(z1_tm[:, i, 0:512], ptf[:, :], x_tm[:, i, 0:512],
                                ALU.add)
                p.inc(v.tensor_tensor(z1_tm[:, i, 512:1024], ptg[:, :],
                                      x_tm[:, i, 512:1024], ALU.add), "ve")
            slab_ve = p.cnt["ve"]

        @blk.gpsimd
        def _(g, s=s, sv=slab_ve):
            p.wait(g, "ve", sv)
            dst = o_z1[s * SLAB:(s + 1) * SLAB, :].rearrange("(a p) c -> p a c", p=128)
            p.inc(g.dma_start(out=dst, in_=z1_tm[:, :, :]), "dg", 16)
        slab_dma = p.cnt["dg"]
        slab_ve = p.cnt["ve"]

    @blk.gpsimd
    def _(g):
        p.wait(g, "dg")

    return p


def run_l4(inputs, x_rows, r1, r2, r3):
    p = build_launch4()
    ta_all = np.concatenate([np.asarray(r1[j]["o_ta"]) for j in range(NCORE)], axis=0)
    tb_all = np.concatenate([np.asarray(r2[j]["o_tb"]) for j in range(NCORE)], axis=0)
    td_all = np.concatenate([np.asarray(r3[j]["o_td"]) for j in range(NCORE)], axis=0)
    urow = np.ascontiguousarray(
        inputs["L0_attn_time_first"].astype(np.float32).reshape(1, C))
    maps = []
    for c in range(NCORE):
        maps.append(dict(
            xp=np.ascontiguousarray(x_rows[c * TL:(c + 1) * TL]),
            kc=np.asarray(r1[c]["o_k"]), vc=np.asarray(r1[c]["o_v"]),
            rc=np.asarray(r1[c]["o_r"]), mf=np.asarray(r3[c]["o_m"]),
            llf=np.asarray(r3[c]["o_ll"]), glf=np.asarray(r3[c]["o_gl"]),
            ta_all=ta_all, tb_all=tb_all, td_all=td_all, pmask=_pmask(c),
            urow=urow))
    return run_bass_kernel_spmd(p.nc, maps, core_ids=list(range(NCORE))).results


def emit_phase5(p, blk, z1d, z1p, wkc, wrc, prow, sflag, o_s, o_rc):
    NIC = 4 * NCH
    identb = p.sb("identb", [128, 128], BF16)
    identf = p.sb("identf", [128, 128], F32)
    wstage = p.sb("wstage", [128, C])
    wbf = p.sb("wbf", [128, C], BF16)
    wkcT = p.sb("wkcT", [128, NCH, 4 * C], BF16)
    wrcT = p.sb("wrcT", [128, NCH, C], BF16)
    ckc = p.sb("ckc", [128, NIC])
    crc = p.sb("crc", [128, NCH])
    pcol = p.sb("pcol", [128, NCH, 8])
    pstage = p.sb("pstage", [128, NCH * 128])
    b2cb = p.sb("b2cb", [128, NCH], BF16)
    halo2 = p.sb("halo2", [1, C])
    zn_h = p.sb("zn_h", [1, C])
    gr = p.sb("gr", [1, C])
    nbg = p.sb("nbg", [1, C])
    g2row = p.sb("g2row", [1, C])
    b2row = p.sb("b2row", [1, C])
    sfc = p.sb("sfc", [128, 2])
    ones1 = p.sb("ones1", [1, 128])
    znf = p.sb("znf", [128, NCH, 1 + SLAB], BF16)
    colt = p.sb("colt", [128, NCH])
    x_tm = p.sb("x_tm", [128, NTS, C])
    zn_tm = p.sb("zn_tm", [128, NTS, C], BF16)
    st = p.sb("st", [128, NTS, 4])
    st2 = p.sb("st2", [128, NTS, 4])
    junk = wstage
    dts = p.sb("dts", [128, NCH, SLAB], BF16)
    tsk = p.sb("tsk", [128, NCH, SLAB], BF16)
    tsr = p.sb("tsr", [128, NCH, SLAB], BF16)
    sout = p.sb("sout", [128, NIC, SLAB], BF16)
    stmp = p.sb("stmp", [128, SLAB])
    rcout = dts

    pbank = [p.ps(f"pb{i}", [128, 512]) for i in range(4)]
    ptb = p.ps("ptb", [128, 1024], BF16)
    ptf = p.ps("ptf", [128, 512], F32)

    emit_identity(p, blk, identb)
    emit_identity(p, blk, identf)

    @blk.gpsimd
    def _(g):
        p.inc(g.memset(pstage[:, :], 0.0), "po")
        p.inc(g.memset(ones1[:], 1.0), "po")

    @blk.sync
    def _(sy):
        p.wait(sy, "po")
        p.inc(sy.dma_start(out=pstage[0:4, :], in_=prow[:, :]), "dl", 16)
        p.inc(sy.dma_start(out=zn_h[:], in_=z1p), "dl", 16)
        p.inc(sy.dma_start(out=sfc[0:1, 0:1], in_=sflag), "dl", 16)
        p.inc(sy.dma_start(out=g2row[:], in_=prow[0:1, :]), "dl", 16)
        p.inc(sy.dma_start(out=b2row[:], in_=prow[1:2, :]), "dl", 16)
        p.wait(sy, "dl")
    ld_par = p.cnt["dl"]

    @blk.tensor
    def _(t):
        p.wait(t, "dl", ld_par)
        p.wait(t, "po")
        p.inc(t.matmul(ptf[:, 0:1], ones1[:], sfc[0:1, 0:1], start=True, stop=True),
              "pe")

    @blk.vector
    def _(v):
        p.wait(v, "pe")
        p.inc(v.tensor_copy(sfc[:, 0:1], ptf[:, 0:1]), "ve")
        p.wait(v, "ve")
        p.inc(v.tensor_scalar(sfc[:, 1:2], sfc[:, 0:1], -1.0, 1.0, ALU.mult, ALU.add),
              "ve")

    @blk.vector
    def _(v):
        p.wait(v, "dl", ld_par)
        p.inc(v.reciprocal(gr[:], g2row[:]), "ve")
        p.wait(v, "ve")
        p.inc(v.scalar_tensor_tensor(nbg[:], b2row[:], -1.0, gr[:],
                                     ALU.mult, ALU.mult), "ve")
        p.wait(v, "ve")
        p.inc(v.tensor_copy(pstage[32:33, :], nbg[:]), "ve")

    emit_ln_batch(p, blk, [zn_h[:, :]], [halo2[:, :]], st, st2, junk,
                  wv=[("dl", ld_par)])

    @blk.vector
    def _(v):
        p.wait(v, "ve")
        p.inc(v.tensor_copy(pstage[64:65, :], halo2[:, :]), "ve")
    pstage_ready = p.cnt["ve"]

    ev_ptf = [p.cnt["ve"]]
    for dc in range(NCH):
        @blk.tensor
        def _(t, dc=dc):
            p.wait(t, "ve", pstage_ready)
            p.wait(t, "ve", ev_ptf[0])
            p.inc(t.transpose(ptf[:, 0:128], pstage[:, bts(dc, 128)], identf[:]),
                  "pe")
        tpd = p.cnt["pe"]

        @blk.vector
        def _(v, dc=dc, tpd=tpd):
            p.wait(v, "pe", tpd)
            v.tensor_copy(pcol[:, dc, 0:4], ptf[:, 0:4])
            v.tensor_copy(pcol[:, dc, 5:6], ptf[:, 32:33])
            ev_ptf[0] = p.inc(v.tensor_copy(pcol[:, dc, 6:7], ptf[:, 64:65]), "ve")

    @blk.vector
    def _(v):
        p.wait(v, "ve")
        inst = None
        for dc in range(NCH):
            inst = v.tensor_tensor(colt[:, dc:dc + 1], pcol[:, dc, 6:7],
                                   pcol[:, dc, 5:6], ALU.subtract)
        p.inc(inst, "ve")
        p.wait(v, "ve")
        for dc in range(NCH):
            inst = v.scalar_tensor_tensor(znf[:, dc, 0:1], colt[:, dc:dc + 1],
                                          sfc[:, 1:2], pcol[:, dc, 5:6],
                                          ALU.mult, ALU.add)
        p.inc(inst, "ve")
        p.wait(v, "ve")
        p.inc(v.tensor_copy(b2cb[:, :], pcol[:, :, 1]), "ve")
    b2_ready = p.cnt["ve"]

    cast_done = 0
    tp_done = 0
    ev_ptb = p.cnt["ve"]
    wspec = [("kc", wkc, wkcT, 4 * NCH), ("rc", wrc, wrcT, NCH)]
    se_pre = p.cnt["se"]
    ve_pre = p.cnt["ve"]
    for nm, wd, wTd, noc in wspec:
        for oc in range(noc):
            @blk.sync
            def _(sy, wd=wd, oc=oc, cd=cast_done):
                p.wait(sy, "ve", max(cd, ve_pre))
                p.wait(sy, "se", se_pre)
                p.inc(sy.dma_start(out=wstage[:, :], in_=wd[bts(oc, 128), :]),
                      "dl", 16)
                p.wait(sy, "dl")
            ldw = p.cnt["dl"]

            @blk.vector
            def _(v, ldw=ldw, tpd=tp_done):
                p.wait(v, "dl", ldw)
                p.wait(v, "pe", tpd)
                p.inc(v.tensor_copy(wbf[:, :], wstage[:, :]), "ve")
            cast_done = p.cnt["ve"]

            @blk.tensor
            def _(t, cd=cast_done, evp=ev_ptb):
                p.wait(t, "ve", cd)
                p.wait(t, "ve", evp)
                inst = None
                for dc in range(NCH):
                    inst = t.transpose(ptb[:, bts(dc, 128)], wbf[:, bts(dc, 128)],
                                       identb[:])
                p.inc(inst, "pe")
            tp_done = p.cnt["pe"]

            @blk.vector
            def _(v, wTd=wTd, oc=oc, tpd=tp_done):
                p.wait(v, "pe", tpd)
                p.inc(v.tensor_copy(wTd[:, :, bts(oc, 128)],
                                    ptb[:, :].rearrange("p (c j) -> p c j", c=NCH)),
                      "ve")
            ev_ptb = p.cnt["ve"]
    wT_ready = p.cnt["ve"]

    @blk.tensor
    def _(t):
        p.wait(t, "ve", max(wT_ready, b2_ready))
        p.wait(t, "ve", ev_ptf[0])
        inst = None
        for oc in range(4 * NCH):
            for dc in range(NCH):
                inst = t.matmul(ptf[:, oc:oc + 1], wkcT[:, dc, bts(oc, 128)],
                                b2cb[:, dc:dc + 1],
                                start=(dc == 0), stop=(dc == NCH - 1))
        for oc in range(NCH):
            for dc in range(NCH):
                inst = t.matmul(ptf[:, 40 + oc:41 + oc], wrcT[:, dc, bts(oc, 128)],
                                b2cb[:, dc:dc + 1],
                                start=(dc == 0), stop=(dc == NCH - 1))
        p.inc(inst, "pe")
    cvec_done = p.cnt["pe"]

    @blk.vector
    def _(v):
        p.wait(v, "pe", cvec_done)
        v.tensor_copy(ckc[:, :], ptf[:, 0:4 * NCH])
        p.inc(v.tensor_copy(crc[:, :], ptf[:, 40:40 + NCH]), "ve")
        p.wait(v, "ve")
        inst = None
        for dc in range(NCH):
            inst = v.tensor_scalar(wkcT[:, dc, :], wkcT[:, dc, :], pcol[:, dc, 0:1],
                                   None, ALU.mult)
        for dc in range(NCH):
            inst = v.tensor_scalar(wrcT[:, dc, :], wrcT[:, dc, :], pcol[:, dc, 0:1],
                                   None, ALU.mult)
        p.inc(inst, "ve")
    prep_done = p.cnt["ve"]

    bank_ev = [p.cnt["se"]] * 4
    bank_ev_v = [p.cnt["ve"]] * 4
    slab_ln_ve = 0
    slab_ln_se = 0
    slab_tp = 0
    slab_mm = 0
    slab_dma = 0
    for s in range(NSLAB):
        @blk.sync
        def _(sy, s=s, lv=slab_ln_ve, ls=slab_ln_se):
            p.wait(sy, "ve", lv)
            p.wait(sy, "se", ls)
            src = z1d[s * SLAB:(s + 1) * SLAB, :].rearrange("(a p) c -> p a c", p=128)
            p.inc(sy.dma_start(out=x_tm[:, :, :], in_=src), "dl", 16)
            p.wait(sy, "dl")
        ld_x = p.cnt["dl"]

        @blk.vector
        def _(v, s=s, ld_x=ld_x, tp=slab_tp):
            p.wait(v, "dl", ld_x)
            p.wait(v, "pe", tp)
            if s > 0:
                p.inc(v.tensor_copy(znf[:, :, 0:1], znf[:, :, SLAB:SLAB + 1]), "ve")
                p.wait(v, "ve")

        @blk.scalar
        def _(sc, ld_x=ld_x):
            p.wait(sc, "dl", ld_x)

        xs = [x_tm[:, i, :] for i in range(NTS)]
        zns = [zn_tm[:, i, :] for i in range(NTS)]
        emit_ln_batch(p, blk, xs, zns, st, st2, junk,
                      wv=[("dl", ld_x), ("ve", prep_done)])
        slab_ln_ve = p.cnt["ve"]
        slab_ln_se = p.cnt["se"]
        zn_ready = p.cnt["ve"]

        for i in range(NTS):
            @blk.tensor
            def _(t, i=i, znr=zn_ready, evp=ev_ptb):
                p.wait(t, "ve", znr)
                p.wait(t, "ve", evp)
                inst = None
                for dc in range(NCH):
                    inst = t.transpose(ptb[:, bts(dc, 128)],
                                       zn_tm[:, i, bts(dc, 128)], identb[:])
                p.inc(inst, "pe")
            tpd = p.cnt["pe"]

            @blk.vector
            def _(v, i=i, tpd=tpd):
                p.wait(v, "pe", tpd)
                dst = znf[:, :, 1 + i * 128: 1 + (i + 1) * 128]
                p.inc(v.tensor_copy(dst,
                                    ptb[:, :].rearrange("p (c j) -> p c j", c=NCH)),
                      "ve")
            ev_ptb = p.cnt["ve"]
        slab_tp = p.cnt["pe"]
        znf_ready = p.cnt["ve"]

        @blk.vector
        def _(v, s=s, zr=znf_ready, mm=slab_mm, dmap=slab_dma):
            p.wait(v, "ve", zr)
            p.wait(v, "pe", mm)
            p.wait(v, "dg", dmap)
            cur = slice(1, 1 + SLAB)
            prv = slice(0, SLAB)
            inst = None
            for dc in range(NCH):
                inst = v.tensor_tensor(dts[:, dc, :], znf[:, dc, cur],
                                       znf[:, dc, prv], ALU.subtract)
            p.inc(inst, "ve")
            p.wait(v, "ve")
            for wi, buf in ((2, tsk), (3, tsr)):
                for dc in range(NCH):
                    inst = v.scalar_tensor_tensor(buf[:, dc, :], dts[:, dc, :],
                                                  pcol[:, dc, wi:wi + 1],
                                                  znf[:, dc, prv], ALU.mult, ALU.add)
            p.inc(inst, "ve")
        ts_ready = p.cnt["ve"]

        stmp_se = [p.cnt["se"]]
        gi = 0
        for oc in range(4 * NCH):
            bk = gi % 4
            gi += 1

            @blk.tensor
            def _(t, oc=oc, bk=bk, s=s, gi=gi, tsrd=ts_ready, dmap=slab_dma):
                p.wait(t, "ve", tsrd)
                p.wait(t, "se", bank_ev[bk])
                p.wait(t, "ve", bank_ev_v[bk])
                if gi == 1:
                    p.wait(t, "dg", dmap)
                inst = None
                for dc in range(NCH):
                    inst = t.matmul(pbank[bk][:, :], wkcT[:, dc, bts(oc, 128)],
                                    tsk[:, dc, :], start=(dc == 0),
                                    stop=(dc == NCH - 1))
                p.inc(inst, "pe")
            mmd = p.cnt["pe"]

            @blk.vector
            def _(v, oc=oc, bk=bk, mmd=mmd, pd=prep_done, ss=stmp_se[0]):
                p.wait(v, "pe", mmd)
                p.wait(v, "ve", pd)
                p.wait(v, "se", ss)
                bank_ev_v[bk] = p.inc(
                    v.tensor_scalar(stmp[:, :], pbank[bk][:, :], ckc[:, oc:oc + 1],
                                    0.0, ALU.add, ALU.max), "ve")

            @blk.scalar
            def _(sc, oc=oc, bk=bk):
                p.wait(sc, "ve")
                bank_ev[bk] = p.inc(
                    sc.activation(sout[:, oc, :], stmp[:, :], AF.Square), "se")
            stmp_se[0] = p.cnt["se"]

        for oc in range(NCH):
            bk = gi % 4
            gi += 1

            @blk.tensor
            def _(t, oc=oc, bk=bk, tsrd=ts_ready):
                p.wait(t, "ve", tsrd)
                p.wait(t, "se", bank_ev[bk])
                p.wait(t, "ve", bank_ev_v[bk])
                inst = None
                for dc in range(NCH):
                    inst = t.matmul(pbank[bk][:, :], wrcT[:, dc, bts(oc, 128)],
                                    tsr[:, dc, :], start=(dc == 0),
                                    stop=(dc == NCH - 1))
                p.inc(inst, "pe")
            mmd = p.cnt["pe"]

            @blk.scalar
            def _(sc, oc=oc, bk=bk, mmd=mmd, pd=prep_done):
                p.wait(sc, "pe", mmd)
                p.wait(sc, "ve", pd)
                bank_ev[bk] = p.inc(
                    sc.activation(rcout[:, oc, :], pbank[bk][:, :], AF.Sigmoid,
                                  bias=crc[:, oc:oc + 1]), "se")
        slab_mm = p.cnt["pe"]
        ev_all = p.cnt["se"]

        @blk.gpsimd
        def _(g, s=s, ev=ev_all):
            p.wait(g, "se", ev)
            p.wait(g, "ve")
            cur = bts(s, SLAB)
            p.inc(g.dma_start(
                out=o_s.rearrange("p (c t) -> p c t", c=4 * NCH)[:, :, cur],
                in_=sout[:, :, :]), "dg", 16)
            p.inc(g.dma_start(
                out=o_rc.rearrange("p (c t) -> p c t", c=NCH)[:, :, cur],
                in_=rcout[:, :, :]), "dg", 16)
        slab_dma = p.cnt["dg"]


def run_l5(inputs, r4):
    p = build_launch5()
    prow = np.ascontiguousarray(np.stack(
        [inputs["L0_attn_ln2_g"], inputs["L0_attn_ln2_b"],
         inputs["L0_attn_cmix_k"], inputs["L0_attn_cmix_r"]]).astype(np.float32))
    maps = []
    for c in range(NCORE):
        sf = np.array([[1.0 if _seq_start(c) else 0.0]], np.float32)
        z1prev = (np.asarray(r4[c - 1]["o_z1"])[TL - 1:TL]
                  if not _seq_start(c) else np.asarray(r4[c]["o_z1"])[0:1])
        maps.append(dict(
            z1d=np.asarray(r4[c]["o_z1"]), z1p=np.ascontiguousarray(z1prev),
            wkc=np.ascontiguousarray(inputs["L0_attn_Wk_c"]),
            wrc=np.ascontiguousarray(inputs["L0_attn_Wr_c"]),
            prow=prow, sflag=sf))
    return run_bass_kernel_spmd(p.nc, maps, core_ids=list(range(NCORE))).results


def emit_phase6(p, blk, z1d, xp, sdat, rcdat, wvc, wff, bffr, o_x2):
    SL6 = 256
    NS6 = TL // SL6
    NT6 = SL6 // 128
    NIC = 4 * NCH

    identb = p.sb("identb", [128, 128], BF16)
    identf = p.sb("identf", [128, 128], F32)
    wstage = p.sb("wstage", [128, 4 * C])
    wbf = p.sb("wbf", [128, 4 * C], BF16)
    wvcT = p.sb("wvcT", [128, NIC, C], BF16)
    wqT = p.sb("wqT", [128, NCH, C], BF16)
    wsc = p.sb("wsc", [128, NCH])
    wrq = p.sb("wrq", [128, NCH])
    bb = p.sb("bb", [128, C])
    ones1 = p.sb("ones1", [1, 128])
    bfr = p.sb("bfr", [1, C])
    k0 = p.sb("k0", [128, 1]); k1 = p.sb("k1", [128, 1])
    kap = p.sb("kap", [128, 1]); lam = p.sb("lam", [128, 1])
    r1t = p.sb("r1t", [128, C]); r2t = p.sb("r2t", [128, C]); r3t = p.sb("r3t", [128, C])
    sslab = p.sb("sslab", [128, NIC, SL6], BF16)
    rcslab = p.sb("rcslab", [128, NCH, SL6])
    z1_tm = p.sb("z1_tm", [128, NT6, C])
    x_tm = p.sb("x_tm", [128, NT6, C])
    x1_tm = p.sb("x1_tm", [128, NT6, C])
    h_tm = p.sb("h_tm", [128, NT6, C])
    xq_tm = p.sb("xq_tm", [128, NT6, C], BF16)
    xqT = p.sb("xqT", [128, NCH, SL6], BF16)
    u2 = p.sb("u2", [128, NCH, SL6])
    ffs = p.sb("ffs", [128, NCH, SL6])
    x2_tm = z1_tm
    st = p.sb("st", [128, NT6, 4])
    st2 = p.sb("st2", [128, NT6, 4])
    stq = p.sb("stq", [128, NT6, 4])
    junk = wstage

    pbank = [p.ps(f"pb{i}", [128, 512]) for i in range(4)]
    ptb = p.ps("ptb", [128, 1024], BF16)
    ptf = p.ps("ptf", [128, 512], F32)
    ptg = p.ps("ptg", [128, 512], F32)

    emit_identity(p, blk, identb)
    emit_identity(p, blk, identf)

    @blk.gpsimd
    def _(g):
        p.inc(g.memset(ones1[:], 1.0), "po")

    @blk.sync
    def _(sy):
        p.inc(sy.dma_start(out=bfr[:], in_=bffr), "dl", 16)
        p.wait(sy, "dl")
    ldb = p.cnt["dl"]

    @blk.vector
    def _(v):
        p.wait(v, "dl", ldb)
        p.inc(v.tensor_scalar(k0[0:1, 0:1], bfr[0:1, 0:1], 0.0, 0.5,
                              ALU.mult, ALU.add), "ve")
        p.wait(v, "ve")
        p.inc(v.tensor_scalar(k1[0:1, 0:1], k0[0:1, 0:1], C23, -C23,
                              ALU.add, ALU.add), "ve")
        p.wait(v, "ve")
        p.inc(v.tensor_scalar(kap[0:1, 0:1], k1[0:1, 0:1], 2.0, 0.0,
                              ALU.mult, ALU.add), "ve")
        p.wait(v, "ve")
        p.inc(v.tensor_scalar(lam[0:1, 0:1], kap[0:1, 0:1], -1.0, 1.0,
                              ALU.mult, ALU.add), "ve")
        p.wait(v, "ve")
    @blk.tensor
    def _(t):
        p.wait(t, "ve")
        p.wait(t, "po")
        t.matmul(ptf[:, 0:1], ones1[:], kap[0:1, 0:1], start=True, stop=True)
        p.inc(t.matmul(ptf[:, 1:2], ones1[:], lam[0:1, 0:1], start=True, stop=True),
              "pe")

    @blk.vector
    def _(v):
        p.wait(v, "pe")
        v.tensor_copy(kap[:, :], ptf[:, 0:1])
        p.inc(v.tensor_copy(lam[:, :], ptf[:, 1:2]), "ve")
    kap_ready = p.cnt["ve"]

    @blk.tensor
    def _(t):
        p.wait(t, "ve", kap_ready)
        t.matmul(ptf[:, 0:512], ones1[:], bfr[0:1, 0:512], start=True, stop=True)
        p.inc(t.matmul(ptg[:, 0:512], ones1[:], bfr[0:1, 512:1024],
                       start=True, stop=True), "pe")
    bbd = p.cnt["pe"]

    @blk.vector
    def _(v, bbd=bbd):
        p.wait(v, "pe", bbd)
        v.tensor_copy(bb[:, 0:512], ptf[:, :])
        p.inc(v.tensor_copy(bb[:, 512:1024], ptg[:, :]), "ve")
    bb_ready = p.cnt["ve"]

    cast_done = 0
    tp_done = 0
    ev_ptb = p.cnt["ve"]
    for oc in range(NCH):
        @blk.sync
        def _(sy, oc=oc, cd=cast_done):
            p.wait(sy, "ve", cd)
            p.inc(sy.dma_start(out=wstage[:, :], in_=wvc[bts(oc, 128), :]),
                  "dl", 16)
            p.wait(sy, "dl")
        ldw = p.cnt["dl"]

        @blk.vector
        def _(v, ldw=ldw, tpd=tp_done):
            p.wait(v, "dl", ldw)
            p.wait(v, "pe", tpd)
            p.inc(v.tensor_copy(wbf[:, :], wstage[:, :]), "ve")
        cast_done = p.cnt["ve"]

        for grp in range(4):
            @blk.tensor
            def _(t, oc=oc, grp=grp, cd=cast_done, evp=ev_ptb):
                p.wait(t, "ve", cd)
                p.wait(t, "ve", evp)
                inst = None
                for j in range(NCH):
                    ic = grp * NCH + j
                    inst = t.transpose(ptb[:, bts(j, 128)], wbf[:, bts(ic, 128)],
                                       identb[:])
                p.inc(inst, "pe")
            tp_done = p.cnt["pe"]

            @blk.vector
            def _(v, oc=oc, grp=grp, tpd=tp_done):
                p.wait(v, "pe", tpd)
                dst = wvcT[:, grp * NCH:(grp + 1) * NCH, bts(oc, 128)]
                p.inc(v.tensor_copy(dst,
                                    ptb[:, :].rearrange("p (c j) -> p c j", c=NCH)),
                      "ve")
            ev_ptb = p.cnt["ve"]
    wvc_ready = p.cnt["ve"]

    for oc in range(NCH):
        @blk.sync
        def _(sy, oc=oc, cd=cast_done):
            p.wait(sy, "ve", cd)
            p.inc(sy.dma_start(out=wstage[:, 0:C], in_=wff[bts(oc, 128), :]),
                  "dl", 16)
            p.wait(sy, "dl")
        ldw = p.cnt["dl"]

        @blk.vector
        def _(v, oc=oc, ldw=ldw):
            p.wait(v, "dl", ldw)
            p.inc(v.reduce_max(wsc[:, oc:oc + 1], wstage[:, 0:C], AX.X,
                               apply_absolute_value=True), "ve")
            p.wait(v, "ve")
            p.inc(v.tensor_scalar(wsc[:, oc:oc + 1], wsc[:, oc:oc + 1], 1e-8,
                                  1.0 / 127.0, ALU.max, ALU.mult), "ve")
            p.wait(v, "ve")

        @blk.scalar
        def _(sc, oc=oc):
            p.wait(sc, "ve")
            p.inc(sc.activation(wrq[:, oc:oc + 1], wsc[:, oc:oc + 1], AF.Ln), "se")
            p.wait(sc, "se")
            p.inc(sc.activation(wrq[:, oc:oc + 1], wrq[:, oc:oc + 1], AF.Exp,
                                scale=-1.0), "se")

        @blk.vector
        def _(v, oc=oc, kr=kap_ready, tpd=tp_done):
            p.wait(v, "se")
            p.wait(v, "ve", kr)
            p.wait(v, "pe", tpd)
            p.inc(v.tensor_scalar(r1t[:, :], wstage[:, 0:C], wrq[:, oc:oc + 1],
                                  None, ALU.mult), "ve")
            p.wait(v, "ve")
            v.tensor_scalar(r2t[:, :], r1t[:, :], lam[:, 0:1], C23, ALU.mult, ALU.add)
            p.inc(v.tensor_scalar(r3t[:, :], r1t[:, :], kap[:, 0:1], C52,
                                  ALU.mult, ALU.add), "ve")
            p.wait(v, "ve")
            p.inc(v.tensor_single_scalar(r2t[:, :], r2t[:, :], -C23, ALU.add), "ve")
            p.wait(v, "ve")
            p.inc(v.scalar_tensor_tensor(r1t[:, :], r3t[:, :], -C52, r2t[:, :],
                                         ALU.add, ALU.add), "ve")
            p.wait(v, "ve")
            p.inc(v.tensor_scalar(wbf[:, 0:C], r1t[:, :], 127.0, -128.0,
                                  ALU.min, ALU.max), "ve")
        cast_done = p.cnt["ve"]

        @blk.tensor
        def _(t, oc=oc, cd=cast_done, evp=ev_ptb):
            p.wait(t, "ve", cd)
            p.wait(t, "ve", evp)
            inst = None
            for dc in range(NCH):
                inst = t.transpose(ptb[:, bts(dc, 128)], wbf[:, bts(dc, 128)],
                                   identb[:])
            p.inc(inst, "pe")
        tp_done = p.cnt["pe"]

        @blk.vector
        def _(v, oc=oc, tpd=tp_done):
            p.wait(v, "pe", tpd)
            p.inc(v.tensor_copy(wqT[:, :, bts(oc, 128)],
                                ptb[:, :].rearrange("p (c j) -> p c j", c=NCH)),
                  "ve")
        ev_ptb = p.cnt["ve"]
    prep_done = p.cnt["ve"]


    def fmv(dram, nc_):
        return dram.rearrange("p (c t) -> p c t", c=nc_)

    bank_ev_v = [p.cnt["ve"]] * 4
    ev_ptfg = p.cnt["ve"]
    slab_ve = 0
    slab_se = 0
    slab_dma = 0
    for s in range(NS6):
        @blk.sync
        def _(sy, s=s, sv=slab_ve, ss=slab_se, dmap=slab_dma):
            p.wait(sy, "ve", sv)
            p.wait(sy, "se", ss)
            p.wait(sy, "dg", dmap)
            p.inc(sy.dma_start(out=sslab[:, :, :],
                               in_=fmv(sdat, NIC)[:, :, bts(s, SL6)]), "dl", 16)
            p.inc(sy.dma_start(
                out=z1_tm[:, :, :],
                in_=z1d[s * SL6:(s + 1) * SL6, :].rearrange("(a p) c -> p a c", p=128)),
                "dl", 16)
            p.inc(sy.dma_start(
                out=x_tm[:, :, :],
                in_=xp[s * SL6:(s + 1) * SL6, :].rearrange("(a p) c -> p a c", p=128)),
                "dl", 16)
            p.wait(sy, "dl")

        @blk.gpsimd
        def _(g, s=s, sv=slab_ve):
            p.wait(g, "ve", sv)
            p.inc(g.dma_start(out=rcslab[:, :, :],
                              in_=fmv(rcdat, NCH)[:, :, bts(s, SL6)]), "dp", 16)
            p.wait(g, "dp")
        ldk = p.cnt["dl"]
        ldr = p.cnt["dp"]

        for oc in range(NCH):
            bk = oc % 4

            @blk.tensor
            def _(t, oc=oc, bk=bk, ldk=ldk, wr=wvc_ready, bev=bank_ev_v[bk],
                  dmap=slab_dma):
                p.wait(t, "dl", ldk)
                p.wait(t, "ve", wr)
                p.wait(t, "ve", bev)
                if oc == 0:
                    p.wait(t, "dg", dmap)
                inst = None
                for ic in range(NIC):
                    inst = t.matmul(pbank[bk][:, 0:SL6], wvcT[:, ic, bts(oc, 128)],
                                    sslab[:, ic, :], start=(ic == 0),
                                    stop=(ic == NIC - 1))
                p.inc(inst, "pe")
            mmd = p.cnt["pe"]

            @blk.vector
            def _(v, oc=oc, bk=bk, mmd=mmd, ldr=ldr):
                p.wait(v, "pe", mmd)
                p.wait(v, "dp", ldr)
                bank_ev_v[bk] = p.inc(
                    v.tensor_tensor(u2[:, oc, :], pbank[bk][:, 0:SL6],
                                    rcslab[:, oc, :], ALU.mult), "ve")
        u2_done = p.cnt["ve"]

        for i in range(NT6):
            @blk.tensor
            def _(t, i=i, ud=u2_done, evp=ev_ptfg):
                p.wait(t, "ve", ud)
                p.wait(t, "ve", evp)
                inst = None
                for dc in range(NCH):
                    bank = ptf if dc < 4 else ptg
                    inst = t.transpose(bank[:, bts(dc % 4, 128)],
                                       u2[:, dc, bts(i, 128)], identf[:])
                p.inc(inst, "pe")
            tpd = p.cnt["pe"]

            @blk.vector
            def _(v, i=i, tpd=tpd):
                p.wait(v, "pe", tpd)
                v.tensor_tensor(x1_tm[:, i, 0:512], ptf[:, :], z1_tm[:, i, 0:512],
                                ALU.add)
                p.inc(v.tensor_tensor(x1_tm[:, i, 512:1024], ptg[:, :],
                                      z1_tm[:, i, 512:1024], ALU.add), "ve")
                p.wait(v, "ve")
                p.inc(v.tensor_tensor(x1_tm[:, i, :], x1_tm[:, i, :], x_tm[:, i, :],
                                      ALU.add), "ve")
            ev_ptfg = p.cnt["ve"]
        x1_done = p.cnt["ve"]

        x1s = [x1_tm[:, i, :] for i in range(NT6)]
        hs = [h_tm[:, i, :] for i in range(NT6)]
        emit_ln_batch(p, blk, x1s, hs, st, st2, junk, wv=[("ve", x1_done)])
        h_done = p.cnt["ve"]

        @blk.vector
        def _(v, s=s, hd=h_done):
            p.wait(v, "ve", hd)
            inst = None
            for i in range(NT6):
                inst = v.reduce_max(stq[:, i, 0:1], h_tm[:, i, :], AX.X,
                                    apply_absolute_value=True)
            p.inc(inst, "ve")
            p.wait(v, "ve")
            p.inc(v.tensor_scalar(stq[:, 0:NT6, 1:2], stq[:, 0:NT6, 0:1], 1e-8,
                                  1.0 / 127.0, ALU.max, ALU.mult), "ve")

        @blk.scalar
        def _(sc, s=s):
            p.wait(sc, "ve")
            p.inc(sc.activation(stq[:, 0:NT6, 2:3], stq[:, 0:NT6, 1:2], AF.Ln), "se")
            p.wait(sc, "se")
            p.inc(sc.activation(stq[:, 0:NT6, 3:4], stq[:, 0:NT6, 2:3], AF.Exp,
                                scale=-1.0), "se")

        @blk.vector
        def _(v, s=s):
            p.wait(v, "se")
            for i in range(NT6):
                p.inc(v.tensor_scalar(junk[:, 0:C], h_tm[:, i, :], stq[:, i, 3:4],
                                      None, ALU.mult), "ve")
                p.wait(v, "ve")
                v.tensor_scalar(r2t[:, :], junk[:, 0:C], lam[:, 0:1], C23,
                                ALU.mult, ALU.add)
                p.inc(v.tensor_scalar(r3t[:, :], junk[:, 0:C], kap[:, 0:1], C52,
                                      ALU.mult, ALU.add), "ve")
                p.wait(v, "ve")
                p.inc(v.tensor_single_scalar(r2t[:, :], r2t[:, :], -C23, ALU.add),
                      "ve")
                p.wait(v, "ve")
                p.inc(v.scalar_tensor_tensor(r1t[:, :], r3t[:, :], -C52, r2t[:, :],
                                             ALU.add, ALU.add), "ve")
                p.wait(v, "ve")
                p.inc(v.tensor_scalar(xq_tm[:, i, :], r1t[:, :], 127.0, -128.0,
                                      ALU.min, ALU.max), "ve")
                p.wait(v, "ve")
        xq_done = p.cnt["ve"]

        for i in range(NT6):
            @blk.tensor
            def _(t, i=i, xd=xq_done, evp=ev_ptb):
                p.wait(t, "ve", xd)
                p.wait(t, "ve", evp)
                inst = None
                for dc in range(NCH):
                    inst = t.transpose(ptb[:, bts(dc, 128)],
                                       xq_tm[:, i, bts(dc, 128)], identb[:])
                p.inc(inst, "pe")
            tpd = p.cnt["pe"]

            @blk.vector
            def _(v, i=i, tpd=tpd):
                p.wait(v, "pe", tpd)
                dst = xqT[:, :, bts(i, 128)]
                p.inc(v.tensor_copy(dst,
                                    ptb[:, :].rearrange("p (c j) -> p c j", c=NCH)),
                      "ve")
            ev_ptb = p.cnt["ve"]
        xqT_done = p.cnt["ve"]

        for oc in range(NCH):
            bk = oc % 4

            @blk.tensor
            def _(t, oc=oc, bk=bk, xd=xqT_done, pd=prep_done, bev=bank_ev_v[bk]):
                p.wait(t, "ve", xd)
                p.wait(t, "ve", pd)
                p.wait(t, "ve", bev)
                inst = None
                for dc in range(NCH):
                    inst = t.matmul(pbank[bk][:, 0:SL6], wqT[:, dc, bts(oc, 128)],
                                    xqT[:, dc, :], start=(dc == 0),
                                    stop=(dc == NCH - 1))
                p.inc(inst, "pe")
            mmd = p.cnt["pe"]

            @blk.vector
            def _(v, oc=oc, bk=bk, mmd=mmd):
                p.wait(v, "pe", mmd)
                bank_ev_v[bk] = p.inc(
                    v.tensor_scalar(ffs[:, oc, :], pbank[bk][:, 0:SL6],
                                    wsc[:, oc:oc + 1], None, ALU.mult), "ve")
        ff_done = p.cnt["ve"]

        for i in range(NT6):
            @blk.tensor
            def _(t, i=i, fd=ff_done, evp=ev_ptfg):
                p.wait(t, "ve", fd)
                p.wait(t, "ve", evp)
                inst = None
                for dc in range(NCH):
                    bank = ptf if dc < 4 else ptg
                    inst = t.transpose(bank[:, bts(dc % 4, 128)],
                                       ffs[:, dc, bts(i, 128)], identf[:])
                p.inc(inst, "pe")
            tpd = p.cnt["pe"]

            @blk.vector
            def _(v, i=i, tpd=tpd):
                p.wait(v, "pe", tpd)
                v.tensor_scalar(x2_tm[:, i, 0:512], ptf[:, :], stq[:, i, 1:2],
                                None, ALU.mult)
                p.inc(v.tensor_scalar(x2_tm[:, i, 512:1024], ptg[:, :],
                                      stq[:, i, 1:2], None, ALU.mult), "ve")
                p.wait(v, "ve")
                p.inc(v.tensor_tensor(x2_tm[:, i, :], x2_tm[:, i, :], bb[:, :],
                                      ALU.add), "ve")
                p.wait(v, "ve")
                p.inc(v.tensor_tensor(x2_tm[:, i, :], x2_tm[:, i, :], x1_tm[:, i, :],
                                      ALU.add), "ve")
            ev_ptfg = p.cnt["ve"]
        slab_ve = p.cnt["ve"]
        slab_se = p.cnt["se"]

        @blk.gpsimd
        def _(g, s=s, sv=slab_ve):
            p.wait(g, "ve", sv)
            dst = o_x2[s * SL6:(s + 1) * SL6, :].rearrange("(a p) c -> p a c", p=128)
            p.inc(g.dma_start(out=dst, in_=x2_tm[:, :, :]), "dg", 16)
        slab_dma = p.cnt["dg"]


def run_l6(inputs, x_rows, r4, r5):
    p = build_launch6()
    maps = []
    for c in range(NCORE):
        maps.append(dict(
            z1d=np.asarray(r4[c]["o_z1"]),
            xp=np.ascontiguousarray(x_rows[c * TL:(c + 1) * TL]),
            sdat=np.asarray(r5[c]["o_s"]), rcdat=np.asarray(r5[c]["o_rc"]),
            wvc=np.ascontiguousarray(inputs["L0_attn_Wv_c"]),
            wff=np.ascontiguousarray(inputs["L0_ffn_weight"]),
            bffr=np.ascontiguousarray(
                inputs["L0_ffn_bias"].astype(np.float32).reshape(1, C))))
    return run_bass_kernel_spmd(p.nc, maps, core_ids=list(range(NCORE))).results


def emit_phase7(p, blk, x2d, wgt, brow, o_phiq, o_kvz):

    identb = p.sb("identb", [128, 128], BF16)
    wstage = p.sb("wstage", [128, C])
    wbf = p.sb("wbf", [128, C], BF16)
    wT = {n: p.sb("wT_" + n, [128, NCH, C], BF16) for n in "qkv"}
    brow_q = p.sb("brow_q", [1, C], BF16)
    brow_k = p.sb("brow_k", [1, C], BF16)
    brow_v = p.sb("brow_v", [1, C], BF16)
    onesb = p.sb("onesb", [1, 512], BF16)
    x_tm = p.sb("x_tm", [128, NTS, C])
    h2b = p.sb("h2b", [128, NTS, C], BF16)
    h2T = p.sb("h2T", [128, NCH, SLAB], BF16)
    st = p.sb("st", [128, NTS, 4])
    st2 = p.sb("st2", [128, NTS, 4])
    junk = p.sb("junk", [128, C])
    sA = p.sb("sA", [128, SLAB])
    sB = p.sb("sB", [128, SLAB])
    sC = p.sb("sC", [128, SLAB])
    phiq = p.sb("phiq", [128, NCH, SLAB], BF16)
    phik = p.sb("phik", [128, NTS, C], BF16)
    vtm = p.sb("vtm", [128, NTS, C], BF16)
    vaug = p.sb("vaug", [128, 16, 65], BF16)
    kvzs = p.sb("kvzs", [64, 16, 65])

    pbank = [p.ps(f"pb{i}", [128, 512]) for i in range(3)]
    ptb = p.ps("ptb", [128, 1024], BF16)
    kvzp = [p.ps(f"kvz{i}", [64, 4 * 128]) for i in range(4)]

    emit_identity(p, blk, identb)

    @blk.gpsimd
    def _(g):
        p.inc(g.memset(onesb[:], 1.0), "po")

    @blk.gpsimd
    def _(g):
        for j, dstt in enumerate((brow_q, brow_k, brow_v)):
            p.inc(g.dma_start(out=dstt[0:1, :], in_=brow[j:j + 1, :]), "dp", 16)
        p.wait(g, "dp")
    ld_par = p.cnt["dp"]

    @blk.vector
    def _(v):
        p.inc(v.memset(vaug[:, :, 64:65], 1.0), "ve")
    brows_done = p.cnt["ve"]

    cast_done = 0
    tp_done = 0
    ev_ptb = p.cnt["ve"]
    for n in "qkv":
        for oc in range(NCH):
            @blk.sync
            def _(sy, n=n, oc=oc, cd=cast_done):
                p.wait(sy, "ve", max(cd, brows_done))
                p.wait(sy, "dp", ld_par)
                p.inc(sy.dma_start(out=wstage[:, :], in_=wgt[n][bts(oc, 128), :]),
                      "dl", 16)
                p.wait(sy, "dl")
            ldw = p.cnt["dl"]

            @blk.vector
            def _(v, ldw=ldw, tpd=tp_done):
                p.wait(v, "dl", ldw)
                p.wait(v, "pe", tpd)
                p.inc(v.tensor_copy(wbf[:, :], wstage[:, :]), "ve")
            cast_done = p.cnt["ve"]

            @blk.tensor
            def _(t, cd=cast_done, evp=ev_ptb):
                p.wait(t, "ve", cd)
                p.wait(t, "ve", evp)
                inst = None
                for dc in range(NCH):
                    inst = t.transpose(ptb[:, bts(dc, 128)], wbf[:, bts(dc, 128)],
                                       identb[:])
                p.inc(inst, "pe")
            tp_done = p.cnt["pe"]

            @blk.vector
            def _(v, n=n, oc=oc, tpd=tp_done):
                p.wait(v, "pe", tpd)
                p.inc(v.tensor_copy(wT[n][:, :, bts(oc, 128)],
                                    ptb[:, :].rearrange("p (c j) -> p c j", c=NCH)),
                      "ve")
            ev_ptb = p.cnt["ve"]
    prep_done = p.cnt["ve"]

    bank_ev = [p.cnt["ve"]] * 3
    slab_ln_ve = 0
    slab_ln_se = 0
    slab_tp = 0
    prev_pe = 0
    phiq_dma = 0
    for s in range(NSLAB):
        @blk.sync
        def _(sy, s=s, lv=slab_ln_ve, ls=slab_ln_se):
            p.wait(sy, "ve", lv)
            p.wait(sy, "se", ls)
            src = x2d[s * SLAB:(s + 1) * SLAB, :].rearrange("(a p) c -> p a c", p=128)
            p.inc(sy.dma_start(out=x_tm[:, :, :], in_=src), "dl", 16)
            p.wait(sy, "dl")
        ld_x = p.cnt["dl"]

        @blk.vector
        def _(v, ld_x=ld_x, tp=prev_pe):
            p.wait(v, "dl", ld_x)
            p.wait(v, "pe", tp)

        @blk.scalar
        def _(sc, ld_x=ld_x):
            p.wait(sc, "dl", ld_x)

        xs = [x_tm[:, i, :] for i in range(NTS)]
        h2s = [h2b[:, i, :] for i in range(NTS)]
        emit_ln_batch(p, blk, xs, h2s, st, st2, junk, wv=[("dl", ld_x)])
        slab_ln_ve = p.cnt["ve"]
        slab_ln_se = p.cnt["se"]
        zn_ready = p.cnt["ve"]

        for i in range(NTS):
            @blk.tensor
            def _(t, i=i, znr=zn_ready, evp=ev_ptb):
                p.wait(t, "ve", znr)
                p.wait(t, "ve", evp)
                inst = None
                for dc in range(NCH):
                    inst = t.transpose(ptb[:, bts(dc, 128)],
                                       h2b[:, i, bts(dc, 128)], identb[:])
                p.inc(inst, "pe")
            tpd = p.cnt["pe"]

            @blk.vector
            def _(v, i=i, tpd=tpd):
                p.wait(v, "pe", tpd)
                dst = h2T[:, :, bts(i, 128)]
                p.inc(v.tensor_copy(dst,
                                    ptb[:, :].rearrange("p (c j) -> p c j", c=NCH)),
                      "ve")
            ev_ptb = p.cnt["ve"]
        slab_tp = p.cnt["pe"]
        h2T_ready = p.cnt["ve"]

        for oc in range(NCH):
            bk = oc % 3

            @blk.tensor
            def _(t, oc=oc, bk=bk, hr=h2T_ready, pd=prep_done, bev=bank_ev[bk],
                  pq=phiq_dma):
                p.wait(t, "ve", max(hr, pd))
                p.wait(t, "ve", bev)
                if oc == 0:
                    p.wait(t, "dg", pq)
                for dc in range(NCH):
                    t.matmul(pbank[bk][:, :], wT["q"][:, dc, bts(oc, 128)],
                             h2T[:, dc, :], start=(dc == 0), stop=False)
                inst = t.matmul(pbank[bk][:, :], brow_q[0:1, bts(oc, 128)],
                                onesb[0:1, :], start=False, stop=True)
                p.inc(inst, "pe")
            mmd = p.cnt["pe"]

            @blk.vector
            def _(v, oc=oc, bk=bk, mmd=mmd):
                p.wait(v, "pe", mmd)
                p.wait(v, "se")
                p.wait(v, "ve")
                v.tensor_scalar(sA[:, :], pbank[bk][:, :], 0.0, None, ALU.min)
                p.inc(v.tensor_scalar(sB[:, :], pbank[bk][:, :], 0.0, None, ALU.max),
                      "ve")
                p.wait(v, "ve")

            @blk.scalar
            def _(sc, oc=oc):
                p.wait(sc, "ve")
                p.inc(sc.activation(sC[:, :], sA[:, :], AF.Exp), "se")

            @blk.vector
            def _(v, oc=oc, bk=bk):
                p.wait(v, "se")
                bank_ev[bk] = p.inc(
                    v.tensor_tensor(phiq[:, oc, :], sC[:, :], sB[:, :], ALU.add),
                    "ve")
        for which, dst in (("k", phik), ("v", vtm)):
            for i in range(NTS):
                for hf in range(2):
                    bk = (i * 2 + hf) % 3

                    @blk.tensor
                    def _(t, which=which, i=i, hf=hf, bk=bk, hr=h2T_ready,
                          bev=bank_ev[bk]):
                        p.wait(t, "ve", hr)
                        p.wait(t, "ve", bev)
                        for dc in range(NCH):
                            t.matmul(pbank[bk][:, :],
                                     h2T[:, dc, bts(i, 128)],
                                     wT[which][:, dc, bts(hf, 512)],
                                     start=(dc == 0), stop=False)
                        brt = brow_k if which == "k" else brow_v
                        inst = t.matmul(pbank[bk][:, :], onesb[0:1, 0:128],
                                        brt[0:1, bts(hf, 512)],
                                        start=False, stop=True)
                        p.inc(inst, "pe")
                    mmd = p.cnt["pe"]

                    if which == "k":
                        @blk.vector
                        def _(v, i=i, hf=hf, bk=bk, mmd=mmd):
                            p.wait(v, "pe", mmd)
                            p.wait(v, "se")
                            p.wait(v, "ve")
                            v.tensor_scalar(sA[:, :], pbank[bk][:, :], 0.0, None,
                                            ALU.min)
                            p.inc(v.tensor_scalar(sB[:, :], pbank[bk][:, :], 0.0,
                                                  None, ALU.max), "ve")
                            p.wait(v, "ve")

                        @blk.scalar
                        def _(sc):
                            p.wait(sc, "ve")
                            p.inc(sc.activation(sC[:, :], sA[:, :], AF.Exp), "se")

                        @blk.vector
                        def _(v, i=i, hf=hf, bk=bk):
                            p.wait(v, "se")
                            bank_ev[bk] = p.inc(
                                v.tensor_tensor(phik[:, i, bts(hf, 512)], sC[:, :],
                                                sB[:, :], ALU.add), "ve")
                    else:
                        @blk.vector
                        def _(v, i=i, hf=hf, bk=bk, mmd=mmd):
                            p.wait(v, "pe", mmd)
                            bank_ev[bk] = p.inc(
                                v.tensor_copy(vtm[:, i, bts(hf, 512)],
                                              pbank[bk][:, :]), "ve")
        kv_done = p.cnt["ve"]

        for i in range(NTS):
            @blk.vector
            def _(v, i=i, kd=kv_done):
                p.wait(v, "ve", kd)
                p.wait(v, "pe")
                p.inc(v.tensor_copy(vaug[:, :, 0:64],
                                    vtm[:, i, :].rearrange("p (h d) -> p h d", d=64)),
                      "ve")
            vg = p.cnt["ve"]

            @blk.tensor
            def _(t, s=s, i=i, vg=vg):
                p.wait(t, "ve", vg)
                inst = None
                for h in range(16):
                    inst = t.matmul(
                        kvzp[h // 4][0:64, (h % 4) * 128:(h % 4) * 128 + 65],
                        phik[:, i, h * 64:(h + 1) * 64], vaug[:, h, :],
                        start=(s == 0 and i == 0 and h % 4 == 0),
                        stop=(s == NSLAB - 1 and i == NTS - 1 and h % 4 == 3),
                        skip_group_check=True)
                p.inc(inst, "pe")

        prev_pe = p.cnt["pe"]

        @blk.gpsimd
        def _(g, s=s):
            p.wait(g, "ve", p.cnt["ve"])
            cur = bts(s, SLAB)
            p.inc(g.dma_start(
                out=o_phiq.rearrange("p (c t) -> p c t", c=NCH)[:, :, cur],
                in_=phiq[:, :, :]), "dg", 16)
        phiq_dma = p.cnt["dg"]

    kvz_pe = p.cnt["pe"]

    @blk.vector
    def _(v):
        p.wait(v, "pe", kvz_pe)
        inst = None
        for b in range(4):
            inst = v.tensor_copy(kvzs[:, 4 * b:4 * (b + 1), :],
                                 kvzp[b][:, :].rearrange("p (h x) -> p h x", h=4)
                                 [:, :, 0:65])
        p.inc(inst, "ve")

    @blk.gpsimd
    def _(g):
        p.wait(g, "ve")
        p.inc(g.dma_start(out=o_kvz, in_=kvzs[:, :, :].rearrange("p h x -> p (h x)")),
              "dg", 16)


def run_l7(inputs, r6):
    p = build_launch7()
    brow = np.ascontiguousarray(np.stack(
        [inputs["L1_attn_bq"], inputs["L1_attn_bk"],
         inputs["L1_attn_bv"]]).astype(np.float32))
    maps = []
    for c in range(NCORE):
        maps.append(dict(
            x2d=np.asarray(r6[c]["o_x2"]),
            wq=np.ascontiguousarray(inputs["L1_attn_Wq"]),
            wk=np.ascontiguousarray(inputs["L1_attn_Wk"]),
            wv=np.ascontiguousarray(inputs["L1_attn_Wv"]), brow=brow,
            wgs=np.ascontiguousarray(wg_full[c * 512:(c + 1) * 512]),
            wus=np.ascontiguousarray(wu_full[c * 512:(c + 1) * 512]),
            wds=np.ascontiguousarray(wd_full[c * 128:(c + 1) * 128]),
            wsums=wsums_all))
    return run_bass_kernel_spmd(p.nc, maps, core_ids=list(range(NCORE))).results



def emit_phaseWQ(p, blk, wgs, wus, wds, wsums, o_wgq, o_wuq, o_wdq, o_wmeans):
    identb = p.sb("identb", [128, 128], BF16)
    wstage = p.sb("wstage", [128, 4 * C])
    wbf = p.sb("wbf", [128, 4 * C], BF16)
    r1t = p.sb("r1t", [128, C]); r2t = p.sb("r2t", [128, C]); r3t = p.sb("r3t", [128, C])
    onesr = p.sb("onesr", [1, 128])
    onescol = p.sb("onescol", [128, 1])
    wsb = p.sb("wsb", [NCORE, 4])
    w3row = p.sb("w3row", [1, 4])
    w3rec = p.sb("w3rec", [1, 4])
    swcol = p.sb("swcol", [128, 4])
    stg = p.sb("stg", [128, NCH, 128], BF16)
    k0 = p.sb("k0", [128, 1]); k1 = p.sb("k1", [128, 1])
    kap = p.sb("kap", [128, 1]); lam = p.sb("lam", [128, 1])
    ptb = p.ps("ptb", [128, 1024], BF16)
    ptf = p.ps("ptf", [128, 512])

    emit_identity(p, blk, identb)

    @blk.gpsimd
    def _(g):
        p.inc(g.memset(onesr[:], 1.0), "po")
        p.inc(g.memset(onescol[:], 1.0), "po")

    @blk.sync
    def _(sy):
        p.inc(sy.dma_start(out=wsb[:, :], in_=wsums), "dl", 16)
        p.wait(sy, "dl")
    ld_ws = p.cnt["dl"]

    @blk.vector
    def _(v):
        p.wait(v, "po")
        p.inc(v.tensor_scalar(k0[:], onescol[:], 0.0, 0.5, ALU.mult, ALU.add), "ve")
        p.wait(v, "ve")
        p.inc(v.tensor_scalar(k1[:], k0[:], C23, -C23, ALU.add, ALU.add), "ve")
        p.wait(v, "ve")
        p.inc(v.tensor_scalar(kap[:], k1[:], 2.0, 0.0, ALU.mult, ALU.add), "ve")
        p.wait(v, "ve")
        p.inc(v.tensor_scalar(lam[:], kap[:], -1.0, 1.0, ALU.mult, ALU.add), "ve")
        p.wait(v, "ve")
    kap_ready = p.cnt["ve"]

    @blk.tensor
    def _(t, lw=ld_ws):
        p.wait(t, "dl", lw)
        p.wait(t, "po")
        p.inc(t.matmul(ptf[0:1, 0:4], onescol[0:NCORE, 0:1], wsb[:, :],
                       start=True, stop=True), "pe")
    smd = p.cnt["pe"]

    @blk.vector
    def _(v, smd=smd):
        p.wait(v, "pe", smd)
        p.inc(v.tensor_scalar(w3row[0:1, :], ptf[0:1, 0:4],
                              1.0 / float(4 * C * C), 1e-5, ALU.mult, ALU.max), "ve")
        p.wait(v, "ve")
        p.inc(v.reciprocal(w3rec[0:1, :], w3row[0:1, :]), "ve")
        p.wait(v, "ve")
    mrow_done = p.cnt["ve"]

    @blk.gpsimd
    def _(g, md=mrow_done):
        p.wait(g, "ve", md)
        p.inc(g.dma_start(out=o_wmeans, in_=w3row[0:1, :]), "dg", 16)
        p.wait(g, "dg")

    @blk.tensor
    def _(t):
        p.wait(t, "ve")
        p.inc(t.matmul(ptf[:, 4:8], onesr[0:1, :], w3rec[0:1, :],
                       start=True, stop=True), "pe")
    bcd = p.cnt["pe"]

    @blk.vector
    def _(v, bcd=bcd):
        p.wait(v, "pe", bcd)
        p.inc(v.tensor_copy(swcol[:, :], ptf[:, 4:8]), "ve")
        p.wait(v, "ve")
    sw_done = p.cnt["ve"]

    jobs = ([("g", wgs[bts(j, 128), :], C, 0, o_wgq, j) for j in range(4)]
            + [("u", wus[bts(j, 128), :], C, 1, o_wuq, j) for j in range(4)]
            + [("d", wds, 4 * C, 2, o_wdq, 0)])
    wstage2 = [wstage, p.sb("wstage_b", [128, 4 * C])]
    wbf2 = [wbf, p.sb("wbf_b", [128, 4 * C], BF16)]
    quant_done = [0, 0]
    tpq_done = [p.cnt["pe"], p.cnt["pe"]]
    ev_ptb = p.cnt["ve"]
    for jidx, (kind, src, wid, sci, dst, j) in enumerate(jobs):
        par = jidx % 2
        wst, wb = wstage2[par], wbf2[par]

        @blk.sync
        def _(sy, src=src, wid=wid, wst=wst, qd=quant_done[par]):
            p.wait(sy, "ve", qd)
            p.inc(sy.dma_start(out=wst[:, 0:wid], in_=src), "dl", 16)
            p.wait(sy, "dl")
        ldw = p.cnt["dl"]

        nq = wid // C
        for q in range(nq):
            @blk.vector
            def _(v, q=q, sci=sci, ldw=ldw, swd=sw_done, wst=wst,
                  tpd=tpq_done[par], pse=p.cnt["se"]):
                p.wait(v, "dl", ldw)
                p.wait(v, "ve", swd)
                p.wait(v, "pe", tpd)
                p.wait(v, "se", pse)
                p.inc(v.tensor_scalar(r1t[:, :], wst[:, bts(q, C)],
                                      swcol[:, sci:sci + 1], None, ALU.mult), "ve")

            @blk.scalar
            def _(sc, q=q):
                p.wait(sc, "ve")
                p.inc(sc.activation(r3t[:, :], r1t[:, :], AF.Sign), "se")

            @blk.vector
            def _(v, q=q, wb=wb):
                p.wait(v, "se")
                p.inc(v.tensor_tensor(r2t[:, :], r1t[:, :], r3t[:, :], ALU.mult),
                      "ve")
                p.wait(v, "ve")
                p.inc(v.tensor_scalar(r2t[:, :], r2t[:, :], -0.5, 1e38,
                                      ALU.add, ALU.mult), "ve")
                p.wait(v, "ve")
                p.inc(v.tensor_scalar(r2t[:, :], r2t[:, :], 0.0, 1.0,
                                      ALU.max, ALU.min), "ve")
                p.wait(v, "ve")
                p.inc(v.tensor_tensor(wb[:, bts(q, C)], r2t[:, :], r3t[:, :],
                                      ALU.mult), "ve")
                p.wait(v, "ve")
        quant_done[par] = p.cnt["ve"]
        cast_done = p.cnt["ve"]

        for grp in range(wid // 1024):
            @blk.tensor
            def _(t, grp=grp, cd=cast_done, evp=ev_ptb, wb=wb):
                p.wait(t, "ve", max(cd, evp))
                inst = None
                for jj in range(NCH):
                    ic = grp * NCH + jj
                    inst = t.transpose(ptb[:, bts(jj, 128)], wb[:, bts(ic, 128)],
                                       identb[:])
                p.inc(inst, "pe")
            tpd2 = p.cnt["pe"]

            @blk.vector
            def _(v, tpd2=tpd2, dmap=p.cnt["dg"]):
                p.wait(v, "pe", tpd2)
                p.wait(v, "dg", dmap)
                p.inc(v.tensor_copy(stg[:, :, :],
                                    ptb[:, :].rearrange("p (c j) -> p c j", c=NCH)),
                      "ve")
            cpd = p.cnt["ve"]

            @blk.gpsimd
            def _(g, kind=kind, dst=dst, j=j, grp=grp, cpd=cpd):
                p.wait(g, "ve", cpd)
                if kind == "d":
                    dd = (dst.rearrange("p (c t) -> p c t", c=4 * NCH)
                          [:, grp * NCH:(grp + 1) * NCH, :])
                else:
                    dd = (dst.rearrange("p (c t) -> p c t", c=NCH)
                          [:, :, bts(j, 128)])
                p.inc(g.dma_start(out=dd, in_=stg[:, :, :]), "dg", 16)
            ev_ptb = cpd
        tpq_done[par] = p.cnt["pe"]

    @blk.gpsimd
    def _(g):
        p.wait(g, "dg")


def build_launch567():
    p = P()
    z1d = p.din("z1d", [TL, C])
    z1p = p.din("z1p", [1, C])
    wkc = p.din("wkc", [4 * C, C])
    wrc = p.din("wrc", [C, C])
    prow5 = p.din("prow5", [4, C])
    sflag = p.din("sflag", [1, 1])
    xp = p.din("xp", [TL, C])
    wvc = p.din("wvc", [C, 4 * C])
    wff = p.din("wff", [C, C])
    bffr = p.din("bffr", [1, C])
    wgt = {n: p.din("w" + n, [C, C]) for n in "qkv"}
    brow = p.din("brow", [3, C])
    wgs = p.din("wgs", [4 * 128, C])
    wus = p.din("wus", [4 * 128, C])
    wds = p.din("wds", [128, 4 * C])
    wsums = p.din("wsums", [NCORE, 4])
    o_x2 = p.dout("o_x2", [TL, C])
    o_phiq = p.dout("o_phiq", [128, NCH * TL], BF16)
    o_kvz = p.dout("o_kvz", [64, 16 * 65])
    o_wgq = p.dout("o_wgq", [128, NCH * 512], BF16)
    o_wuq = p.dout("o_wuq", [128, NCH * 512], BF16)
    o_wdq = p.dout("o_wdq", [128, 4 * NCH * 128], BF16)
    o_wmeans = p.dout("o_wmeans", [1, 4])
    s_int = p.nc.dram_tensor("s_int", [128, 4 * NCH * TL], BF16).ap()
    rc_int = p.nc.dram_tensor("rc_int", [128, NCH * TL], BF16).ap()

    blk = p.stack.enter_context(p.nc.Block())
    p.prefix = "p5_"
    p.push()
    emit_phase5(p, blk, z1d, z1p, wkc, wrc, prow5, sflag, s_int, rc_int)
    p.pop()
    emit_barrier(p, blk)
    p.prefix = "p6_"
    p.push()
    emit_phase6(p, blk, z1d, xp, s_int, rc_int, wvc, wff, bffr, o_x2)
    p.pop()
    emit_barrier(p, blk)
    p.prefix = "p7_"
    p.push()
    emit_phase7(p, blk, o_x2, wgt, brow, o_phiq, o_kvz)
    p.pop()
    emit_barrier(p, blk)
    p.prefix = "wq_"
    p.push()
    emit_phaseWQ(p, blk, wgs, wus, wds, wsums, o_wgq, o_wuq, o_wdq, o_wmeans)
    p.pop()

    @blk.gpsimd
    def _(g):
        p.wait(g, "dg")
        p.wait(g, "dl")
        p.wait(g, "dp")

    return p


def run_l567(inputs, x_rows, r4, r2):
    p = build_launch567()
    wsums_all = np.concatenate(
        [np.asarray(r2[j]["o_wsums"]) for j in range(NCORE)], axis=0)
    wg_full = np.ascontiguousarray(inputs["L1_ffn_w_gate"].astype(np.float32))
    wu_full = np.ascontiguousarray(inputs["L1_ffn_w_up"].astype(np.float32))
    wd_full = np.ascontiguousarray(inputs["L1_ffn_w_down"].astype(np.float32))
    prow5 = np.ascontiguousarray(np.stack(
        [inputs["L0_attn_ln2_g"], inputs["L0_attn_ln2_b"],
         inputs["L0_attn_cmix_k"], inputs["L0_attn_cmix_r"]]).astype(np.float32))
    brow = np.ascontiguousarray(np.stack(
        [inputs["L1_attn_bq"], inputs["L1_attn_bk"],
         inputs["L1_attn_bv"]]).astype(np.float32))
    maps = []
    for c in range(NCORE):
        sf = np.array([[1.0 if _seq_start(c) else 0.0]], np.float32)
        z1prev = (np.asarray(r4[c - 1]["o_z1"])[TL - 1:TL]
                  if not _seq_start(c) else np.asarray(r4[c]["o_z1"])[0:1])
        maps.append(dict(
            z1d=np.asarray(r4[c]["o_z1"]), z1p=np.ascontiguousarray(z1prev),
            wkc=np.ascontiguousarray(inputs["L0_attn_Wk_c"]),
            wrc=np.ascontiguousarray(inputs["L0_attn_Wr_c"]),
            prow5=prow5, sflag=sf,
            xp=np.ascontiguousarray(x_rows[c * TL:(c + 1) * TL]),
            wvc=np.ascontiguousarray(inputs["L0_attn_Wv_c"]),
            wff=np.ascontiguousarray(inputs["L0_ffn_weight"]),
            bffr=np.ascontiguousarray(
                inputs["L0_ffn_bias"].astype(np.float32).reshape(1, C)),
            wq=np.ascontiguousarray(inputs["L1_attn_Wq"]),
            wk=np.ascontiguousarray(inputs["L1_attn_Wk"]),
            wv=np.ascontiguousarray(inputs["L1_attn_Wv"]), brow=brow,
            wgs=np.ascontiguousarray(wg_full[c * 512:(c + 1) * 512]),
            wus=np.ascontiguousarray(wu_full[c * 512:(c + 1) * 512]),
            wds=np.ascontiguousarray(wd_full[c * 128:(c + 1) * 128]),
            wsums=wsums_all))
    return run_bass_kernel_spmd(p.nc, maps, core_ids=list(range(NCORE))).results


def emit_phase8(p, blk, x2d, phiqd, kvz_all, bmask, wo, borow, o_x3):
    _po_guard = True

    identb = p.sb("identb", [128, 128], BF16)
    woT = p.sb("woT", [128, NCH, C], BF16)
    bob = p.sb("bob", [1, C], BF16)
    onesb = p.sb("onesb", [1, 512], BF16)
    kstack = p.sb("kstack", [128, NCORE, 1040])
    bmc = p.sb("bmc", [128, NCORE])
    macc = p.sb("macc", [128, 1040, 2])
    kvzb = p.sb("kvzb", [128, 16, 65], BF16)
    x2_tm = p.sb("x2_tm", [128, NTS, C])
    phiqs = p.sb("phiqs", [128, NCH, SLAB], BF16)
    attb = p.sb("attb", [128, NTS, C], BF16)
    attT = p.sb("attT", [128, NCH, SLAB], BF16)
    att64 = p.sb("att64", [128, 16, 64], BF16)
    normr = p.sb("normr", [128, 16])
    normq = p.sb("normq", [128, 16])

    pbank = [p.ps(f"pb{i}", [128, 512]) for i in range(4)]
    ptb = p.ps("ptb", [128, 1024], BF16)

    emit_identity(p, blk, identb)

    @blk.gpsimd
    def _(g):
        p.inc(g.memset(onesb[:], 1.0), "po")
        p.inc(g.dma_start(out=bob[0:1, :], in_=borow), "dp", 16)
        p.wait(g, "dp")
    ldp = p.cnt["dp"]

    @blk.tensor
    def _(t):
        p.wait(t, "po")

    @blk.sync
    def _(sy):
        src_ = kvz_all.rearrange("(j p) c -> p j c", p=64)
        p.inc(sy.dma_start(out=kstack[0:64, :, :], in_=src_), "dl", 16)
        p.inc(sy.dma_start(out=kstack[64:128, :, :], in_=src_), "dl", 16)
        p.inc(sy.dma_start(out=bmc[:, :], in_=bmask), "dl", 16)
        p.wait(sy, "dl")
    ld0 = p.cnt["dl"]

    @blk.vector
    def _(v):
        p.wait(v, "dl", ld0)
        p.inc(v.memset(macc[:, :, 0], 0.0), "ve")
        p.wait(v, "ve")
        for j in range(NCORE):
            a, b = j % 2, 1 - j % 2
            p.inc(v.scalar_tensor_tensor(macc[:, :, b], kstack[:, j, :],
                                         bmc[:, j:j + 1], macc[:, :, a],
                                         ALU.mult, ALU.add), "ve")
            p.wait(v, "ve")
        p.inc(v.tensor_copy(kvzb[:, :, :],
                            macc[:, :, NCORE % 2].rearrange("p (h x) -> p h x", h=16)),
              "ve")
    kvz_done = p.cnt["ve"]

    cast_done = 0
    tp_done = 0
    ev_ptb = p.cnt["ve"]
    for oc in range(NCH):
        @blk.sync
        def _(sy, oc=oc, cd=cast_done):
            p.wait(sy, "ve", cd)
            p.inc(sy.dma_start(out=x2_tm[:, 0, :], in_=wo[bts(oc, 128), :]), "dl", 16)
            p.wait(sy, "dl")
        ldw = p.cnt["dl"]

        @blk.vector
        def _(v, ldw=ldw, tpd=tp_done):
            p.wait(v, "dl", ldw)
            p.wait(v, "pe", tpd)
            p.inc(v.tensor_copy(attb[:, 0, :], x2_tm[:, 0, :]), "ve")
        cast_done = p.cnt["ve"]

        @blk.tensor
        def _(t, cd=cast_done, evp=ev_ptb):
            p.wait(t, "ve", cd)
            p.wait(t, "ve", evp)
            inst = None
            for dc in range(NCH):
                inst = t.transpose(ptb[:, bts(dc, 128)], attb[:, 0, bts(dc, 128)],
                                   identb[:])
            p.inc(inst, "pe")
        tp_done = p.cnt["pe"]

        @blk.vector
        def _(v, oc=oc, tpd=tp_done):
            p.wait(v, "pe", tpd)
            p.inc(v.tensor_copy(woT[:, :, bts(oc, 128)],
                                ptb[:, :].rearrange("p (c j) -> p c j", c=NCH)), "ve")
        ev_ptb = p.cnt["ve"]
    prep_done = p.cnt["ve"]

    bank_ev = [p.cnt["ve"]] * 4
    slab_end_ve = 0
    slab_dma = 0
    prev_pe = 0
    prep_se = p.cnt.get("se", 0)
    for s in range(NSLAB):
        @blk.sync
        def _(sy, s=s, sv=max(slab_end_ve, prep_done), dmap=slab_dma,
              pse=prep_se):
            p.wait(sy, "ve", sv)
            p.wait(sy, "se", pse)
            p.wait(sy, "dg", dmap)
            p.inc(sy.dma_start(
                out=x2_tm[:, :, :],
                in_=x2d[s * SLAB:(s + 1) * SLAB, :].rearrange("(a p) c -> p a c", p=128)),
                "dl", 16)
            p.inc(sy.dma_start(
                out=phiqs[:, :, :],
                in_=phiqd.rearrange("p (c t) -> p c t", c=NCH)[:, :, bts(s, SLAB)]),
                "dl", 16)
            p.wait(sy, "dl")
        ld_x = p.cnt["dl"]

        for i in range(NTS):
            for h in range(16):
                bk = (i * 16 + h) % 4

                @blk.tensor
                def _(t, i=i, h=h, bk=bk, ld_x=ld_x, kd=kvz_done, bev=bank_ev[bk]):
                    p.wait(t, "dl", ld_x)
                    p.wait(t, "ve", max(kd, bev))
                    base = 64 * (h % 2)
                    p.inc(t.matmul(pbank[bk][:, 0:65],
                                   phiqs[base:base + 64, h // 2, bts(i, 128)],
                                   kvzb[base:base + 64, h, :],
                                   start=True, stop=True), "pe")
                mmd = p.cnt["pe"]

                @blk.vector
                def _(v, i=i, h=h, bk=bk, mmd=mmd):
                    p.wait(v, "pe", mmd)
                    v.tensor_copy(att64[:, h, :], pbank[bk][:, 0:64])
                    bank_ev[bk] = p.inc(
                        v.tensor_copy(normr[:, h:h + 1], pbank[bk][:, 64:65]), "ve")

            @blk.vector
            def _(v, i=i):
                p.wait(v, "ve")
                p.inc(v.tensor_single_scalar(normq[:, :], normr[:, :],
                                             1e-6, ALU.add), "ve")
                p.wait(v, "ve")
                p.inc(v.reciprocal(normq[:, :], normq[:, :]), "ve")
                p.wait(v, "ve")
                inst = None
                for h in range(16):
                    inst = v.tensor_scalar(attb[:, i, h * 64:(h + 1) * 64],
                                           att64[:, h, :], normq[:, h:h + 1],
                                           None, ALU.mult)
                p.inc(inst, "ve")
                p.wait(v, "ve")
        att_done = p.cnt["ve"]

        for i in range(NTS):
            @blk.tensor
            def _(t, i=i, ad=att_done, evp=ev_ptb):
                p.wait(t, "ve", max(ad, evp))
                inst = None
                for dc in range(NCH):
                    inst = t.transpose(ptb[:, bts(dc, 128)],
                                       attb[:, i, bts(dc, 128)], identb[:])
                p.inc(inst, "pe")
            tpd = p.cnt["pe"]

            @blk.vector
            def _(v, i=i, tpd=tpd):
                p.wait(v, "pe", tpd)
                p.inc(v.tensor_copy(attT[:, :, bts(i, 128)],
                                    ptb[:, :].rearrange("p (c j) -> p c j", c=NCH)),
                      "ve")
            ev_ptb = p.cnt["ve"]
        attT_done = p.cnt["ve"]

        for i in range(NTS):
            for hf in range(2):
                bk = (i * 2 + hf) % 4

                @blk.tensor
                def _(t, i=i, hf=hf, bk=bk, ad=attT_done, pd=prep_done,
                      bev=bank_ev[bk], ldp=ldp):
                    p.wait(t, "ve", max(ad, pd, bev))
                    p.wait(t, "dp", ldp)
                    for dc in range(NCH):
                        t.matmul(pbank[bk][:, :], attT[:, dc, bts(i, 128)],
                                 woT[:, dc, bts(hf, 512)], start=(dc == 0),
                                 stop=False)
                    inst = t.matmul(pbank[bk][:, :], onesb[0:1, 0:128],
                                    bob[0:1, bts(hf, 512)], start=False, stop=True)
                    p.inc(inst, "pe")
                mmd = p.cnt["pe"]

                @blk.vector
                def _(v, i=i, hf=hf, bk=bk, mmd=mmd, ld_x=ld_x):
                    p.wait(v, "pe", mmd)
                    p.wait(v, "dl", ld_x)
                    bank_ev[bk] = p.inc(
                        v.tensor_tensor(x2_tm[:, i, bts(hf, 512)], pbank[bk][:, :],
                                        x2_tm[:, i, bts(hf, 512)], ALU.add), "ve")
        slab_end_ve = p.cnt["ve"]
        prev_pe = p.cnt["pe"]

        @blk.gpsimd
        def _(g, s=s, sv=slab_end_ve):
            p.wait(g, "ve", sv)
            dst = o_x3[s * SLAB:(s + 1) * SLAB, :].rearrange("(a p) c -> p a c", p=128)
            p.inc(g.dma_start(out=dst, in_=x2_tm[:, :, :]), "dg", 16)
        slab_dma = p.cnt["dg"]


def run_l8(inputs, r6, r7):
    p = build_launch8()
    kvz_all = np.concatenate([np.asarray(r7[j]["o_kvz"]) for j in range(NCORE)],
                             axis=0)
    maps = []
    for c in range(NCORE):
        bm = np.zeros((128, NCORE), np.float32)
        base = (c // (NCORE // B)) * (NCORE // B)
        bm[:, base:base + NCORE // B] = 1.0
        maps.append(dict(
            x2d=np.asarray(r6[c]["o_x2"]), phiqd=np.asarray(r7[c]["o_phiq"]),
            kvz_all=kvz_all, bmask=bm,
            wo=np.ascontiguousarray(inputs["L1_attn_Wo"]),
            borow=np.ascontiguousarray(
                inputs["L1_attn_bo"].astype(np.float32).reshape(1, C))))
    return run_bass_kernel_spmd(p.nc, maps, core_ids=list(range(NCORE))).results


def emit_phase9(p, blk, x3d, wgq_all, wuq_all, swcg, o_xq2, o_qsc):
    NIC = 4 * NCH
    identb = p.sb("identb", [128, 128], BF16)
    wgT = p.sb("wgT", [128, NCH, 4 * C], BF16)
    wuT = p.sb("wuT", [128, NCH, 4 * C], BF16)
    gbuf = p.sb("gbuf", [128, 4 * C])
    ubuf = p.sb("ubuf", [128, 4 * C], BF16)
    xqb = p.sb("xqb", [128, C], BF16)
    onesf = p.sb("onesf", [128, 1])
    k0 = p.sb("k0", [128, 1]); k1 = p.sb("k1", [128, 1])
    kap = p.sb("kap", [128, 1]); lam = p.sb("lam", [128, 1])
    r1t = p.sb("r1t", [128, C]); r2t = p.sb("r2t", [128, C]); r3t = p.sb("r3t", [128, C])
    x3_tm = p.sb("x3_tm", [128, C])
    h_tm = p.sb("h_tm", [128, C])
    xqTall = p.sb("xqTall", [128, NCH, TL], BF16)
    xq2b = ubuf
    st = p.sb("st", [128, 1, 4])
    st2 = p.sb("st2", [128, 1, 4])
    stq = p.sb("stq", [128, NT, 8])
    qsc = p.sb("qsc", [128, NT])

    pbank = [p.ps(f"pb{i}", [128, 512]) for i in range(4)]
    ptb = p.ps("ptb", [128, 1024], BF16)

    emit_identity(p, blk, identb)

    @blk.gpsimd
    def _(g):
        p.inc(g.memset(onesf[:], 1.0), "po")
        p.inc(g.dma_start(out=wgT[:, :, :].rearrange("p c t -> p (c t)"),
                          in_=wgq_all), "dg", 16)
        p.inc(g.dma_start(out=wuT[:, :, :].rearrange("p c t -> p (c t)"),
                          in_=wuq_all), "dg", 16)
    wT_ld = p.cnt["dg"]

    @blk.vector
    def _(v):
        p.wait(v, "po")
        p.inc(v.tensor_scalar(k0[:], onesf[:], 0.0, 0.5, ALU.mult, ALU.add), "ve")
        p.wait(v, "ve")
        p.inc(v.tensor_scalar(k1[:], k0[:], C23, -C23, ALU.add, ALU.add), "ve")
        p.wait(v, "ve")
        p.inc(v.tensor_scalar(kap[:], k1[:], 2.0, 0.0, ALU.mult, ALU.add), "ve")
        p.wait(v, "ve")
        p.inc(v.tensor_scalar(lam[:], kap[:], -1.0, 1.0, ALU.mult, ALU.add), "ve")
        p.wait(v, "ve")
    kap_ready = p.cnt["ve"]

    ev_ptb = p.cnt["ve"]
    prep_done = 0

    st9 = {"tile_x": 0, "ev_ptb": ev_ptb, "gu_free_ve": p.cnt["ve"],
           "tile_dma": 0, "a_tp": [0] * (NT + 1)}
    a_xqT = [0] * NT
    a_scale = [0] * NT
    bank_ev = [p.cnt["se"]] * 4
    bank_ev_v = [p.cnt["ve"]] * 4

    def emit_A(i):
        @blk.sync
        def _(sy, i=i, tx=st9["tile_x"]):
            p.wait(sy, "ve", tx)
            p.wait(sy, "se", p.cnt["se"])
            p.inc(sy.dma_start(out=x3_tm[:, :], in_=x3d[bts(i, 128), :]), "dl", 16)
            p.wait(sy, "dl")
        ld_x = p.cnt["dl"]

        @blk.vector
        def _(v, ld_x=ld_x):
            p.wait(v, "dl", ld_x)

        @blk.scalar
        def _(sc, ld_x=ld_x):
            p.wait(sc, "dl", ld_x)

        emit_ln_batch(p, blk, [x3_tm[:, :]], [h_tm[:, :]], st, st2, r1t,
                      wv=[("dl", ld_x), ("ve", p.cnt["ve"]), ("se", p.cnt["se"])])
        h_done = p.cnt["ve"]

        @blk.vector
        def _(v, i=i, hd=h_done):
            p.wait(v, "ve", hd)
            p.inc(v.reduce_max(stq[:, i, 2:3], h_tm[:, :], AX.X,
                               apply_absolute_value=True), "ve")
            p.wait(v, "ve")
            p.inc(v.tensor_scalar(stq[:, i, 3:4], stq[:, i, 2:3], 1e-5, None,
                                  ALU.max), "ve")
            p.wait(v, "ve")
            p.inc(v.tensor_scalar(stq[:, i, 6:7], stq[:, i, 3:4], swcg[:, 0:1],
                                  1.0 / 127.0, ALU.mult, ALU.mult), "ve")
            p.inc(v.tensor_scalar(stq[:, i, 7:8], stq[:, i, 3:4], swcg[:, 1:2],
                                  1.0 / 127.0, ALU.mult, ALU.mult), "ve")
        a_scale[i] = p.cnt["ve"]

        @blk.scalar
        def _(sc, i=i):
            p.wait(sc, "ve")
            p.inc(sc.activation(stq[:, i, 4:5], stq[:, i, 3:4], AF.Ln), "se")
            p.wait(sc, "se")
            p.inc(sc.activation(stq[:, i, 4:5], stq[:, i, 4:5], AF.Exp, scale=-1.0),
                  "se")

        @blk.vector
        def _(v, i=i, atp=st9["a_tp"][i]):
            p.wait(v, "se")
            p.wait(v, "pe", atp)
            p.inc(v.tensor_scalar(stq[:, i, 5:6], stq[:, i, 4:5], 127.0, None,
                                  ALU.mult), "ve")
            p.wait(v, "ve")
            p.inc(v.tensor_scalar(r1t[:, :], h_tm[:, :], stq[:, i, 5:6], None,
                                  ALU.mult), "ve")
            p.wait(v, "ve")
            v.tensor_scalar(r2t[:, :], r1t[:, :], lam[:, 0:1], C23, ALU.mult, ALU.add)
            p.inc(v.tensor_scalar(r3t[:, :], r1t[:, :], kap[:, 0:1], C52,
                                  ALU.mult, ALU.add), "ve")
            p.wait(v, "ve")
            p.inc(v.tensor_single_scalar(r2t[:, :], r2t[:, :], -C23, ALU.add), "ve")
            p.wait(v, "ve")
            p.inc(v.scalar_tensor_tensor(r1t[:, :], r3t[:, :], -C52, r2t[:, :],
                                         ALU.add, ALU.add), "ve")
            p.wait(v, "ve")
            p.inc(v.tensor_scalar(xqb[:, :], r1t[:, :], 127.0, -128.0,
                                  ALU.min, ALU.max), "ve")
            p.wait(v, "ve")
        xq_done = p.cnt["ve"]

        @blk.tensor
        def _(t, i=i, xd=xq_done, evp=st9["ev_ptb"]):
            p.wait(t, "ve", max(xd, evp))
            inst = None
            for dc in range(NCH):
                inst = t.transpose(ptb[:, bts(dc, 128)], xqb[:, bts(dc, 128)],
                                   identb[:])
            p.inc(inst, "pe")
        st9["a_tp"][i + 1] = p.cnt["pe"]
        tpd = p.cnt["pe"]

        @blk.vector
        def _(v, i=i, tpd=tpd):
            p.wait(v, "pe", tpd)
            p.inc(v.tensor_copy(xqTall[:, :, bts(i, 128)],
                                ptb[:, :].rearrange("p (c j) -> p c j", c=NCH)), "ve")
        st9["ev_ptb"] = p.cnt["ve"]
        a_xqT[i] = p.cnt["ve"]
        st9["tile_x"] = p.cnt["ve"]

    def emit_B(i):
        for wi, (wTd,) in enumerate(((wgT,), (wuT,))):
            for og in range(8):
                bk = (wi * 8 + og) % 4

                @blk.tensor
                def _(t, wTd=wTd, og=og, bk=bk, i=i, sa=a_xqT[i],
                      bes=bank_ev[bk], bev=bank_ev_v[bk]):
                    p.wait(t, "ve", sa)
                    if i == 0:
                        p.wait(t, "dg", wT_ld)
                    p.wait(t, "se", bes)
                    p.wait(t, "ve", bev)
                    inst = None
                    for dc in range(NCH):
                        inst = t.matmul(pbank[bk][:, :], xqTall[:, dc, bts(i, 128)],
                                        wTd[:, dc, bts(og, 512)], start=(dc == 0),
                                        stop=(dc == NCH - 1))
                    p.inc(inst, "pe")
                mmd = p.cnt["pe"]

                if wi == 0:
                    @blk.scalar
                    def _(sc, og=og, bk=bk, mmd=mmd, i=i, gf=st9["gu_free_ve"],
                          asc=a_scale[i]):
                        p.wait(sc, "pe", mmd)
                        p.wait(sc, "ve", max(gf, asc))
                        bank_ev[bk] = p.inc(
                            sc.activation(gbuf[:, bts(og, 512)], pbank[bk][:, :],
                                          AF.Silu, scale=stq[:, i, 6:7]), "se")
                else:
                    @blk.vector
                    def _(v, og=og, bk=bk, mmd=mmd, i=i, gf=st9["gu_free_ve"],
                          asc=a_scale[i]):
                        p.wait(v, "pe", mmd)
                        p.wait(v, "ve", max(gf, asc))
                        bank_ev_v[bk] = p.inc(
                            v.tensor_scalar(ubuf[:, bts(og, 512)], pbank[bk][:, :],
                                            stq[:, i, 7:8], None, ALU.mult), "ve")
        evac_done_se = p.cnt["se"]
        evac_done_ve = p.cnt["ve"]

        @blk.vector
        def _(v, i=i, es=evac_done_se, ev=evac_done_ve):
            p.wait(v, "se", es)
            p.wait(v, "ve", ev)
            p.inc(v.tensor_tensor(gbuf[:, :], gbuf[:, :], ubuf[:, :], ALU.mult),
                  "ve")
            p.wait(v, "ve")
            p.inc(v.reduce_max(stq[:, i, 2:3], gbuf[:, :], AX.X,
                               apply_absolute_value=True), "ve")
            p.wait(v, "ve")
            p.inc(v.tensor_scalar(stq[:, i, 3:4], stq[:, i, 2:3], 1e-5, None,
                                  ALU.max), "ve")
            p.wait(v, "ve")
            p.inc(v.tensor_scalar(qsc[:, i:i + 1], stq[:, i, 3:4], 1.0 / 127.0,
                                  None, ALU.mult), "ve")

        @blk.scalar
        def _(sc, i=i):
            p.wait(sc, "ve")
            p.inc(sc.activation(stq[:, i, 4:5], stq[:, i, 3:4], AF.Ln), "se")
            p.wait(sc, "se")
            p.inc(sc.activation(stq[:, i, 4:5], stq[:, i, 4:5], AF.Exp, scale=-1.0),
                  "se")
        sc2_done = p.cnt["se"]

        for hq in range(4):
            @blk.vector
            def _(v, i=i, hq=hq, s2=sc2_done):
                p.wait(v, "se", s2)
                p.inc(v.tensor_scalar(r1t[:, :], gbuf[:, bts(hq, C)],
                                      stq[:, i, 4:5], 127.0, ALU.mult, ALU.mult),
                      "ve")
                p.wait(v, "ve")
                v.tensor_scalar(r2t[:, :], r1t[:, :], lam[:, 0:1], C23,
                                ALU.mult, ALU.add)
                p.inc(v.tensor_scalar(r3t[:, :], r1t[:, :], kap[:, 0:1], C52,
                                      ALU.mult, ALU.add), "ve")
                p.wait(v, "ve")
                p.inc(v.tensor_single_scalar(r2t[:, :], r2t[:, :], -C23, ALU.add),
                      "ve")
                p.wait(v, "ve")
                p.inc(v.scalar_tensor_tensor(r1t[:, :], r3t[:, :], -C52, r2t[:, :],
                                             ALU.add, ALU.add), "ve")
                p.wait(v, "ve")
                p.inc(v.tensor_scalar(xq2b[:, bts(hq, C)], r1t[:, :], 127.0, -128.0,
                                      ALU.min, ALU.max), "ve")
                p.wait(v, "ve")
        st9["gu_free_ve"] = p.cnt["ve"]
        xq2_done = p.cnt["ve"]

        for grp in range(4):
            @blk.tensor
            def _(t, i=i, grp=grp, xd=xq2_done, evp=st9["ev_ptb"]):
                p.wait(t, "ve", max(xd, evp))
                inst = None
                for j in range(NCH):
                    ic = grp * NCH + j
                    inst = t.transpose(ptb[:, bts(j, 128)], xq2b[:, bts(ic, 128)],
                                       identb[:])
                p.inc(inst, "pe")
            tpd = p.cnt["pe"]

            @blk.vector
            def _(v, i=i, grp=grp, tpd=tpd, dmap=p.cnt["dg"]):
                p.wait(v, "pe", tpd)
                p.wait(v, "dg", dmap)
                p.inc(v.tensor_copy(xqTall[:, :, bts(i, 128)],
                                    ptb[:, :].rearrange("p (c j) -> p c j", c=NCH)),
                      "ve")
            cpd = p.cnt["ve"]

            @blk.gpsimd
            def _(g, i=i, grp=grp, cpd=cpd):
                p.wait(g, "ve", cpd)
                dst = (o_xq2.rearrange("p (c t) -> p c t", c=4 * NCH)
                       [:, grp * NCH:(grp + 1) * NCH, bts(i, 128)])
                p.inc(g.dma_start(out=dst, in_=xqTall[:, :, bts(i, 128)]),
                      "dg", 16)
            st9["ev_ptb"] = cpd

    emit_A(0)
    for i in range(NT):
        if i + 1 < NT:
            emit_A(i + 1)
        emit_B(i)

    @blk.gpsimd
    def _(g):
        p.wait(g, "ve")
        p.inc(g.dma_start(out=o_qsc, in_=qsc[:, :]), "dg", 16)


def run_l9(inputs, r8):
    p = build_launch9()
    maps = []
    for c in range(NCORE):
        maps.append(dict(
            x3d=np.asarray(r8[c]["o_x3"]),
            wg=np.ascontiguousarray(inputs["L1_ffn_w_gate"]),
            wu=np.ascontiguousarray(inputs["L1_ffn_w_up"])))
    return run_bass_kernel_spmd(p.nc, maps, core_ids=list(range(NCORE))).results


def emit_phase10(p, blk, x3d, xq2d, qscd, wdq_all, swcg, o_out):
    NIC = 4 * NCH

    wdT = p.sb("wdT", [128, NIC, C], BF16)
    qsc = p.sb("qsc", [128, NT])
    dsc = p.sb("dsc", [128, NT])
    x3_tm = p.sb("x3_tm", [128, C])
    xq2T = p.sb("xq2T", [128, NIC, 128], BF16)
    y_tm = p.sb("y_tm", [128, C])
    pbank = [p.ps(f"pb{i}", [128, 512]) for i in range(4)]

    @blk.gpsimd
    def _(g):
        p.inc(g.dma_start(out=wdT[:, :, :].rearrange("p c t -> p (c t)"),
                          in_=wdq_all), "dg", 16)
    wT_ld = p.cnt["dg"]

    @blk.sync
    def _(sy):
        p.inc(sy.dma_start(out=qsc[:, :], in_=qscd), "dl", 16)
        p.wait(sy, "dl")
    ldq = p.cnt["dl"]

    @blk.vector
    def _(v, ldq=ldq):
        p.wait(v, "dl", ldq)
        p.inc(v.tensor_scalar(dsc[:, :], qsc[:, :], swcg[:, 2:3], None, ALU.mult),
              "ve")
        p.wait(v, "ve")
    prep_done = p.cnt["ve"]

    @blk.tensor
    def _(t, wl=wT_ld):
        p.wait(t, "dg", wl)

    bank_ev = [p.cnt["ve"]] * 4
    tile_end = 0
    tile_dma = 0
    for i in range(NT):
        @blk.sync
        def _(sy, i=i, sv=tile_end, dmap=tile_dma):
            p.wait(sy, "ve", sv)
            p.wait(sy, "dg", dmap)
            p.inc(sy.dma_start(out=x3_tm[:, :], in_=x3d[bts(i, 128), :]), "dl", 16)
            p.inc(sy.dma_start(
                out=xq2T[:, :, :],
                in_=xq2d.rearrange("p (c t) -> p c t", c=NIC)[:, :, bts(i, 128)]),
                "dl", 16)
            p.wait(sy, "dl")
        ld_x = p.cnt["dl"]

        for hf in range(2):
            bk = hf % 4

            @blk.tensor
            def _(t, i=i, hf=hf, bk=bk, ld_x=ld_x, pd=prep_done, bev=bank_ev[bk]):
                p.wait(t, "dl", ld_x)
                p.wait(t, "ve", max(pd, bev))
                inst = None
                for ic in range(NIC):
                    inst = t.matmul(pbank[bk][:, :], xq2T[:, ic, :],
                                    wdT[:, ic, bts(hf, 512)], start=(ic == 0),
                                    stop=(ic == NIC - 1))
                p.inc(inst, "pe")
            mmd = p.cnt["pe"]

            @blk.vector
            def _(v, i=i, hf=hf, bk=bk, mmd=mmd, ld_x=ld_x):
                p.wait(v, "pe", mmd)
                p.wait(v, "dl", ld_x)
                p.inc(v.tensor_scalar(y_tm[:, bts(hf, 512)], pbank[bk][:, :],
                                      dsc[:, i:i + 1], None, ALU.mult), "ve")
                p.wait(v, "ve")
                bank_ev[bk] = p.inc(
                    v.tensor_tensor(y_tm[:, bts(hf, 512)], y_tm[:, bts(hf, 512)],
                                    x3_tm[:, bts(hf, 512)], ALU.add), "ve")
        tile_end = p.cnt["ve"]

        @blk.gpsimd
        def _(g, i=i, sv=tile_end):
            p.wait(g, "ve", sv)
            p.inc(g.dma_start(out=o_out[bts(i, 128), :], in_=y_tm[:, :]), "dg", 16)
        tile_dma = p.cnt["dg"]


def run_l10(inputs, r8, r9):
    p = build_launch10()
    maps = []
    for c in range(NCORE):
        maps.append(dict(
            x3d=np.asarray(r8[c]["o_x3"]), xq2d=np.asarray(r9[c]["o_xq2"]),
            qscd=np.asarray(r9[c]["o_qsc"]),
            wd=np.ascontiguousarray(inputs["L1_ffn_w_down"])))
    return run_bass_kernel_spmd(p.nc, maps, core_ids=list(range(NCORE))).results



def build_launch8910():
    NIC = 4 * NCH
    p = P()
    x2d = p.din("x2d", [TL, C])
    phiqd = p.din("phiqd", [128, NCH * TL], BF16)
    kvz_all = p.din("kvz_all", [NCORE * 64, 16 * 65])
    bmask = p.din("bmask", [128, NCORE])
    wo = p.din("wo", [C, C])
    borow = p.din("borow", [1, C])
    wgq_all = p.din("wgq_all", [128, NCH * 4 * C], BF16)
    wuq_all = p.din("wuq_all", [128, NCH * 4 * C], BF16)
    wdq_all = p.din("wdq_all", [128, NIC * C], BF16)
    wmeans = p.din("wmeans", [1, 4])
    o_out = p.dout("o_out", [TL, C])
    x3_int = p.nc.dram_tensor("x3_int", [TL, C], F32).ap()
    xq2_int = p.nc.dram_tensor("xq2_int", [128, NIC * TL], BF16).ap()
    qsc_int = p.nc.dram_tensor("qsc_int", [128, NT], F32).ap()

    blk = p.stack.enter_context(p.nc.Block())

    wmrow = p.sb("wmrow", [1, 4])
    swcg = p.sb("swcg", [128, 4])
    onesr1 = p.sb("onesr1", [1, 128])
    ptw = p.ps("ptw", [128, 512])

    @blk.gpsimd
    def _(g):
        p.inc(g.memset(onesr1[:], 1.0), "po")

    @blk.sync
    def _(sy):
        p.inc(sy.dma_start(out=wmrow[:, :], in_=wmeans), "dl", 16)
        p.wait(sy, "dl")
    ld_wm = p.cnt["dl"]

    @blk.tensor
    def _(t, lw=ld_wm):
        p.wait(t, "dl", lw)
        p.wait(t, "po")
        p.inc(t.matmul(ptw[:, 0:4], onesr1[0:1, :], wmrow[0:1, :],
                       start=True, stop=True), "pe")
    bcw = p.cnt["pe"]

    @blk.vector
    def _(v, bcw=bcw):
        p.wait(v, "pe", bcw)
        p.inc(v.tensor_copy(swcg[:, :], ptw[:, 0:4]), "ve")
        p.wait(v, "ve")

    p.prefix = "p8_"
    p.push()
    emit_phase8(p, blk, x2d, phiqd, kvz_all, bmask, wo, borow, x3_int)
    p.pop()
    emit_barrier(p, blk)
    p.prefix = "p9_"
    p.push()
    emit_phase9(p, blk, x3_int, wgq_all, wuq_all, swcg, xq2_int, qsc_int)
    p.pop()
    emit_barrier(p, blk)
    p.prefix = "pA_"
    p.push()
    emit_phase10(p, blk, x3_int, xq2_int, qsc_int, wdq_all, swcg, o_out)
    p.pop()

    @blk.gpsimd
    def _(g):
        p.wait(g, "dg")
        p.wait(g, "dl")
        p.wait(g, "dp")

    return p


def run_l8910(inputs, r567):
    p = build_launch8910()
    kvz_all = np.concatenate([np.asarray(r567[j]["o_kvz"]) for j in range(NCORE)],
                             axis=0)
    NIC = 4 * NCH
    wgq_all = np.concatenate(
        [np.asarray(r567[j]["o_wgq"]).reshape(128, NCH, 512) for j in range(NCORE)],
        axis=2).reshape(128, NCH * 4 * C)
    wuq_all = np.concatenate(
        [np.asarray(r567[j]["o_wuq"]).reshape(128, NCH, 512) for j in range(NCORE)],
        axis=2).reshape(128, NCH * 4 * C)
    wdq_all = np.concatenate(
        [np.asarray(r567[j]["o_wdq"]).reshape(128, NIC, 128) for j in range(NCORE)],
        axis=2).reshape(128, NIC * C)
    wmeans = np.ascontiguousarray(np.asarray(r567[0]["o_wmeans"]))
    maps = []
    for c in range(NCORE):
        bm = np.zeros((128, NCORE), np.float32)
        base = (c // (NCORE // B)) * (NCORE // B)
        bm[:, base:base + NCORE // B] = 1.0
        maps.append(dict(
            x2d=np.asarray(r567[c]["o_x2"]), phiqd=np.asarray(r567[c]["o_phiq"]),
            kvz_all=kvz_all, bmask=bm,
            wo=np.ascontiguousarray(inputs["L1_attn_Wo"]),
            borow=np.ascontiguousarray(
                inputs["L1_attn_bo"].astype(np.float32).reshape(1, C)),
            wgq_all=wgq_all, wuq_all=wuq_all, wdq_all=wdq_all, wmeans=wmeans))
    return run_bass_kernel_spmd(p.nc, maps, core_ids=list(range(NCORE))).results


def kernel(**inputs):
    x = np.asarray(inputs["L0_attn_x"], np.float32)
    _ = inputs["L0_attn_time_decay"]
    rows = np.ascontiguousarray(x.reshape(B * T, C))
    r1 = run_l1(inputs, rows)
    r2 = run_l2(inputs, r1)
    r3 = run_l3(r1, r2)
    r4 = run_l4(inputs, rows, r1, r2, r3)
    r567 = run_l567(inputs, rows, r4, r2)
    r8910 = run_l8910(inputs, r567)
    out = np.concatenate([np.asarray(r8910[c]["o_out"]) for c in range(NCORE)],
                         axis=0)
    return np.ascontiguousarray(out.reshape(B, T, C).astype(np.float32))

